# version 06 (final)
# baseline (speedup 1.0000x reference)
import numpy as np
import ml_dtypes
import concourse.bass as bass
import concourse.mybir as mybir
from concourse.bass import IndirectOffsetOnAxis
from concourse.bass_utils import run_bass_kernel_spmd
from contextlib import ExitStack

F32 = mybir.dt.float32
BF16 = mybir.dt.bfloat16
U32 = mybir.dt.uint32
AF = mybir.ActivationFunctionType
ALU = mybir.AluOpType
AX = mybir.AxisListType
BFNP = ml_dtypes.bfloat16

NCORE = 8
B, N, D = 2, 8192, 1024
T = 2048
NT = T // 128
ND = D // 128
H, dh = 16, 64
NE = 8
I0 = 3584
NI0 = I0 // 128
H1 = 4096
NH1 = H1 // 128
CAPL = 1024
RROWS = NE * CAPL
C1 = 5120
EPS_LN = 1e-5
EPS_RMS = 1e-6
EPS_G = 1.1920929e-07
OOBID = 10 ** 8

ENGS = ("tensor", "vector", "scalar", "gpsimd", "sync")
_SC_LIMIT = None
_SC_PHASE = None


class Prog:

    def __init__(self, name):
        self.name = name
        self.nc = bass.Bass("TRN2", target_bir_lowering=False, debug=False)
        self.stack = ExitStack()
        self.nbar = 0
        self.dma_cnt = {"sync": 0, "gpsimd": 0}
        self.block = None

    def sb(self, name, shape, dtype):
        return self.stack.enter_context(
            self.nc.sbuf_tensor(name, list(shape), dtype))

    def ps(self, name, shape, dtype=F32):
        shape = list(shape)
        per = int(np.prod(shape[1:])) * mybir.dt.size(dtype)
        assert per <= 2048, f"psum tensor {name} exceeds one bank"
        if per < 2048:
            shape = [shape[0], 2048 // mybir.dt.size(dtype)]
        return self.stack.enter_context(
            self.nc.psum_tensor(name, shape, dtype))

    def sem(self, name):
        return self.stack.enter_context(self.nc.semaphore(name))

    def use_sem(self, tag, nincs):
        if not hasattr(self, "_usems"):
            self._usems = {}
            self._usem_base = {}
        if tag not in self._usems:
            self._usems[tag] = self.sem(f"u_{tag}")
            self._usem_base[tag] = 0
        b = self._usem_base[tag]
        self._usem_base[tag] = b + nincs
        return self._usems[tag], b

    def inp(self, name, shape, dtype=F32):
        return self.nc.dram_tensor(name, list(shape), dtype,
                                   kind="ExternalInput").ap()

    def outp(self, name, shape, dtype=F32):
        return self.nc.dram_tensor(name, list(shape), dtype,
                                   kind="ExternalOutput").ap()

    def init_sems(self):
        self.bsem = {e: self.sem(f"b_{e}") for e in ENGS}
        self.ldsem = {"sync": self.sem("ld_sync"), "gpsimd": self.sem("ld_gp")}
        self.block = self.stack.enter_context(self.nc.Block())
        self.last_compute = {e: None for e in ENGS}
        self._proxies = {}
        self.bar_sb = self.sb("bar_sb", [1, 8], F32)
        self.bar_mm_src = None
        self.bar_ps = None
        self.vsem = self.sem("vsem")
        self.gsem = self.sem("gsem")
        self._vcnt = 0
        self._gcnt = 0

    _NON_COMPUTE = {"dma_start", "indirect_dma_start", "dma_start_transpose",
                    "wait_ge", "nop", "engine_nop", "maybe_nop_then_inc",
                    "then_inc"}

    class _EngProxy:
        def __init__(self, prog, eng_name, raw):
            self._p = prog
            self._e = eng_name
            self._raw = raw

        def __getattr__(self, name):
            fn = getattr(self._raw, name)
            if not callable(fn) or name.startswith("_"):
                return fn
            if name in Prog._NON_COMPUTE:
                return fn

            def wrapped(*a, **kw):
                p = self._p
                if self._e == "vector":
                    if p._vcnt > 0:
                        self._raw.wait_ge(p.vsem, p._vcnt)
                    r = fn(*a, **kw)
                    if r is not None and hasattr(r, "then_inc"):
                        r.then_inc(p.vsem, 1)
                        p._vcnt += 1
                    return r
                if self._e == "gpsimd":
                    if p._gcnt > 0:
                        self._raw.wait_ge(p.gsem, p._gcnt)
                    r = fn(*a, **kw)
                    if r is not None and hasattr(r, "then_inc"):
                        r.then_inc(p.gsem, 1)
                        p._gcnt += 1
                    return r
                r = fn(*a, **kw)
                if r is not None and hasattr(r, "then_inc"):
                    p.last_compute[self._e] = r
                return r
            return wrapped

    def on(self, eng):
        dec = getattr(self.block, eng)

        def deco(fn):
            def wrapped(raw):
                px = self._proxies.get(eng)
                if px is None or px._raw is not raw:
                    px = Prog._EngProxy(self, eng, raw)
                    self._proxies[eng] = px
                fn(px)
            dec(wrapped)
        return deco

    def dma(self, q, eng_obj, out, in_, **kw):
        self.dma_cnt[q] += 1
        eng_obj.dma_start(out=out, in_=in_, **kw).then_inc(self.ldsem[q], 16)

    def idma(self, eng_obj, **kw):
        self.dma_cnt["gpsimd"] += 1
        eng_obj.indirect_dma_start(**kw).then_inc(self.ldsem["gpsimd"], 16)

    def idma_lane(self, eng_obj, sem, **kw):
        eng_obj.indirect_dma_start(**kw).then_inc(sem, 16)

    def barrier(self):
        self.nbar += 1
        k = self.nbar
        cnt = dict(self.dma_cnt)
        for e in ENGS:
            li = self.last_compute[e]
            self.last_compute[e] = None
            busy = li is not None
            si = li.ins.sync_info if li is not None else None
            if li is not None and si is not None and len(si.on_update) > 0:
                li = None
            if li is not None:
                li.then_inc(self.bsem[e], 1)
            vcnt, gcnt = self._vcnt, self._gcnt

            @self.on(e)
            def _(eng, e=e, li=li, busy=busy):
                if e in ("sync", "gpsimd"):
                    eng.wait_ge(self.ldsem[e], 16 * cnt[e])
                if e == "vector":
                    eng._raw.wait_ge(self.vsem, vcnt) if hasattr(eng, "_raw")                         else eng.wait_ge(self.vsem, vcnt)
                    eng.nop().then_inc(self.bsem[e], 1)
                elif e == "gpsimd":
                    eng.wait_ge(self.gsem, gcnt)
                    eng.nop().then_inc(self.bsem[e], 1)
                elif li is None:
                    if busy and e == "scalar":
                        eng.memzero(self.bar_sb[0:1, 2:3]).then_inc(self.bsem[e], 1)
                    elif busy and e == "tensor" and self.bar_mm_src is not None:
                        eng.matmul(self.bar_ps[0:1, 0:1],
                                   self.bar_mm_src[0:1, 0:1],
                                   self.bar_mm_src[0:1, 0:1],
                                   start=True, stop=True).then_inc(self.bsem[e], 1)
                    else:
                        eng.nop().then_inc(self.bsem[e], 1)
                for o in ENGS:
                    eng.wait_ge(self.bsem[o], k)
                eng.wait_ge(self.ldsem["sync"], 16 * cnt["sync"])
                eng.wait_ge(self.ldsem["gpsimd"], 16 * cnt["gpsimd"])

    def finish(self):
        self.stack.close()
        return self.nc



def alloc_norm_scratch(p, nt):
    p.n_s = p.sb("n_s", [128, nt], F32)
    p.n_q = p.sb("n_q", [128, nt], F32)
    p.n_var = p.sb("n_var", [128, nt], F32)
    p.n_mu = p.sb("n_mu", [128, nt], F32)
    p.n_ab = p.sb("n_ab", [128, 2 * nt], F32)
    p.n_sq = p.sb("n_sq", [128, 1024], F32)
    p.n_t = p.sb("n_t", [128, 4], F32)


def emit_norm_stats(p, x_tiles, nt, eps, with_mean, width=D):
    @p.on("scalar")
    def _(s):
        for i in range(nt):
            s.activation(out=p.n_sq[:, 0:width], in_=x_tiles[i][:],
                         func=AF.Square, accum_out=p.n_q[:, i:i + 1])

    if with_mean:
        @p.on("vector")
        def _(v):
            for i in range(nt):
                v.reduce_sum(out=p.n_s[:, i:i + 1], in_=x_tiles[i][:], axis=AX.X)
    p.barrier()

    @p.on("vector")
    def _(v):
        inv = 1.0 / width
        if with_mean:
            v.tensor_scalar(out=p.n_mu[:, 0:nt], in0=p.n_s[:, 0:nt],
                            scalar1=inv, scalar2=None, op0=ALU.mult)
            v.tensor_scalar(out=p.n_var[:, 0:nt], in0=p.n_q[:, 0:nt],
                            scalar1=inv, scalar2=None, op0=ALU.mult)
            v.tensor_tensor(out=p.n_s[:, 0:nt], in0=p.n_mu[:, 0:nt],
                            in1=p.n_mu[:, 0:nt], op=ALU.mult)
            v.tensor_tensor(out=p.n_var[:, 0:nt], in0=p.n_var[:, 0:nt],
                            in1=p.n_s[:, 0:nt], op=ALU.subtract)
        else:
            v.tensor_scalar(out=p.n_var[:, 0:nt], in0=p.n_q[:, 0:nt],
                            scalar1=inv, scalar2=None, op0=ALU.mult)
        v.tensor_scalar(out=p.n_var[:, 0:nt], in0=p.n_var[:, 0:nt],
                        scalar1=eps, scalar2=None, op0=ALU.add)
    p.barrier()

    @p.on("scalar")
    def _(s):
        s.activation(out=p.n_var[:, 0:nt], in_=p.n_var[:, 0:nt], func=AF.Sqrt)
    p.barrier()

    @p.on("vector")
    def _(v):
        v.reciprocal(out=p.n_var[:, 0:nt], in_=p.n_var[:, 0:nt])
        for i in range(nt):
            v.tensor_copy(out=p.n_ab[:, 2 * i:2 * i + 1],
                          in_=p.n_var[:, i:i + 1])
            if with_mean:
                v.tensor_tensor(out=p.n_ab[:, 2 * i + 1:2 * i + 2],
                                in0=p.n_mu[:, i:i + 1], in1=p.n_var[:, i:i + 1],
                                op=ALU.mult)
                v.tensor_scalar(out=p.n_ab[:, 2 * i + 1:2 * i + 2],
                                in0=p.n_ab[:, 2 * i + 1:2 * i + 2],
                                scalar1=-1.0, scalar2=None, op0=ALU.mult)
            else:
                v.memset(p.n_ab[:, 2 * i + 1:2 * i + 2], 0.0)
    p.barrier()


def emit_norm_apply(p, x_tiles, out_tiles, nt):
    @p.on("scalar")
    def _(s):
        for i in range(nt):
            s.activation(out=out_tiles[i][:], in_=x_tiles[i][:], func=AF.Identity,
                         scale=p.n_ab[:, 2 * i:2 * i + 1],
                         bias=p.n_ab[:, 2 * i + 1:2 * i + 2])
    p.barrier()


def alloc_router_scratch(p, nt):
    p.r_nm = p.sb("r_nm", [128, nt], F32)
    p.r_sum = p.sb("r_sum", [128, nt], F32)
    p.r_rk = p.sb("r_rk", [128, 8], F32)
    p.r_64 = p.sb("r_64", [128, 64], F32)
    p.r_64b = p.sb("r_64b", [128, 64], F32)
    p.r_t8 = p.sb("r_t8", [128, 8], F32)
    p.r_t2 = p.sb("r_t2", [128, 2], F32)


def emit_router(p, lg_tiles, sc_tiles, m0_t, m1_t, g01_t, ltmask, nt, clip_eps):
    @p.on("vector")
    def _(v):
        for i in range(nt):
            v.reduce_max(out=p.r_nm[:, i:i + 1], in_=lg_tiles[i][:], axis=AX.X)
        v.tensor_scalar(out=p.r_nm[:, 0:nt], in0=p.r_nm[:, 0:nt],
                        scalar1=-1.0, scalar2=None, op0=ALU.mult)
    p.barrier()

    @p.on("scalar")
    def _(s):
        for i in range(nt):
            s.activation(out=sc_tiles[i][:], in_=lg_tiles[i][:], func=AF.Exp,
                         bias=p.r_nm[:, i:i + 1],
                         accum_out=p.r_sum[:, i:i + 1])
    p.barrier()

    @p.on("vector")
    def _(v):
        v.reciprocal(out=p.r_sum[:, 0:nt], in_=p.r_sum[:, 0:nt])
        for i in range(nt):
            v.tensor_scalar(out=sc_tiles[i][:], in0=sc_tiles[i][:],
                            scalar1=p.r_sum[:, i:i + 1], scalar2=None,
                            op0=ALU.mult)
    p.barrier()

    @p.on("vector")
    def _(v):
        for i in range(nt):
            sc = sc_tiles[i][:]
            A = sc.rearrange("p (a e) -> p a e", a=1).broadcast_to([128, 8, 8])
            Bv = sc.rearrange("p (e a) -> p e a", a=1).broadcast_to([128, 8, 8])
            g3 = p.r_64[:].rearrange("p (e f) -> p e f", f=8)
            g3b = p.r_64b[:].rearrange("p (e f) -> p e f", f=8)
            v.tensor_tensor(out=g3, in0=Bv, in1=A, op=ALU.is_lt)
            v.reduce_sum(out=p.r_rk[:].rearrange("p (e a) -> p e a", a=1),
                         in_=g3, axis=AX.X)
            v.tensor_tensor(out=g3, in0=Bv, in1=A, op=ALU.is_equal)
            v.tensor_tensor(out=g3b, in0=g3,
                            in1=ltmask[:].rearrange("p (e f) -> p e f", f=8),
                            op=ALU.mult)
            v.reduce_sum(out=p.r_t8[:].rearrange("p (e a) -> p e a", a=1),
                         in_=g3b, axis=AX.X)
            v.tensor_tensor(out=p.r_rk[:], in0=p.r_rk[:], in1=p.r_t8[:],
                            op=ALU.add)
            v.tensor_scalar(out=m0_t[i][:], in0=p.r_rk[:], scalar1=0.5,
                            scalar2=None, op0=ALU.is_lt)
            v.tensor_scalar(out=m1_t[i][:], in0=p.r_rk[:], scalar1=0.5,
                            scalar2=None, op0=ALU.is_ge)
            v.tensor_scalar(out=p.r_t8[:], in0=p.r_rk[:], scalar1=1.5,
                            scalar2=None, op0=ALU.is_lt)
            v.tensor_tensor(out=m1_t[i][:], in0=m1_t[i][:], in1=p.r_t8[:],
                            op=ALU.mult)
            v.tensor_tensor(out=p.r_t8[:], in0=m0_t[i][:], in1=sc, op=ALU.mult)
            v.reduce_sum(out=g01_t[i][:, 0:1], in_=p.r_t8[:], axis=AX.X)
            v.tensor_tensor(out=p.r_t8[:], in0=m1_t[i][:], in1=sc, op=ALU.mult)
            v.reduce_sum(out=g01_t[i][:, 1:2], in_=p.r_t8[:], axis=AX.X)
            v.tensor_tensor(out=p.r_t2[:, 0:1], in0=g01_t[i][:, 0:1],
                            in1=g01_t[i][:, 1:2], op=ALU.add)
            if clip_eps is not None:
                v.tensor_scalar(out=p.r_t2[:, 0:1], in0=p.r_t2[:, 0:1],
                                scalar1=clip_eps, scalar2=None, op0=ALU.max)
            v.reciprocal(out=p.r_t2[:, 0:1], in_=p.r_t2[:, 0:1])
            v.tensor_scalar(out=g01_t[i][:], in0=g01_t[i][:],
                            scalar1=p.r_t2[:, 0:1], scalar2=None, op0=ALU.mult)
    p.barrier()


def _ap_base(ap):
    row = int(np.prod(ap.tensor.shape[1:]))
    return int(ap.offset) // row if row else 0


def emit_transpose_seq(p, pairs, ident, tps):
    p._tseq = getattr(p, "_tseq", 0) + 1
    s_t = p.sem(f"tt{p._tseq}")
    nb = len(tps)
    v0 = p._vcnt

    @p.on("tensor")
    def _(t):
        for k, (src, dst) in enumerate(pairs):
            if k >= nb:
                t.wait_ge(p.vsem, v0 + k - nb + 1)
            bp_ = src.tensor.base_partition if hasattr(src.tensor, "base_partition") else 0
            b0 = src.ap[0]
            t.transpose(tps[k % nb][0:src.shape[1], 0:src.shape[0]], src,
                        ident[0:src.shape[0], 0:src.shape[0]]
                        if _ap_base(src) == 0 else
                        ident[_ap_base(src):_ap_base(src) + src.shape[0],
                              _ap_base(src):_ap_base(src) + src.shape[0]]
                        ).then_inc(s_t, 1)

    @p.on("vector")
    def _(v):
        for k, (src, dst) in enumerate(pairs):
            v.wait_ge(s_t, k + 1)
            v.tensor_copy(out=dst,
                          in_=tps[k % nb][0:src.shape[1], 0:src.shape[0]])
    p.barrier()


def emit_perhead_to_chan(p, row16, out128, Asel, Bsel, colps, outps, scr16):
    @p.on("tensor")
    def _(t):
        t.transpose(colps[0:16, 0:1], row16[:], p._id16[0:1, 0:1])
    p.barrier()

    @p.on("vector")
    def _(v):
        v.tensor_copy(out=scr16[:, 0:1], in_=colps[0:16, 0:1])
        v.tensor_scalar(out=scr16[:, 1:9], in0=Bsel[:], scalar1=scr16[:, 0:1],
                        scalar2=None, op0=ALU.mult)
    p.barrier()

    @p.on("tensor")
    def _(t):
        t.matmul(outps[:, 0:8], Asel[:], scr16[:, 1:9], start=True, stop=True)
    p.barrier()

    @p.on("vector")
    def _(v):
        v.tensor_copy(out=out128[:], in_=outps[:, 0:8])
    p.barrier()



def build_p1(debug=False):
    p = Prog("p1")
    x = p.inp("x", [T, D])
    gate_wT = p.inp("gate_wT", [D, NE])
    rms_w_pre = p.inp("rms_w_pre", [128, ND])
    lam_log = p.inp("lam_log", [1, H])
    ltmask_i = p.inp("ltmask", [128, 64])
    ident_i = p.inp("ident", [128, 128])
    tokbase = p.inp("tokbase", [128, 1])
    asel_i = p.inp("asel", [16, 128])
    bsel_i = p.inp("bsel", [16, 8])
    rms_row = p.inp("rms_row", [1, D])

    if debug:
        dbg_xn = p.outp("dbg_xn", [128, D])
        dbg_lamc = p.outp("dbg_lamc", [128, ND])
        dbg_ab = p.outp("dbg_ab", [128, 2 * NT])
        dbg_xnfm = p.outp("dbg_xnfm", [128, 256])
        dbg_lg = p.outp("dbg_lg", [128, 8])
        dbg_rk = p.outp("dbg_rk", [128, 8])
    h_out = p.outp("h_tm", [T, D], BF16)
    uend_out = p.outp("u_end", [1, D])
    ids_out = p.outp("ids", [NE, CAPL])
    pos_out = p.outp("pos", [T, 2])
    g01_out = p.outp("g01", [T, 2])

    p.init_sems()
    xt = [p.sb(f"xt{i}", [128, D], F32) for i in range(NT)]
    xn_fm = [p.sb(f"xnfm{k}", [128, T], F32) for k in range(ND)]
    u1 = p.sb("u1", [128, T], F32)
    ht = [p.sb(f"ht{i}", [128, D], BF16) for i in range(2)]
    lamc = p.sb("lamc", [128, ND], F32)
    gwp = [p.sb(f"gwp{k}", [128, 8], F32) for k in range(ND)]
    lgt = [p.sb(f"lgt{i}", [128, 8], F32) for i in range(NT)]
    sct = [p.sb(f"sct{i}", [128, 8], F32) for i in range(NT)]
    m0t = [p.sb(f"m0t{i}", [128, 8], F32) for i in range(NT)]
    m1t = [p.sb(f"m1t{i}", [128, 8], F32) for i in range(NT)]
    g01t = [p.sb(f"g01t{i}", [128, 2], F32) for i in range(NT)]
    rk_tm = [p.sb(f"rktm{i}", [128, 8], F32) for i in range(NT)]
    ltmask = p.sb("ltm", [128, 64], F32)
    ident = p.sb("id128", [128, 128], F32)
    p._id16 = ident
    lamrow = p.sb("lamrow", [1, H], F32)
    asel = p.sb("asel_s", [16, 128], F32)
    bsel = p.sb("bsel_s", [16, 8], F32)
    scr16 = p.sb("scr16", [16, 16], F32)
    mscan = p.sb("mscan", [128, T], F32)
    rkscan = p.sb("rkscan", [128, T], F32)
    z81 = p.sb("z81", [128, 1], F32)
    tokb = p.sb("tokb", [128, 1], F32)
    iotap = p.sb("iotap", [128, 1], F32)
    econ = p.sb("econ", [128, 8], F32)
    scr = p.sb("scrA", [128, 16], F32)
    scrB = p.sb("scrB", [128, 16], F32)
    dstu = p.sb("dstu", [128, 1], U32)
    vals = p.sb("vals", [128, 1], F32)
    posf = [p.sb(f"posf{i}", [128, 2], F32) for i in range(NT)]
    oobfill = p.sb("oobfill", [128, 64], F32)
    ue8 = p.sb("ue8", [128, ND], F32)
    rmsb = p.sb("rmsb", [128, ND], F32)
    tp2 = [p.ps(f"tp{i}", [128, 128]) for i in range(2)]
    lgps = p.ps("lgps", [128, 8])
    php = p.ps("php", [128, 128])
    p.bar_ps = p.ps("barps", [128, 512])
    p.bar_mm_src = ident
    alloc_norm_scratch(p, NT)
    alloc_router_scratch(p, NT)

    @p.on("sync")
    def _(s):
        for i in range(NT):
            p.dma("sync", s, xt[i][:], x[128 * i:128 * (i + 1), :])
        p.dma("sync", s, ltmask[:], ltmask_i)
        p.dma("sync", s, ident[:], ident_i)
        p.dma("sync", s, lamrow[:], lam_log)
        p.dma("sync", s, tokb[:], tokbase)
        p.dma("sync", s, rmsb[:], rms_w_pre)
        p.dma("sync", s, asel[:], asel_i)
        p.dma("sync", s, bsel[:], bsel_i)
        for k in range(ND):
            p.dma("sync", s, gwp[k][:], gate_wT[128 * k:128 * (k + 1), :])

    @p.on("gpsimd")
    def _(g):
        g.iota(iotap[:], pattern=[[0, 1]], base=0, channel_multiplier=1,
               allow_small_or_imprecise_dtypes=True)
        g.iota(econ[:], pattern=[[1, 8]], base=0, channel_multiplier=0,
               allow_small_or_imprecise_dtypes=True)
        g.memset(oobfill[:], float(OOBID))
        g.memset(z81[:], 0.0)
    p.barrier()

    @p.on("gpsimd")
    def _(g):
        p.dma("gpsimd", g,
              ids_out.rearrange("e (r q) -> (e r) q", q=64), oobfill[:])
    p.barrier()

    emit_norm_stats(p, xt, NT, EPS_LN, with_mean=True)
    emit_norm_apply(p, xt, xt, NT)

    pairs = []
    for i in range(NT):
        for k in range(ND):
            pairs.append((xt[i][:, 128 * k:128 * (k + 1)],
                          xn_fm[k][:, 128 * i:128 * (i + 1)]))
    emit_transpose_seq(p, pairs, ident, tp2)

    @p.on("scalar")
    def _(s):
        s.activation(out=lamrow[:], in_=lamrow[:], func=AF.Sigmoid)
    p.barrier()
    emit_perhead_to_chan(p, lamrow, lamc, asel, bsel, tp2[0], php, scr16)

    @p.on("vector")
    def _(v):
        for k in range(ND):
            v.tensor_tensor_scan(out=u1[:], data0=lamc[:, k:k + 1].broadcast_to([128, T]),
                                 data1=xn_fm[k][:], initial=0.0,
                                 op0=ALU.mult, op1=ALU.add)
            v.tensor_copy(out=ue8[:, k:k + 1], in_=u1[:, T - 1:T])
    p.barrier()
    @p.on("tensor")
    def _(t):
        t.transpose(tp2[0][0:8, 0:128], ue8[:], ident[:])
    p.barrier()

    @p.on("vector")
    def _(v):
        v.tensor_copy(out=mscan[0:8, 0:128], in_=tp2[0][0:8, 0:128])
    p.barrier()

    @p.on("gpsimd")
    def _(g):
        p.dma("gpsimd", g, uend_out.rearrange("a (k q) -> (a k) q", q=128),
              mscan[0:8, 0:128])
    p.barrier()

    emit_norm_stats(p, xt, NT, EPS_RMS, with_mean=False)

    rmsw_row = p.sb("rmsw_row", [128, D], F32)
    onescol = p.sb("onescol", [1, 128], F32)
    rmsrow_s = p.sb("rmsrow_s", [1, D], F32)
    rps = [p.ps(f"rps{i}", [128, 512]) for i in range(2)]

    @p.on("vector")
    def _(v):
        v.memset(onescol[:], 1.0)

    @p.on("sync")
    def _(s):
        p.dma("sync", s, rmsrow_s[:], rms_row)
    p.barrier()

    @p.on("tensor")
    def _(t):
        for half in range(2):
            t.matmul(rps[half][:], onescol[:],
                     rmsrow_s[:, 512 * half:512 * (half + 1)],
                     start=True, stop=True)
    p.barrier()

    @p.on("vector")
    def _(v):
        for half in range(2):
            v.tensor_copy(out=rmsw_row[:, 512 * half:512 * (half + 1)],
                          in_=rps[half][:])
    p.barrier()

    v0h = p._vcnt
    dbase = p.dma_cnt["gpsimd"]

    @p.on("vector")
    def _(v):
        for i in range(NT):
            if i >= 2:
                v.wait_ge(p.ldsem["gpsimd"], 16 * (dbase + i - 1))
            v.scalar_tensor_tensor(out=ht[i % 2][:], in0=xt[i][:],
                                   scalar=p.n_ab[:, 2 * i:2 * i + 1],
                                   in1=rmsw_row[:], op0=ALU.mult,
                                   op1=ALU.mult)

    @p.on("gpsimd")
    def _(g):
        for i in range(NT):
            g.wait_ge(p.vsem, v0h + i + 1)
            p.dma("gpsimd", g, h_out[128 * i:128 * (i + 1), :], ht[i % 2][:])
    p.barrier()

    @p.on("vector")
    def _(v):
        for k in range(ND):
            v.tensor_scalar(out=gwp[k][:], in0=gwp[k][:],
                            scalar1=rmsb[:, k:k + 1], scalar2=None,
                            op0=ALU.mult)
    p.barrier()

    lgsem = p.sem("lgsem")
    v0l = p._vcnt

    @p.on("tensor")
    def _(t):
        for i in range(NT):
            if i >= 1:
                t.wait_ge(p.vsem, v0l + i)
            for k in range(ND):
                mm = t.matmul(lgps[:, 0:8], xn_fm[k][:, 128 * i:128 * (i + 1)],
                              gwp[k][:], start=(k == 0), stop=(k == ND - 1))
            mm.then_inc(lgsem, 1)

    @p.on("vector")
    def _(v):
        for i in range(NT):
            v.wait_ge(lgsem, i + 1)
            v.tensor_scalar(out=lgt[i][:], in0=lgps[:, 0:8],
                            scalar1=p.n_ab[:, 2 * i:2 * i + 1], scalar2=None,
                            op0=ALU.mult)
    p.barrier()

    emit_router(p, lgt, sct, m0t, m1t, g01t, ltmask, NT, clip_eps=None)

    @p.on("vector")
    def _(v):
        for i in range(NT):
            v.tensor_tensor(out=rk_tm[i][:], in0=m0t[i][:], in1=m1t[i][:],
                            op=ALU.add)
    p.barrier()
    pairs = [(rk_tm[i][:, 0:8], mscan[0:8, 128 * i:128 * (i + 1)])
             for i in range(NT)]
    emit_transpose_seq(p, pairs, ident, tp2)

    @p.on("vector")
    def _(v):
        v.tensor_tensor_scan(out=rkscan[0:8, 0:T],
                             data0=z81[0:8, 0:1].broadcast_to([8, T]),
                             data1=mscan[0:8, 0:T], initial=0.0,
                             op0=ALU.add, op1=ALU.add)
    p.barrier()
    pairs = [(rkscan[0:8, 128 * i:128 * (i + 1)], rk_tm[i][:, 0:8])
             for i in range(NT)]
    emit_transpose_seq(p, pairs, ident, tp2)

    if debug:
        @p.on("sync")
        def _(s):
            p.dma("sync", s, dbg_xn, xt[0][:])
            p.dma("sync", s, dbg_lamc, lamc[:])
            p.dma("sync", s, dbg_ab, p.n_ab[:])
            p.dma("sync", s, dbg_xnfm, xn_fm[0][:, 0:256])
            p.dma("sync", s, dbg_lg, lgt[0][:])
            p.dma("sync", s, dbg_rk, rk_tm[0][:])
        p.barrier()

    @p.on("vector")
    def _(v):
        for i in range(NT):
            for j, mt in ((0, m0t), (1, m1t)):
                v.tensor_tensor(out=scr[:, 0:8], in0=mt[i][:], in1=rk_tm[i][:],
                                op=ALU.mult)
                v.reduce_sum(out=scr[:, 8:9], in_=scr[:, 0:8], axis=AX.X)
                v.tensor_tensor(out=scr[:, 0:8], in0=mt[i][:], in1=econ[:],
                                op=ALU.mult)
                v.reduce_sum(out=scr[:, 9:10], in_=scr[:, 0:8], axis=AX.X)
                v.tensor_scalar(out=scr[:, 10:11], in0=scr[:, 9:10],
                                scalar1=float(CAPL), scalar2=None, op0=ALU.mult)
                v.tensor_tensor(out=scr[:, 10:11], in0=scr[:, 10:11],
                                in1=scr[:, 8:9], op=ALU.add)
                v.tensor_scalar(out=scr[:, 10:11], in0=scr[:, 10:11],
                                scalar1=-1.0, scalar2=None, op0=ALU.add)
                v.tensor_scalar(out=scr[:, 11:12], in0=scr[:, 8:9],
                                scalar1=float(CAPL) + 0.5, scalar2=None,
                                op0=ALU.is_gt)
                v.tensor_scalar(out=scr[:, 11:12], in0=scr[:, 11:12],
                                scalar1=float(10 ** 7), scalar2=None,
                                op0=ALU.mult)
                v.tensor_tensor(out=posf[i][:, j:j + 1], in0=scr[:, 10:11],
                                in1=scr[:, 11:12], op=ALU.add)
    p.barrier()

    ids_flat = ids_out.rearrange("e (c a) -> (e c) a", a=1)
    v0s = p._vcnt
    dbase2 = p.dma_cnt["gpsimd"]

    @p.on("vector")
    def _(v):
        for i in range(NT):
            for j in range(2):
                s = 2 * i + j
                if s >= 1:
                    v.wait_ge(p.ldsem["gpsimd"], 16 * (dbase2 + s))
                v.tensor_tensor(out=vals[:], in0=iotap[:], in1=tokb[:],
                                op=ALU.add)
                v.tensor_scalar(out=vals[:], in0=vals[:],
                                scalar1=float(128 * i), scalar2=None,
                                op0=ALU.add)
                v.tensor_copy(out=dstu[:], in_=posf[i][:, j:j + 1])

    @p.on("gpsimd")
    def _(g):
        for i in range(NT):
            for j in range(2):
                s = 2 * i + j
                g.wait_ge(p.vsem, v0s + 3 * (s + 1))
                p.idma(g, out=ids_flat, out_offset=IndirectOffsetOnAxis(
                    ap=dstu[:], axis=0), in_=vals[:], in_offset=None,
                    bounds_check=NE * CAPL - 1, oob_is_err=False)
    p.barrier()

    @p.on("sync")
    def _(s):
        for i in range(NT):
            p.dma("sync", s, pos_out[128 * i:128 * (i + 1), :], posf[i][:])
            p.dma("sync", s, g01_out[128 * i:128 * (i + 1), :], g01t[i][:])
    p.barrier()
    return p.finish()



def build_p2():
    p = Prog("p2")
    RH = RROWS // 2
    NTQ = RH // 512
    NRT = RH // 128

    w1T = p.inp("w1T", [D, I0])
    w3T = p.inp("w3T", [D, I0])
    w2T = p.inp("w2T", [I0, D])
    hfull = p.inp("hfull", [NCORE * T, D], BF16)
    myids = p.inp("myids", [128, RROWS // 128])
    ident_i = p.inp("ident", [128, 128])
    y_out = p.outp("y0", [RROWS, D], BF16)

    p.init_sems()
    ident = p.sb("id128", [128, 128], F32)
    idsb = p.sb("idsb", [128, RROWS // 128], F32)
    idsu = p.sb("idsu", [128, RROWS // 128], U32)
    hrow = [p.sb(f"hrow{i}", [128, D], BF16) for i in range(4)]
    h_fm = [p.sb(f"hfm{k}", [128, RH], BF16) for k in range(ND)]
    w1s = [p.sb(f"w1s{i}", [128, ND, 128], BF16) for i in range(2)]
    w3s = [p.sb(f"w3s{i}", [128, ND, 128], BF16) for i in range(2)]
    w2s = [p.sb(f"w2s{i}", [128, NI0, 128], BF16) for i in range(2)]
    asil = p.sb("asil", [128, 512], BF16)
    act = [p.sb(f"act{i}", [128, 512], BF16) for i in range(NI0)]
    y_fm = [p.sb(f"yfm{m}", [128, 512], BF16) for m in range(ND)]
    y_tm = [p.sb(f"ytm{i}", [128, D], BF16) for i in range(2)]
    aps = p.ps("aps", [128, 512])
    gps = p.ps("gps", [128, 512])
    eps2 = [p.ps(f"eps{i}", [128, 512]) for i in range(2)]
    tpp = [p.ps(f"tpp{i}", [128, 1024], BF16) for i in range(2)]
    p.bar_ps = p.ps("barps", [128, 512])
    p.bar_mm_src = ident

    w1b = p.nc.dram_tensor("w1b", [D, I0], BF16, kind="Internal").ap()
    w3b = p.nc.dram_tensor("w3b", [D, I0], BF16, kind="Internal").ap()
    w2b = p.nc.dram_tensor("w2b", [I0, D], BF16, kind="Internal").ap()

    @p.on("sync")
    def _(s):
        p.dma("sync", s, ident[:], ident_i)
        p.dma("sync", s, idsb[:], myids)

    @p.on("gpsimd")
    def _(g):
        p.dma("gpsimd", g, w1b, w1T)
        p.dma("gpsimd", g, w3b, w3T)
        p.dma("gpsimd", g, w2b, w2T)
    p.barrier()

    identb = p.sb("id128b", [128, 128], BF16)

    @p.on("vector")
    def _(v):
        v.tensor_scalar(out=idsb[:], in0=idsb[:], scalar1=float(NCORE * T - 1),
                        scalar2=None, op0=ALU.min)
        v.tensor_copy(out=idsu[:], in_=idsb[:])
        v.tensor_copy(out=identb[:], in_=ident[:])
    p.barrier()

    for half in range(2):
        s_t = p.sem(f"h{half}t")
        v0 = p._vcnt
        d0 = p.dma_cnt["gpsimd"]
        tick_ms = lambda r: v0 + 9 * r + 1
        tick_ev = lambda kk: v0 + 9 * (kk // 8) + (kk % 8) + 2

        @p.on("vector")
        def _(v):
            for r in range(NRT):
                if r >= 4:
                    v.wait_ge(s_t, 8 * (r - 3))
                v.memset(hrow[r % 4][:], 0.0)
                for k in range(ND):
                    kk = 8 * r + k
                    v.wait_ge(s_t, kk + 1)
                    v.tensor_copy(out=h_fm[k][:, 128 * r:128 * (r + 1)],
                                  in_=tpp[kk % 2][0:128, 0:128])

        sgl = [p.sem(f"sg{half}_{l}") for l in range(4)]

        @p.on("gpsimd")
        def _(g):
            for r in range(NRT):
                g.wait_ge(p.vsem, tick_ms(r))
                if r >= 4:
                    g.wait_ge(sgl[r % 4], 16 * (r // 4))
                p.idma_lane(g, sgl[r % 4], out=hrow[r % 4][:], out_offset=None,
                            in_=hfull,
                            in_offset=IndirectOffsetOnAxis(
                                ap=idsu[:, half * NRT + r:half * NRT + r + 1],
                                axis=0))

        @p.on("tensor")
        def _(t):
            for r in range(NRT):
                t.wait_ge(sgl[r % 4], 16 * (r // 4 + 1))
                for k in range(ND):
                    kk = 8 * r + k
                    if kk >= 2:
                        t.wait_ge(p.vsem, tick_ev(kk - 2))
                    t.transpose(tpp[kk % 2][0:128, 0:128],
                                hrow[r % 4][:, 128 * k:128 * (k + 1)],
                                identb[:]).then_inc(s_t, 1)
        p.barrier()

        for tq in range(NTQ):
            s_w = p.sem(f"w{half}_{tq}")
            s_mm = p.sem(f"mm{half}_{tq}")
            s_si = p.sem(f"si{half}_{tq}")
            d1 = p.dma_cnt["sync"]
            v1 = p._vcnt

            @p.on("sync")
            def _(s):
                for i in range(NI0):
                    buf = i % 2
                    if i >= 2:
                        s.wait_ge(s_mm, i - 1)
                    p.dma("sync", s, w1s[buf][:],
                          w1b.rearrange("(k pp) i -> pp k i", pp=128)
                          [:, :, 128 * i:128 * (i + 1)])
                    p.dma("sync", s, w3s[buf][:],
                          w3b.rearrange("(k pp) i -> pp k i", pp=128)
                          [:, :, 128 * i:128 * (i + 1)])

            @p.on("tensor")
            def _(t):
                for i in range(NI0):
                    buf = i % 2
                    t.wait_ge(p.ldsem["sync"], 16 * (d1 + 2 * (i + 1)))
                    if i >= 1:
                        t.wait_ge(s_si, i)
                        t.wait_ge(p.vsem, v1 + i)
                    for k in range(ND):
                        t.matmul(aps[:], w1s[buf][:, k, :],
                                 h_fm[k][:, 512 * tq:512 * (tq + 1)],
                                 start=(k == 0), stop=(k == ND - 1))
                    for k in range(ND):
                        mm = t.matmul(gps[:], w3s[buf][:, k, :],
                                      h_fm[k][:, 512 * tq:512 * (tq + 1)],
                                      start=(k == 0), stop=(k == ND - 1))
                    mm.then_inc(s_mm, 1)

            @p.on("scalar")
            def _(s):
                for i in range(NI0):
                    s.wait_ge(s_mm, i + 1)
                    if i >= 1:
                        s.wait_ge(p.vsem, v1 + i)
                    s.activation(out=asil[:], in_=aps[:],
                                 func=AF.Silu).then_inc(s_si, 1)

            @p.on("vector")
            def _(v):
                for i in range(NI0):
                    v.wait_ge(s_si, i + 1)
                    v.tensor_tensor(out=act[i][:], in0=asil[:], in1=gps[:],
                                    op=ALU.mult)
            p.barrier()

            s_eo = p.sem(f"eo{half}_{tq}")
            d2 = p.dma_cnt["sync"]
            v2 = p._vcnt

            @p.on("sync")
            def _(s):
                for m in range(ND):
                    buf = m % 2
                    if m >= 2:
                        s.wait_ge(s_eo, m - 1)
                    p.dma("sync", s, w2s[buf][:],
                          w2b.rearrange("(ii pp) dd -> pp ii dd", pp=128)
                          [:, :, 128 * m:128 * (m + 1)])

            @p.on("tensor")
            def _(t):
                for m in range(ND):
                    buf = m % 2
                    t.wait_ge(p.ldsem["sync"], 16 * (d2 + m + 1))
                    if m >= 2:
                        t.wait_ge(p.vsem, v2 + m - 1)
                    for i in range(NI0):
                        mm = t.matmul(eps2[m % 2][:], w2s[buf][:, i, :],
                                      act[i][:],
                                      start=(i == 0), stop=(i == NI0 - 1))
                    mm.then_inc(s_eo, 1)

            @p.on("vector")
            def _(v):
                for m in range(ND):
                    v.wait_ge(s_eo, m + 1)
                    v.tensor_copy(out=y_fm[m][:], in_=eps2[m % 2][:])
            p.barrier()

            s_yt = p.sem(f"yt{half}_{tq}")
            v3 = p._vcnt
            d3 = p.dma_cnt["gpsimd"]

            @p.on("tensor")
            def _(t):
                for rr in range(4):
                    for m in range(ND):
                        kk = 8 * rr + m
                        if kk >= 2:
                            t.wait_ge(p.vsem, v3 + kk - 1)
                        t.transpose(tpp[kk % 2][0:128, 0:128],
                                    y_fm[m][:, 128 * rr:128 * (rr + 1)],
                                    identb[:]).then_inc(s_yt, 1)

            @p.on("vector")
            def _(v):
                for rr in range(4):
                    if rr >= 2:
                        v.wait_ge(p.ldsem["gpsimd"], 16 * (d3 + rr - 1))
                    for m in range(ND):
                        kk = 8 * rr + m
                        v.wait_ge(s_yt, kk + 1)
                        v.tensor_copy(out=y_tm[rr % 2][:, 128 * m:128 * (m + 1)],
                                      in_=tpp[kk % 2][0:128, 0:128])

            @p.on("gpsimd")
            def _(g):
                for rr in range(4):
                    g.wait_ge(p.vsem, v3 + 8 * (rr + 1))
                    row0 = half * RH + 512 * tq + 128 * rr
                    p.dma("gpsimd", g, y_out[row0:row0 + 128, :],
                          y_tm[rr % 2][:])
            p.barrier()
    return p.finish()



def build_p3():
    p = Prog("p3")
    x = p.inp("x", [T, D])
    uends = p.inp("uends", [128, NE, ND])
    cec = p.inp("cec", [128, NE])
    cmask = p.inp("cmask", [128, NE])
    lam_log = p.inp("lam_log", [1, H])
    qvec = p.inp("qvec", [H, dh])
    kvec = p.inp("kvec", [H, dh])
    nw_row = p.inp("nw_row", [1, D])
    o_wT = p.inp("o_wT", [D, D])
    y0own = p.inp("y0own", [RROWS, D], BF16)
    pos0 = p.inp("pos0", [T, 2])
    g01_0 = p.inp("g01_0", [T, 2])
    wg = p.inp("wg", [D, NE])
    ltmask_i = p.inp("ltmask", [128, 64])
    ident_i = p.inp("ident", [128, 128])
    tokbase = p.inp("tokbase", [128, 1])
    asel_i = p.inp("asel", [16, 128])
    bsel_i = p.inp("bsel", [16, 8])

    s0g0_out = p.outp("s0g0", [T, D], BF16)
    xn1_out = p.outp("xn1", [T, D], BF16)
    ids1_out = p.outp("ids1", [NE, CAPL])
    pos1_out = p.outp("pos1", [T, 2])
    idx1_out = p.outp("idx1", [T, 2])
    locj_out = p.outp("locj", [T, 2])
    g011_out = p.outp("g011", [T, 2])
    cnts_out = p.outp("cnts", [2, NE])

    p.init_sems()
    xs = [p.sb(f"xs{i}", [128, D], F32) for i in range(3)]
    xn0b = [p.sb(f"xn0b{i}", [128, D], BF16) for i in range(NT)]
    s0p = [p.sb(f"s0p{i}", [128, D], BF16) for i in range(NT)]
    otb = [p.sb(f"otb{i}", [128, D], BF16) for i in range(NT)]
    fmb = [p.sb(f"fmb{k}", [128, T], BF16) for k in range(ND)]
    p8 = [p.sb(f"p8_{i}", [128, T], F32) for i in range(2)]
    xnfb = p.sb("xnfb", [128, T], BF16)
    lamc = p.sb("lamc", [128, ND], F32)
    cpp = p.sb("cpp", [128, ND], F32)
    carry = p.sb("carry", [128, ND], F32)
    loglam = p.sb("loglam", [128, ND], F32)
    tmp8 = p.sb("tmp8", [128, ND], F32)
    uendt = p.sb("uendt", [128, NE, ND], F32)
    cect = p.sb("cect", [128, NE], F32)
    cmaskt = p.sb("cmaskt", [128, NE], F32)
    qkt = p.sb("qkt", [16, 64], F32)
    qk2 = p.sb("qk2", [16, 64], F32)
    ccol = p.sb("ccol", [16, 16], F32)
    nwb = p.sb("nwb", [128, D], F32)
    nwrow_s = p.sb("nwrow_s", [1, D], F32)
    onescol = p.sb("onescol", [1, 128], F32)
    yrow = [p.sb(f"yrow{i}", [128, D], BF16) for i in range(2)]
    posb = p.sb("posb", [128, NT, 2], F32)
    posu = p.sb("posu", [128, NT, 2], U32)
    g01b = p.sb("g01b", [128, NT, 2], F32)
    wgs = [p.sb(f"wgs{k}", [128, 8], F32) for k in range(ND)]
    wgsb = [p.sb(f"wgsb{k}", [128, 8], BF16) for k in range(ND)]
    lgt = [p.sb(f"lgt{i}", [128, 8], F32) for i in range(NT)]
    sct = [p.sb(f"sct{i}", [128, 8], F32) for i in range(NT)]
    m0t = [p.sb(f"m0t{i}", [128, 8], F32) for i in range(NT)]
    m1t = [p.sb(f"m1t{i}", [128, 8], F32) for i in range(NT)]
    g01t = [p.sb(f"g01t{i}", [128, 2], F32) for i in range(NT)]
    rk_tm = [p.sb(f"rktm{i}", [128, 8], F32) for i in range(NT)]
    rkj_tm = [p.sb(f"rkj{i}", [128, 16], F32) for i in range(NT)]
    ltmask = p.sb("ltm", [128, 64], F32)
    ident = p.sb("id128", [128, 128], F32)
    identb = p.sb("id128b", [128, 128], BF16)
    p._id16 = ident
    lamrow = p.sb("lamrow", [1, H], F32)
    asel = p.sb("asel_s", [16, 128], F32)
    bsel = p.sb("bsel_s", [16, 8], F32)
    scr16 = p.sb("scr16", [16, 16], F32)
    z81 = p.sb("z81", [128, 1], F32)
    tokb = p.sb("tokb", [128, 1], F32)
    iotap = p.sb("iotap", [128, 1], F32)
    econ = p.sb("econ", [128, 8], F32)
    scr = p.sb("scrA", [128, 16], F32)
    dstu = p.sb("dstu", [128, 1], U32)
    vals = p.sb("vals", [128, 1], F32)
    posf1 = [p.sb(f"posf1_{i}", [128, 2], F32) for i in range(NT)]
    locjf = [p.sb(f"locjf{i}", [128, 2], F32) for i in range(NT)]
    idxf = [p.sb(f"idxf{i}", [128, 2], F32) for i in range(NT)]
    oobfill = p.sb("oobfill", [128, 64], F32)
    tp2 = [p.ps(f"tp{i}", [128, 128]) for i in range(2)]
    tpb = [p.ps(f"tpb{i}", [128, 128], BF16) for i in range(2)]
    lgps = p.ps("lgps", [128, 8])
    mmps = [p.ps(f"mmps{i}", [128, 512]) for i in range(2)]
    php = mmps[0]
    p.bar_ps = p.ps("barps", [128, 512])
    p.bar_mm_src = ident
    alloc_norm_scratch(p, NT)
    alloc_router_scratch(p, NT)

    @p.on("sync")
    def _(s):
        p.dma("sync", s, ltmask[:], ltmask_i)
        p.dma("sync", s, ident[:], ident_i)
        p.dma("sync", s, lamrow[:], lam_log)
        p.dma("sync", s, tokb[:], tokbase)
        p.dma("sync", s, asel[:], asel_i)
        p.dma("sync", s, bsel[:], bsel_i)
        p.dma("sync", s, uendt[:], uends)
        p.dma("sync", s, cect[:], cec)
        p.dma("sync", s, cmaskt[:], cmask)
        p.dma("sync", s, qkt[:], qvec)
        p.dma("sync", s, qk2[:], kvec)
        p.dma("sync", s, nwrow_s[:], nw_row)
        p.dma("sync", s, posb[:], pos0.rearrange("(i pp) j -> pp i j", pp=128))
        p.dma("sync", s, g01b[:], g01_0.rearrange("(i pp) j -> pp i j", pp=128))
        for k in range(ND):
            p.dma("sync", s, wgs[k][:], wg[128 * k:128 * (k + 1), :])

    @p.on("gpsimd")
    def _(g):
        g.iota(iotap[:], pattern=[[0, 1]], base=0, channel_multiplier=1,
               allow_small_or_imprecise_dtypes=True)
        g.iota(econ[:], pattern=[[1, 8]], base=0, channel_multiplier=0,
               allow_small_or_imprecise_dtypes=True)
        g.memset(oobfill[:], float(OOBID))
        g.memset(z81[:], 0.0)
    p.barrier()

    @p.on("gpsimd")
    def _(g):
        p.dma("gpsimd", g,
              ids1_out.rearrange("e (r q) -> (e r) q", q=64), oobfill[:])

    @p.on("vector")
    def _(v):
        v.memset(onescol[:], 1.0)
        v.tensor_copy(out=identb[:], in_=ident[:])
        v.tensor_copy(out=posu[:], in_=posb[:])
    p.barrier()

    @p.on("tensor")
    def _(t):
        for hf in range(2):
            t.matmul(mmps[hf][:], onescol[:],
                     nwrow_s[:, 512 * hf:512 * (hf + 1)], start=True, stop=True)
    p.barrier()

    @p.on("vector")
    def _(v):
        for hf in range(2):
            v.tensor_copy(out=nwb[:, 512 * hf:512 * (hf + 1)], in_=mmps[hf][:])
    p.barrier()

    @p.on("scalar")
    def _(s):
        s.activation(out=lamrow[:], in_=lamrow[:], func=AF.Sigmoid)
    p.barrier()
    emit_perhead_to_chan(p, lamrow, lamc, asel, bsel, tp2[0], php, scr16)

    @p.on("vector")
    def _(v):
        v.tensor_tensor(out=qk2[:], in0=qkt[:], in1=qk2[:], op=ALU.mult)
        v.reduce_sum(out=ccol[:, 0:1], in_=qk2[:], axis=AX.X)
    p.barrier()

    @p.on("tensor")
    def _(t):
        t.transpose(tp2[0][0:1, 0:16], ccol[:, 0:1], ident[0:16, 0:16])
    p.barrier()

    @p.on("vector")
    def _(v):
        v.tensor_copy(out=ccol[0:1, 0:16], in_=tp2[0][0:1, 0:16])
    p.barrier()
    emit_perhead_to_chan(p, ccol[0:1, 0:16], cpp, asel, bsel, tp2[0], php,
                         scr16)

    @p.on("scalar")
    def _(s):
        s.activation(out=loglam[:], in_=lamc[:], func=AF.Ln)
    p.barrier()

    @p.on("vector")
    def _(v):
        v.memset(carry[:], 0.0)
    p.barrier()
    for j in range(NE):
        @p.on("scalar")
        def _(s, j=j):
            s.activation(out=tmp8[:], in_=loglam[:], func=AF.Exp,
                         scale=cect[:, j:j + 1])
        p.barrier()

        @p.on("vector")
        def _(v, j=j):
            v.tensor_scalar(out=tmp8[:], in0=tmp8[:],
                            scalar1=cmaskt[:, j:j + 1], scalar2=None,
                            op0=ALU.mult)
            v.tensor_tensor(out=tmp8[:], in0=tmp8[:], in1=uendt[:, j, :],
                            op=ALU.mult)
            v.tensor_tensor(out=carry[:], in0=carry[:], in1=tmp8[:],
                            op=ALU.add)
        p.barrier()

    def stream_stats(src_dram, with_mean, eps, extra=None):
        d0 = p.dma_cnt["sync"]
        s_used = p.sem(f"su{p.nbar}")

        @p.on("sync")
        def _(s):
            for i in range(NT):
                if i >= 2:
                    s.wait_ge(s_used, i - 1)
                p.dma("sync", s, xs[i % 2][:],
                      src_dram[128 * i:128 * (i + 1), :])

        @p.on("scalar")
        def _(s):
            for i in range(NT):
                s.wait_ge(p.ldsem["sync"], 16 * (d0 + i + 1))
                if extra is not None:
                    extra(i, "scalar", s)
                s.activation(out=p.n_sq[:], in_=(xs[i % 2][:] if extra is None
                                                else xs[2][:]),
                             func=AF.Square,
                             accum_out=p.n_q[:, i:i + 1]).then_inc(s_used, 1)

        if with_mean:
            @p.on("vector")
            def _(v):
                for i in range(NT):
                    v.wait_ge(s_used, i + 1)
                    v.reduce_sum(out=p.n_s[:, i:i + 1],
                                 in_=(xs[i % 2][:] if extra is None
                                      else xs[2][:]), axis=AX.X)
        p.barrier()
        emit_stats_rows(p, NT, eps, with_mean)

    def emit_stats_rows(p, nt, eps, with_mean):
        @p.on("vector")
        def _(v):
            inv = 1.0 / D
            if with_mean:
                v.tensor_scalar(out=p.n_mu[:, 0:nt], in0=p.n_s[:, 0:nt],
                                scalar1=inv, scalar2=None, op0=ALU.mult)
                v.tensor_scalar(out=p.n_var[:, 0:nt], in0=p.n_q[:, 0:nt],
                                scalar1=inv, scalar2=None, op0=ALU.mult)
                v.tensor_tensor(out=p.n_s[:, 0:nt], in0=p.n_mu[:, 0:nt],
                                in1=p.n_mu[:, 0:nt], op=ALU.mult)
                v.tensor_tensor(out=p.n_var[:, 0:nt], in0=p.n_var[:, 0:nt],
                                in1=p.n_s[:, 0:nt], op=ALU.subtract)
            else:
                v.tensor_scalar(out=p.n_var[:, 0:nt], in0=p.n_q[:, 0:nt],
                                scalar1=inv, scalar2=None, op0=ALU.mult)
            v.tensor_scalar(out=p.n_var[:, 0:nt], in0=p.n_var[:, 0:nt],
                            scalar1=eps, scalar2=None, op0=ALU.add)
        p.barrier()

        @p.on("scalar")
        def _(s):
            s.activation(out=p.n_var[:, 0:nt], in_=p.n_var[:, 0:nt],
                         func=AF.Sqrt)
        p.barrier()

        @p.on("vector")
        def _(v):
            v.reciprocal(out=p.n_var[:, 0:nt], in_=p.n_var[:, 0:nt])
            for i in range(nt):
                v.tensor_copy(out=p.n_ab[:, 2 * i:2 * i + 1],
                              in_=p.n_var[:, i:i + 1])
                if with_mean:
                    v.tensor_tensor(out=p.n_ab[:, 2 * i + 1:2 * i + 2],
                                    in0=p.n_mu[:, i:i + 1],
                                    in1=p.n_var[:, i:i + 1], op=ALU.mult)
                    v.tensor_scalar(out=p.n_ab[:, 2 * i + 1:2 * i + 2],
                                    in0=p.n_ab[:, 2 * i + 1:2 * i + 2],
                                    scalar1=-1.0, scalar2=None, op0=ALU.mult)
                else:
                    v.memset(p.n_ab[:, 2 * i + 1:2 * i + 2], 0.0)
        p.barrier()

    stream_stats(x, True, EPS_LN)

    d0 = p.dma_cnt["sync"]
    s_ap = p.sem("s_ap0")

    @p.on("sync")
    def _(s):
        for i in range(NT):
            if i >= 2:
                s.wait_ge(s_ap, i - 1)
            p.dma("sync", s, xs[i % 2][:], x[128 * i:128 * (i + 1), :])

    @p.on("scalar")
    def _(s):
        for i in range(NT):
            s.wait_ge(p.ldsem["sync"], 16 * (d0 + i + 1))
            s.activation(out=xn0b[i][:], in_=xs[i % 2][:], func=AF.Identity,
                         scale=p.n_ab[:, 2 * i:2 * i + 1],
                         bias=p.n_ab[:, 2 * i + 1:2 * i + 2]).then_inc(s_ap, 1)
    p.barrier()

    for k in range(ND):
        pairs = [(xn0b[i][:, 128 * k:128 * (k + 1)],
                  xnfb[:, 128 * i:128 * (i + 1)]) for i in range(NT)]
        emit_transpose_seq(p, pairs, identb, tpb)

        @p.on("vector")
        def _(v, k=k):
            v.tensor_tensor_scan(out=p8[0][:],
                                 data0=lamc[:, k:k + 1].broadcast_to([128, T]),
                                 data1=xnfb[:], initial=carry[:, k:k + 1],
                                 op0=ALU.mult, op1=ALU.add)
            v.tensor_scalar(out=p8[0][:], in0=p8[0][:],
                            scalar1=cpp[:, k:k + 1], scalar2=None,
                            op0=ALU.mult)
        p.barrier()
        pairs = [(p8[0][:, 128 * i:128 * (i + 1)],
                  otb[i][:, 128 * k:128 * (k + 1)]) for i in range(NT)]
        emit_transpose_seq(p, pairs, ident, tp2)

    @p.on("scalar")
    def _(s):
        for i in range(NT):
            s.activation(out=p.n_sq[:], in_=otb[i][:], func=AF.Square,
                         accum_out=p.n_q[:, i:i + 1])
    p.barrier()
    emit_stats_rows(p, NT, EPS_RMS, False)

    @p.on("vector")
    def _(v):
        for i in range(NT):
            v.scalar_tensor_tensor(out=otb[i][:], in0=otb[i][:],
                                   scalar=p.n_ab[:, 2 * i:2 * i + 1],
                                   in1=nwb[:], op0=ALU.mult, op1=ALU.mult)
    p.barrier()

    pairs = []
    for i in range(NT):
        for k in range(ND):
            pairs.append((otb[i][:, 128 * k:128 * (k + 1)],
                          fmb[k][:, 128 * i:128 * (i + 1)]))
    emit_transpose_seq(p, pairs, identb, tpb)

    s_ow = p.sem("s_ow")
    d_ow = p.dma_cnt["gpsimd"]
    v_ow = p._vcnt
    owts8 = p.sb("owts8", [128, ND, D], BF16)

    @p.on("gpsimd")
    def _(g):
        for k in range(ND):
            p.dma("gpsimd", g, owts8[:, k, :].rearrange(
                "pp (db dd) -> pp db dd", dd=128),
                o_wT[128 * k:128 * (k + 1), :]
                .rearrange("pp (db dd) -> pp db dd", dd=128))
    p.barrier()

    @p.on("tensor")
    def _(t):
        for i in range(NT):
            for dh2 in range(2):
                s = 2 * i + dh2
                if s >= 2:
                    t.wait_ge(p.vsem, v_ow + s - 1)
                for k in range(ND):
                    mm = t.matmul(mmps[s % 2][:],
                                  fmb[k][:, 128 * i:128 * (i + 1)],
                                  owts8[:, k, 512 * dh2:512 * (dh2 + 1)],
                                  start=(k == 0), stop=(k == ND - 1))
                mm.then_inc(s_ow, 1)

    @p.on("vector")
    def _(v):
        for i in range(NT):
            for dh2 in range(2):
                v.wait_ge(s_ow, 2 * i + dh2 + 1)
                v.tensor_copy(out=s0p[i][:, 512 * dh2:512 * (dh2 + 1)],
                              in_=mmps[(2 * i + dh2) % 2][:])
    p.barrier()
    return _p3_part2(p, locals())


def _p3_part2(p, L):
    (x, xs, xn0b, s0p, otb, fmb, yrow, posu, g01b) = (
        L["x"], L["xs"], L["xn0b"], L["s0p"], L["otb"], L["fmb"],
        L["yrow"], L["posu"], L["g01b"])
    y0own = L["y0own"]
    (wgs, lgt, sct, m0t, m1t, g01t, rk_tm, rkj_tm) = (
        L["wgs"], L["lgt"], L["sct"], L["m0t"], L["m1t"], L["g01t"],
        L["rk_tm"], L["rkj_tm"])
    (ltmask, ident, identb, z81, tokb, iotap, econ, scr) = (
        L["ltmask"], L["ident"], L["identb"], L["z81"], L["tokb"],
        L["iotap"], L["econ"], L["scr"])
    (dstu, vals, posf1, locjf, idxf, p8) = (
        L["dstu"], L["vals"], L["posf1"], L["locjf"], L["idxf"], L["p8"])
    tp2, tpb, lgps = L["tp2"], L["tpb"], L["lgps"]
    (s0g0_out, xn1_out, ids1_out, pos1_out, idx1_out, locj_out, g011_out,
     cnts_out) = (L["s0g0_out"], L["xn1_out"], L["ids1_out"], L["pos1_out"],
                  L["idx1_out"], L["locj_out"], L["g011_out"], L["cnts_out"])
    emit_stats_rows = L["emit_stats_rows"]

    d_y = p.dma_cnt["gpsimd"]
    d_x = p.dma_cnt["sync"]
    v_y = p._vcnt
    s_f = p.sem("s_f")
    s_x1 = p.sem("s_x1")

    syl = [p.sem(f"syl{l}") for l in range(2)]

    @p.on("gpsimd")
    def _(gg):
        for i in range(NT):
            for j in range(2):
                s = 2 * i + j
                if s >= 2:
                    gg.wait_ge(p.vsem, v_y + 8 * (i - 1) + 2 + 2 * j)
                    gg.wait_ge(syl[j], 16 * i)
                p.idma_lane(gg, syl[j], out=yrow[s % 2][:], out_offset=None,
                            in_=y0own,
                            in_offset=IndirectOffsetOnAxis(
                                ap=posu[:, i, j:j + 1], axis=0))

    @p.on("sync")
    def _(s):
        for i in range(NT):
            if i >= 2:
                s.wait_ge(p.vsem, v_y + 8 * (i - 1))
                s.wait_ge(s_x1, i - 1)
            p.dma("sync", s, xs[i % 2][:], x[128 * i:128 * (i + 1), :])

    sqd = p.sb("sqdmy", [128, D], BF16)

    @p.on("vector")
    def _(v):
        for i in range(NT):
            v.wait_ge(syl[0], 16 * (i + 1))
            v.tensor_copy(out=xs[2][:], in_=xn0b[i][:])
            v.tensor_scalar(out=p.n_sq[:], in0=yrow[0][:],
                            scalar1=g01b[:, i, 0:1], scalar2=None,
                            op0=ALU.mult)
            v.tensor_tensor(out=xs[2][:], in0=xs[2][:], in1=p.n_sq[:],
                            op=ALU.add)
            v.wait_ge(syl[1], 16 * (i + 1))
            v.tensor_scalar(out=p.n_sq[:], in0=yrow[1][:],
                            scalar1=g01b[:, i, 1:2], scalar2=None,
                            op0=ALU.mult)
            v.tensor_tensor(out=xs[2][:], in0=xs[2][:], in1=p.n_sq[:],
                            op=ALU.add)
            v.wait_ge(s_f, i + 1)
            v.tensor_tensor(out=s0p[i][:], in0=s0p[i][:], in1=xs[2][:],
                            op=ALU.mult)
            v.wait_ge(p.ldsem["sync"], 16 * (d_x + i + 1))
            v.tensor_tensor(out=xs[i % 2][:], in0=xs[i % 2][:],
                            in1=s0p[i][:], op=ALU.add)
            v.reduce_sum(out=p.n_s[:, i:i + 1], in_=xs[i % 2][:], axis=AX.X)

    @p.on("scalar")
    def _(s):
        for i in range(NT):
            s.wait_ge(p.vsem, v_y + 8 * i + 5)
            s.activation(out=xs[2][:], in_=xs[2][:],
                         func=AF.Silu).then_inc(s_f, 1)
            s.wait_ge(p.vsem, v_y + 8 * i + 7)
            s.activation(out=sqd[:], in_=xs[i % 2][:], func=AF.Square,
                         accum_out=p.n_q[:, i:i + 1]).then_inc(s_x1, 1)
    p.barrier()

    @p.on("gpsimd")
    def _(gg):
        for i in range(NT):
            p.dma("gpsimd", gg, s0g0_out[128 * i:128 * (i + 1), :], s0p[i][:])

    emit_stats_rows(p, NT, EPS_LN, True)

    d0 = p.dma_cnt["sync"]
    s_ap = p.sem("s_ap1")
    v_a = p._vcnt

    @p.on("sync")
    def _(s):
        for i in range(NT):
            if i >= 2:
                s.wait_ge(s_ap, i - 1)
            p.dma("sync", s, xs[i % 2][:], x[128 * i:128 * (i + 1), :])

    @p.on("vector")
    def _(v):
        for i in range(NT):
            v.wait_ge(p.ldsem["sync"], 16 * (d0 + i + 1))
            v.tensor_tensor(out=xs[i % 2][:], in0=xs[i % 2][:],
                            in1=s0p[i][:], op=ALU.add)

    @p.on("scalar")
    def _(s):
        for i in range(NT):
            s.wait_ge(p.vsem, v_a + i + 1)
            s.activation(out=otb[i][:], in_=xs[i % 2][:], func=AF.Identity,
                         scale=p.n_ab[:, 2 * i:2 * i + 1],
                         bias=p.n_ab[:, 2 * i + 1:2 * i + 2]).then_inc(s_ap, 1)
    p.barrier()

    @p.on("gpsimd")
    def _(gg):
        for i in range(NT):
            p.dma("gpsimd", gg, xn1_out[128 * i:128 * (i + 1), :], otb[i][:])

    pairs = []
    for i in range(NT):
        for k in range(ND):
            pairs.append((otb[i][:, 128 * k:128 * (k + 1)],
                          fmb[k][:, 128 * i:128 * (i + 1)]))
    emit_transpose_seq(p, pairs, identb, tpb)

    wgsb = L["wgsb"]

    @p.on("vector")
    def _(v):
        for k in range(ND):
            v.tensor_copy(out=wgsb[k][:], in_=wgs[k][:])
    p.barrier()
    s_lg = p.sem("s_lg")
    v_lg = p._vcnt

    @p.on("tensor")
    def _(t):
        for i in range(NT):
            if i >= 1:
                t.wait_ge(p.vsem, v_lg + i)
            for k in range(ND):
                mm = t.matmul(lgps[:, 0:8], fmb[k][:, 128 * i:128 * (i + 1)],
                              wgsb[k][:], start=(k == 0), stop=(k == ND - 1))
            mm.then_inc(s_lg, 1)

    @p.on("vector")
    def _(v):
        for i in range(NT):
            v.wait_ge(s_lg, i + 1)
            v.tensor_copy(out=lgt[i][:], in_=lgps[:, 0:8])
    p.barrier()

    emit_router(p, lgt, sct, m0t, m1t, g01t, ltmask, NT, clip_eps=EPS_G)

    @p.on("vector")
    def _(v):
        for i in range(NT):
            v.tensor_tensor(out=rk_tm[i][:], in0=m0t[i][:], in1=m1t[i][:],
                            op=ALU.add)
            v.tensor_copy(out=rkj_tm[i][:, 0:8], in_=m0t[i][:])
            v.tensor_copy(out=rkj_tm[i][:, 8:16], in_=m1t[i][:])
    p.barrier()
    pairs = [(rk_tm[i][:, 0:8], p8[0][0:8, 128 * i:128 * (i + 1)])
             for i in range(NT)]
    emit_transpose_seq(p, pairs, ident, tp2)
    pairs = [(rkj_tm[i][:, 0:16], p8[1][0:16, 128 * i:128 * (i + 1)])
             for i in range(NT)]
    emit_transpose_seq(p, pairs, ident, tp2)

    rks8 = p8[0][32:40, 0:T]
    rks16 = p8[1][32:48, 0:T]

    @p.on("vector")
    def _(v):
        v.tensor_tensor_scan(out=rks8,
                             data0=z81[0:8, 0:1].broadcast_to([8, T]),
                             data1=p8[0][0:8, 0:T], initial=0.0,
                             op0=ALU.add, op1=ALU.add)
        v.tensor_tensor_scan(out=rks16,
                             data0=z81[0:16, 0:1].broadcast_to([16, T]),
                             data1=p8[1][0:16, 0:T], initial=0.0,
                             op0=ALU.add, op1=ALU.add)
    p.barrier()
    pairs = [(p8[0][32:40, 128 * i:128 * (i + 1)], rk_tm[i][:, 0:8])
             for i in range(NT)]
    emit_transpose_seq(p, pairs, ident, tp2)
    pairs = [(p8[1][32:48, 128 * i:128 * (i + 1)], rkj_tm[i][:, 0:16])
             for i in range(NT)]
    emit_transpose_seq(p, pairs, ident, tp2)

    @p.on("gpsimd")
    def _(gg):
        p.dma("gpsimd", gg, cnts_out.rearrange("j e -> (j e)")
              .rearrange("(q a) -> q a", a=1), p8[1][32:48, T - 1:T])

    @p.on("vector")
    def _(v):
        for i in range(NT):
            for j, mt in ((0, m0t), (1, m1t)):
                v.tensor_tensor(out=scr[:, 0:8], in0=mt[i][:], in1=rk_tm[i][:],
                                op=ALU.mult)
                v.reduce_sum(out=scr[:, 8:9], in_=scr[:, 0:8], axis=AX.X)
                v.tensor_tensor(out=scr[:, 0:8], in0=mt[i][:], in1=econ[:],
                                op=ALU.mult)
                v.reduce_sum(out=idxf[i][:, j:j + 1], in_=scr[:, 0:8],
                             axis=AX.X)
                v.tensor_scalar(out=scr[:, 10:11], in0=idxf[i][:, j:j + 1],
                                scalar1=float(CAPL), scalar2=None,
                                op0=ALU.mult)
                v.tensor_tensor(out=scr[:, 10:11], in0=scr[:, 10:11],
                                in1=scr[:, 8:9], op=ALU.add)
                v.tensor_scalar(out=posf1[i][:, j:j + 1], in0=scr[:, 10:11],
                                scalar1=-1.0, scalar2=None, op0=ALU.add)
                v.tensor_tensor(out=scr[:, 0:8], in0=mt[i][:],
                                in1=rkj_tm[i][:, 8 * j:8 * (j + 1)],
                                op=ALU.mult)
                v.reduce_sum(out=locjf[i][:, j:j + 1], in_=scr[:, 0:8],
                             axis=AX.X)
    p.barrier()

    v0s = p._vcnt
    dbase2 = p.dma_cnt["gpsimd"]
    ids_flat = ids1_out.rearrange("e (c a) -> (e c) a", a=1)

    @p.on("vector")
    def _(v):
        for i in range(NT):
            for j in range(2):
                s = 2 * i + j
                if s >= 1:
                    v.wait_ge(p.ldsem["gpsimd"], 16 * (dbase2 + s))
                v.tensor_tensor(out=vals[:], in0=iotap[:], in1=tokb[:],
                                op=ALU.add)
                v.tensor_scalar(out=vals[:], in0=vals[:],
                                scalar1=float(128 * i), scalar2=None,
                                op0=ALU.add)
                v.tensor_copy(out=dstu[:], in_=posf1[i][:, j:j + 1])

    @p.on("gpsimd")
    def _(gg):
        for i in range(NT):
            for j in range(2):
                s = 2 * i + j
                gg.wait_ge(p.vsem, v0s + 3 * (s + 1))
                p.idma(gg, out=ids_flat, out_offset=IndirectOffsetOnAxis(
                    ap=dstu[:], axis=0), in_=vals[:], in_offset=None,
                    bounds_check=NE * CAPL - 1, oob_is_err=False)
    p.barrier()

    @p.on("sync")
    def _(s):
        for i in range(NT):
            p.dma("sync", s, pos1_out[128 * i:128 * (i + 1), :], posf1[i][:])
            p.dma("sync", s, idx1_out[128 * i:128 * (i + 1), :], idxf[i][:])
            p.dma("sync", s, locj_out[128 * i:128 * (i + 1), :], locjf[i][:])
            p.dma("sync", s, g011_out[128 * i:128 * (i + 1), :], g01t[i][:])
    p.barrier()
    return p.finish()



def build_p45():
    p = Prog("p45")
    TB = N
    NQC = TB // 512
    NKT = TB // 128
    NH = 4

    xn1_in = [p.inp(f"xn1_{c}", [T, D], BF16) for c in range(NCORE)]
    myids1 = p.inp("myids1", [128, RROWS // 128])
    fc1T = p.inp("fc1T", [D, H1])
    fc2n = p.inp("fc2n", [H1, D])
    fc1b_pre = p.inp("fc1b_pre", [128, NH1])
    fc2b_pre = p.inp("fc2b_pre", [128, ND])
    wq_sl = p.inp("wq_sl", [D, 256])
    wk_sl = p.inp("wk_sl", [D, 256])
    wv_sl = p.inp("wv_sl", [D, 256])
    wgd_sl = p.inp("wgd_sl", [D, 256])
    bq_pre = p.inp("bq_pre", [128, 2])
    bk_pre = p.inp("bk_pre", [128, 2])
    bv_row = p.inp("bv_row", [1, 256])
    bg_pre = p.inp("bg_pre", [128, 2])
    w1_sl = p.inp("w1_sl", [D, 1024])
    b1_pre = p.inp("b1_pre", [128, 8])
    w2_sl = p.inp("w2_sl", [1024, 16])
    b2_row = p.inp("b2_row", [1, 16])
    r5 = p.inp("r5", [5, NH, 128, 128])
    tconsts = p.inp("tconsts", [1, 2 * NH])
    boff_i = p.inp("boff", [128, 1])
    ident_i = p.inp("ident", [128, 128])

    y1_out = p.outp("y1", [RROWS, D], BF16)
    numg_out = p.outp("numg", [256, TB], BF16)
    gate_out = p.outp("gate", [256, TB], BF16)
    den_out = p.outp("den", [NH, TB])
    edgep_out = p.outp("edgep", [16, TB])

    p.init_sems()
    xn1full = p.nc.dram_tensor("xn1full", [NCORE * T, D], BF16,
                               kind="Internal").ap()
    fc1bf = p.nc.dram_tensor("fc1bf", [D, H1], BF16, kind="Internal").ap()
    fc2bf = p.nc.dram_tensor("fc2bf", [H1, D], BF16, kind="Internal").ap()

    G = [p.sb(f"G{i}", [128, 8192], BF16) for i in range(6)]
    Vt = p.sb("Vt", [128, NKT * 4 * 65], BF16)
    ident = p.sb("id128", [128, 128], F32)
    identb = p.sb("id128b", [128, 128], BF16)
    idsb = p.sb("idsb", [128, RROWS // 128], F32)
    idsu = p.sb("idsu", [128, RROWS // 128], U32)
    hrow = [p.sb(f"hrow{i}", [128, D], BF16) for i in range(2)]
    w1s = [p.sb(f"w1s{i}", [128, ND, 128], BF16) for i in range(2)]
    w2s1 = p.sb("w2s1", [128, NH1, 128], BF16)
    w2s = [w2s1, w2s1]
    fc1bt = p.sb("fc1bt", [128, NH1], F32)
    fc2bt = p.sb("fc2bt", [128, ND], F32)
    vmask = p.sb("vmask", [128, RROWS // 128], F32)
    boffc = p.sb("boffc", [128, 1], F32)
    tns = p.sb("tns", [128, RROWS // 128], F32)
    iotap = p.sb("iotap", [128, 1], F32)
    xtm = [p.sb(f"xtm{i}", [128, D], BF16) for i in range(4)]
    xfm = [p.sb(f"xfm{k}", [128, 512], BF16) for k in range(ND)]
    y_fm = xfm
    y_tm = xtm[0:2]
    asil = p.sb("asil", [128, 512], BF16)
    m1fm = p.sb("m1fm", [128, ND, 512], BF16)
    wqs = p.sb("wqs", [128, ND, 256], BF16)
    wks = p.sb("wks", [128, ND, 256], BF16)
    wvs = p.sb("wvs", [128, ND, 256], BF16)
    wgs = p.sb("wgs", [128, ND, 256], BF16)
    w1es = [p.sb(f"w1es{i}", [128, ND, 128], BF16) for i in range(2)]
    w2ss = p.sb("w2ss", [128, ND, 16], BF16)
    bqt = p.sb("bqt", [128, 2], F32)
    bkt = p.sb("bkt", [128, 2], F32)
    bgt = p.sb("bgt", [128, 2], F32)
    b1t = p.sb("b1t", [128, 8], F32)
    bvrow = p.sb("bvrow", [1, 256], F32)
    b2row = p.sb("b2row", [1, 16], F32)
    bvb = p.sb("bvb", [128, 256], F32)
    onescol = p.sb("onescol", [1, 128], F32)
    eb = p.sb("eb", [128, 5 * NH * 128], BF16)
    r5row = p.sb("r5row", [1, 1], F32)
    chs = p.sb("chs", [128, 12], F32)
    chrow = p.sb("chrow", [1, 16], F32)
    sq64 = p.sb("sq64", [128, 512], F32)
    den_sb = p.sb("den_sb", [128, 512], F32)
    ng_sb = p.sb("ng_sb", [128, 512], BF16)
    tpb = [p.ps(f"tpb{i}", [128, 1024], BF16) for i in range(2)]
    aps = p.ps("aps", [128, 512])
    gps = p.ps("gps", [128, 512])
    eps2 = [p.ps(f"eps{i}", [128, 512]) for i in range(2)]
    p.bar_ps = p.ps("barps", [128, 512])
    p.bar_mm_src = ident

    RH = RROWS // 2
    NRT = RH // 128
    NTQ = RH // 512

    @p.on("sync")
    def _(s):
        p.dma("sync", s, ident[:], ident_i)
        p.dma("sync", s, idsb[:], myids1)
        p.dma("sync", s, fc1bt[:], fc1b_pre)
        p.dma("sync", s, fc2bt[:], fc2b_pre)
        p.dma("sync", s, bqt[:], bq_pre)
        p.dma("sync", s, bkt[:], bk_pre)
        p.dma("sync", s, bgt[:], bg_pre)
        p.dma("sync", s, b1t[:], b1_pre)
        p.dma("sync", s, bvrow[:], bv_row)
        p.dma("sync", s, b2row[:], b2_row)
        p.dma("sync", s, boffc[:], boff_i)
        for c in range(NCORE):
            p.dma("sync", s, xn1full[T * c:T * (c + 1), :], xn1_in[c])

    @p.on("gpsimd")
    def _(g):
        g.iota(iotap[:], pattern=[[0, 1]], base=0, channel_multiplier=1,
               allow_small_or_imprecise_dtypes=True)
        p.dma("gpsimd", g, wqs[:], wq_sl.rearrange("(k pp) m -> pp k m", pp=128))
        p.dma("gpsimd", g, wks[:], wk_sl.rearrange("(k pp) m -> pp k m", pp=128))
        p.dma("gpsimd", g, wvs[:], wv_sl.rearrange("(k pp) m -> pp k m", pp=128))
        p.dma("gpsimd", g, wgs[:], wgd_sl.rearrange("(k pp) m -> pp k m", pp=128))
        p.dma("gpsimd", g, w2ss[:], w2_sl.rearrange("(k pp) m -> pp k m", pp=128))
        p.dma("gpsimd", g, eb[:].rearrange("pp (a b c) -> pp a b c", a=5, b=NH),
              r5.rearrange("a b pp c -> pp a b c"))
        p.dma("gpsimd", g, fc1bf, fc1T)
        p.dma("gpsimd", g, fc2bf, fc2n)
    p.barrier()

    @p.on("vector")
    def _(v):
        v.tensor_scalar(out=vmask[:], in0=idsb[:], scalar1=float(OOBID) / 2,
                        scalar2=None, op0=ALU.is_lt)
        v.tensor_scalar(out=idsb[:], in0=idsb[:], scalar1=boffc[:, 0:1],
                        scalar2=None, op0=ALU.subtract)
        v.tensor_scalar(out=tns[:], in0=idsb[:], scalar1=0.0,
                        scalar2=None, op0=ALU.is_lt)
        v.scalar_tensor_tensor(out=idsb[:], in0=tns[:],
                               scalar=float(NCORE * T), op0=ALU.mult,
                               op1=ALU.add, in1=idsb[:])
        v.tensor_scalar(out=idsb[:], in0=idsb[:], scalar1=float(NCORE * T - 1),
                        scalar2=None, op0=ALU.min)
        v.tensor_copy(out=idsu[:], in_=idsb[:])
        v.tensor_copy(out=identb[:], in_=ident[:])
        v.memset(onescol[:], 1.0)

    @p.on("scalar")
    def _(s):
        s.activation(out=eb[:], in_=eb[:], func=AF.Exp)
    p.barrier()
    return _p45_moe(p, locals())


def _p45_moe(p, L):
    G, Vt = L["G"], L["Vt"]
    xn1full, idsu = L["xn1full"], L["idsu"]
    fc1T, fc2n = L["fc1T"], L["fc2n"]
    fc1bf, fc2bf = L["fc1bf"], L["fc2bf"]
    fc1bt, fc2bt = L["fc1bt"], L["fc2bt"]
    hrow, w1s, w2s, asil = L["hrow"], L["w1s"], L["w2s"], L["asil"]
    y_fm, y_tm = L["y_fm"], L["y_tm"]
    hrow4 = [hrow[0], hrow[1], L["xtm"][2], L["xtm"][3]]
    identb, ident = L["identb"], L["ident"]
    aps, gps, eps2, tpb = L["aps"], L["gps"], L["eps2"], L["tpb"]
    y1_out = L["y1_out"]
    vmask, iotap, idsb = L["vmask"], L["iotap"], L["idsb"]
    RH = RROWS // 2
    NRT = RH // 128
    NTQ = RH // 512
    h1_fm = [G[k // 2][:, 4096 * (k % 2):4096 * (k % 2 + 1)] for k in range(ND)]
    act = [G[4 + i // 16][:, 512 * (i % 16):512 * (i % 16) + 512]
           for i in range(NH1)]

    for half in range(2):
        s_t, b_t = p.use_sem("m_tr", NRT * ND)
        v0 = p._vcnt
        d0 = p.dma_cnt["gpsimd"]
        tick_ms = lambda r: v0 + 9 * r + 1
        tick_ev = lambda kk: v0 + 9 * (kk // 8) + (kk % 8) + 2

        @p.on("vector")
        def _(v):
            for r in range(NRT):
                if r >= 4:
                    v.wait_ge(s_t, b_t + 8 * (r - 3))
                v.memset(hrow4[r % 4][:], 0.0)
                for k in range(ND):
                    kk = 8 * r + k
                    v.wait_ge(s_t, b_t + kk + 1)
                    v.tensor_copy(out=h1_fm[k][:, 128 * r:128 * (r + 1)],
                                  in_=tpb[kk % 2][0:128, 0:128])

        sgl = [p.sem(f"mg{half}_{l}") for l in range(4)]

        @p.on("gpsimd")
        def _(g):
            for r in range(NRT):
                g.wait_ge(p.vsem, tick_ms(r))
                if r >= 4:
                    g.wait_ge(sgl[r % 4], 16 * (r // 4))
                p.idma_lane(g, sgl[r % 4], out=hrow4[r % 4][:],
                            out_offset=None, in_=xn1full,
                            in_offset=IndirectOffsetOnAxis(
                                ap=idsu[:, half * NRT + r:half * NRT + r + 1],
                                axis=0))

        @p.on("tensor")
        def _(t):
            for r in range(NRT):
                t.wait_ge(sgl[r % 4], 16 * (r // 4 + 1))
                for k in range(ND):
                    kk = 8 * r + k
                    if kk >= 2:
                        t.wait_ge(p.vsem, tick_ev(kk - 2))
                    t.transpose(tpb[kk % 2][0:128, 0:128],
                                hrow4[r % 4][:, 128 * k:128 * (k + 1)],
                                identb[:]).then_inc(s_t, 1)
        p.barrier()

        for tq in range(NTQ):
            s_mm, b_mm = p.use_sem("m_mm", NH1)
            s_si, b_si = p.use_sem("m_si", NH1)
            d1 = p.dma_cnt["sync"]

            @p.on("sync")
            def _(s):
                for i in range(NH1):
                    buf = i % 2
                    if i >= 2:
                        s.wait_ge(s_mm, b_mm + i - 1)
                    p.dma("sync", s, w1s[buf][:],
                          fc1bf.rearrange("(k pp) i -> pp k i", pp=128)
                          [:, :, 128 * i:128 * (i + 1)])

            @p.on("tensor")
            def _(t):
                for i in range(NH1):
                    buf = i % 2
                    t.wait_ge(p.ldsem["sync"], 16 * (d1 + i + 1))
                    if i >= 1:
                        t.wait_ge(s_si, b_si + i)
                    for k in range(ND):
                        mm = t.matmul(aps[:], w1s[buf][:, k, :],
                                      h1_fm[k][:, 512 * tq:512 * (tq + 1)],
                                      start=(k == 0), stop=(k == ND - 1))
                    mm.then_inc(s_mm, 1)

            @p.on("scalar")
            def _(s):
                for i in range(NH1):
                    s.wait_ge(s_mm, b_mm + i + 1)
                    s.activation(out=act[i][:], in_=aps[:], func=AF.Silu,
                                 bias=fc1bt[:, i:i + 1]).then_inc(s_si, 1)
            p.barrier()

            s_eo, b_eo = p.use_sem("m_eo", ND)
            d2 = p.dma_cnt["sync"]
            s_ev, b_ev = p.use_sem("m_ev", ND)

            @p.on("sync")
            def _(s):
                for m in range(ND):
                    buf = m % 2
                    if m >= 1:
                        s.wait_ge(s_eo, b_eo + m)
                    p.dma("sync", s, w2s[buf][:],
                          fc2bf.rearrange("(ii pp) dd -> pp ii dd", pp=128)
                          [:, :, 128 * m:128 * (m + 1)])

            @p.on("tensor")
            def _(t):
                for m in range(ND):
                    buf = m % 2
                    t.wait_ge(p.ldsem["sync"], 16 * (d2 + m + 1))
                    if m >= 2:
                        t.wait_ge(s_ev, b_ev + m - 1)
                    for i in range(NH1):
                        mm = t.matmul(eps2[m % 2][:], w2s[buf][:, i, :],
                                      act[i][:],
                                      start=(i == 0), stop=(i == NH1 - 1))
                    mm.then_inc(s_eo, 1)

            @p.on("scalar")
            def _(s):
                for m in range(ND):
                    s.wait_ge(s_eo, b_eo + m + 1)
                    s.activation(out=y_fm[m][:], in_=eps2[m % 2][:],
                                 func=AF.Identity,
                                 bias=fc2bt[:, m:m + 1]).then_inc(s_ev, 1)
            p.barrier()

            s_yt, b_yt = p.use_sem("m_yt", 32)
            v3 = p._vcnt
            d3 = p.dma_cnt["gpsimd"]

            @p.on("tensor")
            def _(t):
                for rr in range(4):
                    for m in range(ND):
                        kk = 8 * rr + m
                        if kk >= 2:
                            kp = kk - 2
                            t.wait_ge(p.vsem, v3 + kp + 1 + kp // 8)
                        t.transpose(tpb[kk % 2][0:128, 0:128],
                                    y_fm[m][:, 128 * rr:128 * (rr + 1)],
                                    identb[:]).then_inc(s_yt, 1)

            @p.on("vector")
            def _(v):
                for rr in range(4):
                    if rr >= 2:
                        v.wait_ge(p.ldsem["gpsimd"], 16 * (d3 + rr - 1))
                    for m in range(ND):
                        kk = 8 * rr + m
                        v.wait_ge(s_yt, b_yt + kk + 1)
                        v.tensor_copy(out=y_tm[rr % 2][:, 128 * m:128 * (m + 1)],
                                      in_=tpb[kk % 2][0:128, 0:128])
                    rt = half * NRT + 4 * tq + rr
                    v.tensor_scalar(out=y_tm[rr % 2][:], in0=y_tm[rr % 2][:],
                                    scalar1=vmask[:, rt:rt + 1], scalar2=None,
                                    op0=ALU.mult)

            @p.on("gpsimd")
            def _(g):
                for rr in range(4):
                    g.wait_ge(p.vsem, v3 + 9 * rr + 9)
                    row0 = half * RH + 512 * tq + 128 * rr
                    p.dma("gpsimd", g, y1_out[row0:row0 + 128, :],
                          y_tm[rr % 2][:])
            p.barrier()
    return _p45_attn(p, L)


def _p45_attn(p, L):
    G, Vt = L["G"], L["Vt"]
    xn1full = L["xn1full"]
    (wqs, wks, wvs, wgs, w1es, w2ss) = (L["wqs"], L["wks"], L["wvs"],
                                        L["wgs"], L["w1es"], L["w2ss"])
    (bqt, bkt, bgt, b1t, bvrow, b2row, bvb, onescol) = (
        L["bqt"], L["bkt"], L["bgt"], L["b1t"], L["bvrow"], L["b2row"],
        L["bvb"], L["onescol"])
    xtm, xfm, m1fm = L["xtm"], L["xfm"], L["m1fm"]
    eb, chs, chrow, sq64, den_sb, ng_sb = (L["eb"], L["chs"], L["chrow"],
                                           L["sq64"], L["den_sb"], L["ng_sb"])
    ident, identb = L["ident"], L["identb"]
    tpb, aps, gps, eps2 = L["tpb"], L["aps"], L["gps"], L["eps2"]
    (y1_out, numg_out, gate_out, den_out, edgep_out) = (
        L["y1_out"], L["numg_out"], L["gate_out"], L["den_out"],
        L["edgep_out"])
    r5, tconsts, w1_sl = L["r5"], L["tconsts"], L["w1_sl"]
    TB, NQC, NKT, NH = N, N // 512, N // 128, 4
    ep16 = p.sb("ep16", [16, 512], F32)
    gevac = L["asil"]

    @p.on("tensor")
    def _(t):
        t.matmul(aps[:, 0:256], onescol[:], bvrow[:], start=True, stop=True)
    p.barrier()

    @p.on("vector")
    def _(v):
        v.tensor_copy(out=bvb[:], in_=aps[:, 0:256])
    p.barrier()

    for tq in range(NQC):
        @p.on("sync")
        def _(s, tq=tq):
            for i4 in range(4):
                p.dma("sync", s, xtm[i4][:],
                      xn1full[512 * tq + 128 * i4:512 * tq + 128 * (i4 + 1), :])
        p.barrier()
        pairs = []
        for i4 in range(4):
            for k in range(ND):
                pairs.append((xtm[i4][:, 128 * k:128 * (k + 1)],
                              xfm[k][:, 128 * i4:128 * (i4 + 1)]))
        emit_transpose_seq(p, pairs, identb, tpb)

        s_mm, b_mm = p.use_sem("a_mm", 18)
        s_ev, b_ev = p.use_sem("a_ev", 14)
        v_vv = p._vcnt
        ngrp = 0
        specs = []
        for mc in range(2):
            specs.append(("q", mc))
            specs.append(("k", mc))
            specs.append(("g", mc))
        for ts4 in range(4):
            specs.append(("v", ts4))
        for mc in range(ND):
            specs.append(("m1", mc))

        d_w = p.dma_cnt["gpsimd"]

        @p.on("gpsimd")
        def _(g, tq=tq):
            for mi, (kind, mc) in enumerate([sp for sp in specs
                                             if sp[0] == "m1"]):
                buf = mc % 2
                if mc >= 2:
                    g.wait_ge(s_mm, b_mm + 10 + mc - 1)
                p.dma("gpsimd", g, w1es[buf][:],
                      w1_sl.rearrange("(k pp) m -> pp k m", pp=128)
                      [:, :, 128 * mc:128 * (mc + 1)])

        kinds_list = [k for k, m in specs]

        def evc(gi):
            ns = sum(1 for g2 in range(gi + 1) if kinds_list[g2] != "v")
            nv = sum(1 for g2 in range(gi + 1) if kinds_list[g2] == "v")
            return ns, nv

        @p.on("tensor")
        def _(t, tq=tq):
            gi = 0
            for kind, mc in specs:
                ps = aps if gi % 2 == 0 else gps
                if gi >= 2:
                    ns, nv = evc(gi - 2)
                    t.wait_ge(s_ev, b_ev + ns)
                    if nv > 0:
                        t.wait_ge(p.vsem, v_vv + 2 * nv)
                if kind == "q":
                    for k in range(ND):
                        mm = t.matmul(ps[:], wqs[:, k, 128 * mc:128 * (mc + 1)],
                                      xfm[k][:], start=(k == 0),
                                      stop=(k == ND - 1))
                elif kind == "k":
                    for k in range(ND):
                        mm = t.matmul(ps[:], wks[:, k, 128 * mc:128 * (mc + 1)],
                                      xfm[k][:], start=(k == 0),
                                      stop=(k == ND - 1))
                elif kind == "g":
                    for k in range(ND):
                        mm = t.matmul(ps[:], wgs[:, k, 128 * mc:128 * (mc + 1)],
                                      xfm[k][:], start=(k == 0),
                                      stop=(k == ND - 1))
                elif kind == "v":
                    for k in range(ND):
                        mm = t.matmul(ps[:, 0:256],
                                      xfm[k][:, 128 * mc:128 * (mc + 1)],
                                      wvs[:, k, :], start=(k == 0),
                                      stop=(k == ND - 1))
                else:
                    t.wait_ge(p.ldsem["gpsimd"], 16 * (d_w + mc + 1))
                    for k in range(ND):
                        mm = t.matmul(ps[:], w1es[mc % 2][:, k, :], xfm[k][:],
                                      start=(k == 0), stop=(k == ND - 1))
                mm.then_inc(s_mm, 1)
                gi += 1

        v_ev = p._vcnt

        @p.on("scalar")
        def _(s, tq=tq):
            gi = 0
            for kind, mc in specs:
                ps = aps if gi % 2 == 0 else gps
                s.wait_ge(s_mm, b_mm + gi + 1)
                if kind == "q":
                    s.activation(out=G[mc][:, 512 * tq:512 * (tq + 1)],
                                 in_=ps[:], func=AF.Identity,
                                 bias=bqt[:, mc:mc + 1]).then_inc(s_ev, 1)
                elif kind == "k":
                    s.activation(out=G[2 + mc][:, 512 * tq:512 * (tq + 1)],
                                 in_=ps[:], func=AF.Identity,
                                 bias=bkt[:, mc:mc + 1]).then_inc(s_ev, 1)
                elif kind == "g":
                    s.activation(out=(gevac if mc == 0 else ng_sb)[:],
                                 in_=ps[:], func=AF.Sigmoid,
                                 bias=bgt[:, mc:mc + 1]).then_inc(s_ev, 1)
                elif kind == "m1":
                    s.activation(out=m1fm[:, mc, :], in_=ps[:], func=AF.Silu,
                                 bias=b1t[:, mc:mc + 1]).then_inc(s_ev, 1)
                else:
                    s.nop()
                gi += 1

        @p.on("vector")
        def _(v, tq=tq):
            gi = 0
            for kind, mc in specs:
                ps = aps if gi % 2 == 0 else gps
                if kind == "v":
                    v.wait_ge(s_mm, b_mm + gi + 1)
                    kt = 4 * tq + mc
                    vv = Vt[:, 260 * kt:260 * (kt + 1)].rearrange(
                        "pp (h q) -> pp h q", q=65)
                    v.tensor_tensor(out=vv[:, :, 0:64],
                                    in0=ps[:, 0:256].rearrange(
                                        "pp (h q) -> pp h q", q=64),
                                    in1=bvb[:].rearrange(
                                        "pp (h q) -> pp h q", q=64),
                                    op=ALU.add)
                    v.memset(vv[:, :, 64:65], 1.0)
                gi += 1

        @p.on("gpsimd")
        def _(g, tq=tq):
            gidx = 0
            for gi, (kind, mc) in enumerate(specs):
                if kind == "g":
                    ns, _ = evc(gi)
                    g.wait_ge(s_ev, b_ev + ns)
                    p.dma("gpsimd", g,
                          gate_out[128 * mc:128 * (mc + 1),
                                   512 * tq:512 * (tq + 1)],
                          (gevac if mc == 0 else ng_sb)[:])
                    gidx += 1
        p.barrier()

        s_ed, b_ed = p.use_sem("a_ed", 1)

        @p.on("tensor")
        def _(t, tq=tq):
            for mc in range(ND):
                mm = t.matmul(eps2[0][0:16, :], w2ss[:, mc, :], m1fm[:, mc, :],
                              start=(mc == 0), stop=(mc == ND - 1))
            mm.then_inc(s_ed, 1)

        @p.on("vector")
        def _(v, tq=tq, b_ed=b_ed):
            v.wait_ge(s_ed, b_ed + 1)
            v.tensor_copy(out=ep16[:], in_=eps2[0][0:16, :])

        @p.on("gpsimd")
        def _(g, tq=tq):
            g.wait_ge(p.vsem, p._vcnt)
            p.dma("gpsimd", g, edgep_out[:, 512 * tq:512 * (tq + 1)], ep16[:])
        p.barrier()
    return _p45_scores(p, L)


def _p45_scores(p, L):
    G, Vt = L["G"], L["Vt"]
    eb, chs, chrow, sq64, den_sb, ng_sb = (L["eb"], L["chs"], L["chrow"],
                                           L["sq64"], L["den_sb"], L["ng_sb"])
    ident, identb, onescol = L["ident"], L["identb"], L["onescol"]
    tpb, aps, gps, eps2 = L["tpb"], L["aps"], L["gps"], L["eps2"]
    numg_out, den_out = L["numg_out"], L["den_out"]
    r5, tconsts = L["r5"], L["tconsts"]
    TB, NQC, NKT, NH = N, N // 512, N // 128, 4
    m1fm = L["m1fm"]
    ones2 = p.sb("onesb2", [128, 1], BF16)
    onesf = p.sb("onesf", [128, 1], F32)
    mx1 = p.sb("mx1", [1, 1], F32)
    pt2 = [m1fm[:, i, :] for i in range(2)]
    tct = p.sb("tct2", [1, 2 * NH], F32)

    @p.on("sync")
    def _(s):
        p.dma("sync", s, tct[:], tconsts)

    @p.on("vector")
    def _(v):
        v.memset(ones2[:], 1.0)
        v.memset(onesf[:], 1.0)
    p.barrier()

    @p.on("vector")
    def _(v):
        v.memset(chrow[:, 0:16], -1.0e30)
    p.barrier()
    if _SC_PHASE is not None and _SC_PHASE <= 0:
        return p.finish()

    sqb = p.sb("sqb", [128, 512], BF16)
    for h in range(NH):
        b0 = 64 * (h % 2)
        for (which, Gt, col) in (("q", G[h // 2], 0), ("k", G[2 + h // 2], 4)):
            s_sq, b_sq = p.use_sem("a_sq", 2 * NQC)
            v_sq = p._vcnt

            @p.on("scalar")
            def _(s, Gt=Gt, b0=b0, s_sq=s_sq, b_sq=b_sq):
                for tq in range(NQC):
                    if tq >= 1:
                        s.wait_ge(s_sq, b_sq + 2 * tq)
                    s.activation(out=sqb[b0:b0 + 64, 0:512],
                                 in_=Gt[b0:b0 + 64, 512 * tq:512 * (tq + 1)],
                                 func=AF.Square).then_inc(s_sq, 1)

            @p.on("tensor")
            def _(t, b0=b0, s_sq=s_sq, b_sq=b_sq, v_sq=v_sq):
                for tq in range(NQC):
                    t.wait_ge(s_sq, b_sq + 2 * tq + 1)
                    if tq >= 1:
                        t.wait_ge(p.vsem, v_sq + 2 * tq - 1)
                    t.matmul(gps[0:1, :], ones2[b0:b0 + 64, 0:1],
                             sqb[b0:b0 + 64, 0:512],
                             start=True, stop=True).then_inc(s_sq, 1)
            @p.on("vector")
            def _(v, h=h, col=col):
                for tq in range(NQC):
                    v.wait_ge(s_sq, b_sq + 2 * (tq + 1))
                    v.reduce_max(out=mx1[0:1, 0:1], in_=gps[0:1, :],
                                 axis=AX.X)
                    v.tensor_tensor(out=chrow[0:1, col + h:col + h + 1],
                                    in0=chrow[0:1, col + h:col + h + 1],
                                    in1=mx1[0:1, 0:1], op=ALU.max)
            p.barrier()

    if _SC_PHASE is not None and _SC_PHASE <= 1:
        return p.finish()
    mx128 = p.sb("mx128", [128, 1], F32)
    mx128b = p.sb("mx128b", [128, 1], F32)
    for h in range(NH):
        @p.on("vector")
        def _(v, h=h):
            v.memset(mx128[:], -1.0e30)
            for cl in range(5):
                v.reduce_max(out=mx128b[:],
                             in_=eb[:, 128 * (NH * cl + h):
                                    128 * (NH * cl + h + 1)], axis=AX.X)
                v.tensor_tensor(out=mx128[:], in0=mx128[:], in1=mx128b[:],
                                op=ALU.max)
        p.barrier()

        @p.on("tensor")
        def _(t, h=h):
            t.transpose(gps[0:1, 0:128], mx128[:], ident[:])
        p.barrier()

        @p.on("vector")
        def _(v, h=h):
            v.tensor_copy(out=sq64[0:1, 0:128], in_=gps[0:1, 0:128])
            v.reduce_max(out=chrow[0:1, 8 + h:8 + h + 1],
                         in_=sq64[0:1, 0:128], axis=AX.X)
        p.barrier()

    if _SC_PHASE is not None and _SC_PHASE <= 2:
        return p.finish()
    @p.on("scalar")
    def _(s):
        s.activation(out=chrow[0:1, 8:12], in_=chrow[0:1, 8:12], func=AF.Ln)
    p.barrier()

    @p.on("scalar")
    def _(s):
        s.activation(out=chrow[0:1, 0:8], in_=chrow[0:1, 0:8], func=AF.Sqrt)
    p.barrier()

    @p.on("vector")
    def _(v):
        v.tensor_tensor(out=chrow[0:1, 12:16], in0=chrow[0:1, 0:4],
                        in1=chrow[0:1, 4:8], op=ALU.mult)
        v.tensor_scalar(out=chrow[0:1, 12:16], in0=chrow[0:1, 12:16],
                        scalar1=0.125, scalar2=None, op0=ALU.mult)
        v.tensor_tensor(out=chrow[0:1, 12:16], in0=chrow[0:1, 12:16],
                        in1=chrow[0:1, 8:12], op=ALU.add)
    p.barrier()

    brow = p.sb("brow", [1, 12], F32)

    @p.on("vector")
    def _(v):
        v.tensor_scalar(out=brow[0:1, 0:4], in0=chrow[0:1, 12:16],
                        scalar1=-1.0, scalar2=None, op0=ALU.mult)
        v.tensor_tensor(out=brow[0:1, 4:8], in0=tct[0:1, NH:2 * NH],
                        in1=brow[0:1, 0:4], op=ALU.add)
        v.tensor_tensor(out=brow[0:1, 8:12], in0=tct[0:1, 0:NH],
                        in1=brow[0:1, 0:4], op=ALU.add)
    p.barrier()

    @p.on("tensor")
    def _(t):
        t.matmul(aps[:, 0:12], onescol[:], brow[:], start=True, stop=True)
    p.barrier()

    @p.on("vector")
    def _(v):
        v.tensor_copy(out=chs[:, 0:12], in_=aps[:, 0:12])
    p.barrier()

    if _SC_PHASE is not None and _SC_PHASE <= 3:
        return p.finish()
    _sc_carry = {}
    for h in range(NH):
        b0 = 64 * (h % 2)
        for qc in range(NQC):
            if _SC_LIMIT is not None and h * NQC + qc >= _SC_LIMIT:
                continue
            s_sc, b_sc = p.use_sem("s_sc", NKT)
            s_ex, b_ex = p.use_sem("s_ex", NKT)
            s_av, b_av = p.use_sem("s_av", NKT)
            v_ib = p._vcnt


            def klass(kt, qsub, qc=qc):
                d = 128 * kt - (512 * qc + 128 * qsub)
                if -256 <= d <= 256:
                    return ("ib", (d + 256) // 128)
                return ("lo", None) if d < 0 else ("hi", None)

            ibcnt = []
            tot = 0
            for kt in range(NKT):
                n = sum(1 for qs in range(4) if klass(kt, qs)[0] == "ib")
                ibcnt.append(tot)
                tot += n

            @p.on("scalar")
            def _(s, h=h, qc=qc):
                for kt in range(NKT):
                    ps = eps2[kt % 2]
                    s.wait_ge(s_sc, b_sc + kt + 1)
                    if b_av + kt - 1 > 0:
                        s.wait_ge(s_av, b_av + kt - 1)
                    kinds = [klass(kt, qs) for qs in range(4)]
                    if all(kk[0] != "ib" for kk in kinds):
                        if all(kk[0] == kinds[0][0] for kk in kinds):
                            col = 4 + h if kinds[0][0] == "lo" else 8 + h
                            s.activation(out=pt2[kt % 2], in_=ps[:],
                                         func=AF.Exp, scale=0.125,
                                         bias=chs[:, col:col + 1]
                                         ).then_inc(s_ex, 1)
                            continue
                    for qs in range(4):
                        kk = kinds[qs]
                        col = (h if kk[0] == "ib"
                               else (4 + h if kk[0] == "lo" else 8 + h))
                        act = s.activation(
                            out=pt2[kt % 2][:, 128 * qs:128 * (qs + 1)],
                            in_=ps[:, 128 * qs:128 * (qs + 1)],
                            func=AF.Exp, scale=0.125,
                            bias=chs[:, col:col + 1])
                    act.then_inc(s_ex, 1)

            @p.on("vector")
            def _(v, h=h, qc=qc):
                for kt in range(NKT):
                    for qs in range(4):
                        kk = klass(kt, qs)
                        if kk[0] == "ib":
                            v.wait_ge(s_ex, b_ex + kt + 1)
                            v.tensor_tensor(
                                out=pt2[kt % 2][:, 128 * qs:128 * (qs + 1)],
                                in0=pt2[kt % 2][:, 128 * qs:128 * (qs + 1)],
                                in1=eb[:, 128 * (5 * h + kk[1]):
                                       128 * (5 * h + kk[1] + 1)],
                                op=ALU.mult)

            pe_prev = _sc_carry.get("ev2")

            @p.on("tensor")
            def _(t, h=h, qc=qc, b0=b0, pe_prev=pe_prev):
                def av(kt):
                    nib_k = sum(1 for qs in range(4)
                                if klass(kt, qs)[0] == "ib")
                    if nib_k > 0:
                        t.wait_ge(p.vsem, v_ib + ibcnt[kt] + nib_k)
                    else:
                        t.wait_ge(s_ex, b_ex + kt + 1)
                    if kt == 0 and pe_prev is not None:
                        t.wait_ge(p.vsem, pe_prev)
                    mm = t.matmul(aps[0:65, :], Vt[:, 260 * kt + 65 * h:
                                                   260 * kt + 65 * (h + 1)],
                                  pt2[kt % 2], start=(kt == 0),
                                  stop=(kt == NKT - 1))
                    mm.then_inc(s_av, 1)

                for kt in range(NKT):
                    ps = eps2[kt % 2]
                    if b_ex + kt - 1 > 0:
                        t.wait_ge(s_ex, b_ex + kt - 1)
                    t.matmul(ps[:], G[2 + h // 2][b0:b0 + 64,
                                                  128 * kt:128 * (kt + 1)],
                             G[h // 2][b0:b0 + 64, 512 * qc:512 * (qc + 1)],
                             start=True, stop=True).then_inc(s_sc, 1)
                    if kt >= 1:
                        av(kt - 1)
                av(NKT - 1)

            d_prev = _sc_carry.get("d")
            t_ev2 = v_ib + tot + 2

            @p.on("vector")
            def _(v, s_av=s_av, b_av=b_av, d_prev=d_prev):
                v.wait_ge(s_av, b_av + NKT)
                if d_prev is not None:
                    v.wait_ge(p.ldsem["gpsimd"], 16 * (d_prev + 2))
                v.tensor_copy(out=ng_sb[0:64, :], in_=aps[0:64, :])
                v.tensor_copy(out=den_sb[64:65, :], in_=aps[64:65, :])

            _sc_carry["d"] = p.dma_cnt["gpsimd"]
            _sc_carry["ev2"] = t_ev2

            @p.on("gpsimd")
            def _(g, h=h, qc=qc, t_ev2=t_ev2):
                g.wait_ge(p.vsem, t_ev2)
                p.dma("gpsimd", g,
                      numg_out[64 * h:64 * (h + 1), 512 * qc:512 * (qc + 1)],
                      ng_sb[0:64, :])
                p.dma("gpsimd", g, den_out[h:h + 1, 512 * qc:512 * (qc + 1)],
                      den_sb[64:65, :])
    p.barrier()
    return p.finish()



def build_p6():
    p = Prog("p6")
    attng_i = p.inp("attng", [D, T], BF16)
    gateg_i = p.inp("gateg", [D, T], BF16)
    deng_i = p.inp("deng", [H, T])
    edgep_i = p.inp("edgepg", [4 * H, T])
    wo = p.inp("wo", [D, D])
    bo_pre = p.inp("bo_pre", [128, ND])
    b2_col = p.inp("b2_col", [H, 1])
    lng_row_i = p.inp("lng_row", [1, D])
    lnb_row_i = p.inp("lnb_row", [1, D])
    y1own = p.inp("y1own", [RROWS, D], BF16)
    x_in = p.inp("x", [T, D])
    s0g0_i = p.inp("s0g0", [T, D], BF16)
    xn1_i = p.inp("xn1", [T, D], BF16)
    pos1 = p.inp("pos1", [T, 2])
    idx1 = p.inp("idx1", [T, 2])
    locj = p.inp("locj", [T, 2])
    g011 = p.inp("g011", [T, 2])
    cnts_all = p.inp("cnts_all", [NE, 2 * NE])
    cmask6 = p.inp("cmask6", [NE, 1])
    iota8 = p.inp("iota8", [1, NE])
    fsel = p.inp("fsel", [H, D])
    ident_i = p.inp("ident", [128, 128])

    x2_out = p.outp("x2", [T, D])

    p.init_sems()

    atg = p.sb("atg", [128, ND, T], BF16)
    gtg = p.sb("gtg", [128, ND, T], BF16)
    wos = p.sb("wos", [128, ND, D], BF16)
    ident = p.sb("id128", [128, 128], F32)
    e16 = p.sb("e16", [H, T], F32)
    ebuf = [p.sb(f"ebuf{i}", [H, T], F32) for i in range(2)]
    densb = p.sb("densb", [H, T], F32)
    fxb = p.sb("fxb", [128, 512], BF16)
    lngr = p.sb("lngr", [1, D], F32)
    lnbr = p.sb("lnbr", [1, D], F32)
    lngb = p.sb("lngb", [128, D], F32)
    lnbb = p.sb("lnbb", [128, D], F32)
    iotab = p.sb("iotab", [128, NE], F32)
    basejb = p.sb("basejb", [128, 2 * NE], F32)
    baser = p.sb("baser", [1, 2 * NE], F32)
    cntsb = p.sb("cntsb", [NE, 2 * NE], F32)
    cmk = p.sb("cmk", [NE, 1], F32)
    ones8 = p.sb("ones8", [NE, 1], F32)
    onesr = p.sb("onesr", [1, 128], F32)
    io8 = p.sb("io8", [1, NE], F32)
    b2c = p.sb("b2c", [H, 1], F32)
    bot = p.sb("bot", [128, ND], F32)
    fselb = p.sb("fselb", [H, D], F32)
    posb = p.sb("posb", [128, NT, 2], F32)
    posu = p.sb("posu", [128, NT, 2], U32)
    idxb = p.sb("idxb", [128, NT, 2], F32)
    locb = p.sb("locb", [128, NT, 2], F32)
    gb = p.sb("gb", [128, NT, 2], F32)
    xs = [p.sb(f"xs{i}", [128, D], F32) for i in range(2)]
    sg = [p.sb(f"sg{i}", [128, D], BF16) for i in range(2)]
    xnl = [p.sb(f"xnl{i}", [128, D], BF16) for i in range(2)]
    yr = [p.sb(f"yr{i}", [128, D], BF16) for i in range(2)]
    s1f = p.sb("s1f", [128, D], F32)
    s1t = p.sb("s1t", [128, D], F32)
    t0f = p.sb("t0f", [128, D], F32)
    t1f = p.sb("t1f", [128, D], F32)
    res = p.sb("res", [128, D], F32)
    tmpg = p.sb("tmpg", [128, D], F32)
    g1s = p.sb("g1s", [128, D], F32)
    xf32 = p.sb("xf32", [128, D], F32)
    oh8 = p.sb("oh8", [128, NE], F32)
    t8b = p.sb("t8b", [128, NE], F32)
    gec = p.sb("gec", [128, 2], F32)
    bsel = p.sb("bsel", [128, 1], F32)
    alloc_norm_scratch(p, NT)

    sps = [p.ps(f"sps{i}", [128, 512]) for i in range(2)]
    tps = [p.ps(f"tps{i}", [128, 512]) for i in range(2)]
    fps = [p.ps(f"fps{i}", [128, 512]) for i in range(2)]
    p.bar_ps = p.ps("barps", [128, 512])
    p.bar_mm_src = ident

    @p.on("sync")
    def _(s):
        p.dma("sync", s, ident[:], ident_i)
        for k in range(ND):
            p.dma("sync", s, atg[:, k, :], attng_i[128 * k:128 * (k + 1), :])
            p.dma("sync", s, gtg[:, k, :], gateg_i[128 * k:128 * (k + 1), :])
        p.dma("sync", s, densb[:], deng_i)
        p.dma("sync", s, ebuf[0][:], edgep_i[0:H, :])
        p.dma("sync", s, ebuf[1][:], edgep_i[H:2 * H, :])
        p.dma("sync", s, bot[:], bo_pre)
        p.dma("sync", s, b2c[:], b2_col)
        p.dma("sync", s, cntsb[:], cnts_all)
        p.dma("sync", s, cmk[:], cmask6)
        p.dma("sync", s, io8[:], iota8)
        p.dma("sync", s, fselb[:], fsel)
        p.dma("sync", s, lngr[:], lng_row_i)
        p.dma("sync", s, lnbr[:], lnb_row_i)
        p.dma("sync", s, posb[:], pos1.rearrange("(i pp) j -> pp i j", pp=128))
        p.dma("sync", s, idxb[:], idx1.rearrange("(i pp) j -> pp i j", pp=128))
        p.dma("sync", s, locb[:], locj.rearrange("(i pp) j -> pp i j", pp=128))
        p.dma("sync", s, gb[:], g011.rearrange("(i pp) j -> pp i j", pp=128))

    @p.on("gpsimd")
    def _(g):
        p.dma("gpsimd", g, wos[:], wo.rearrange("(k pp) d -> pp k d", pp=128))

    @p.on("vector")
    def _(v):
        v.memset(onesr[:], 1.0)
        v.memset(ones8[:], 1.0)
    p.barrier()

    @p.on("vector")
    def _(v):
        v.tensor_tensor(out=e16[:], in0=ebuf[0][:], in1=ebuf[1][:], op=ALU.add)
        v.tensor_copy(out=posu[:], in_=posb[:])
    p.barrier()

    @p.on("sync")
    def _(s):
        p.dma("sync", s, ebuf[0][:], edgep_i[2 * H:3 * H, :])
        p.dma("sync", s, ebuf[1][:], edgep_i[3 * H:4 * H, :])
    p.barrier()

    @p.on("vector")
    def _(v):
        v.tensor_tensor(out=e16[:], in0=e16[:], in1=ebuf[0][:], op=ALU.add)
        v.tensor_tensor(out=e16[:], in0=e16[:], in1=ebuf[1][:], op=ALU.add)
        v.tensor_scalar(out=e16[:], in0=e16[:], scalar1=b2c[:, 0:1],
                        scalar2=None, op0=ALU.add)
        v.tensor_scalar(out=e16[:], in0=e16[:], scalar1=60.0,
                        scalar2=None, op0=ALU.min)
        v.tensor_scalar(out=e16[:], in0=e16[:], scalar1=-60.0,
                        scalar2=None, op0=ALU.max)

    @p.on("tensor")
    def _(t):
        t.matmul(sps[0][:, 0:512], onesr[0:1, :], lngr[0:1, 0:512],
                 start=True, stop=True)
        t.matmul(sps[1][:, 0:512], onesr[0:1, :], lngr[0:1, 512:1024],
                 start=True, stop=True)
        t.matmul(fps[0][:, 0:512], onesr[0:1, :], lnbr[0:1, 0:512],
                 start=True, stop=True)
        t.matmul(fps[1][:, 0:512], onesr[0:1, :], lnbr[0:1, 512:1024],
                 start=True, stop=True)
        t.matmul(tps[0][:, 0:8], onesr[0:1, :], io8[0:1, :],
                 start=True, stop=True)
        t.matmul(tps[1][0:1, 0:16], cmk[:, 0:1], cntsb[:, :],
                 start=True, stop=True)
        t.matmul(tps[1][0:1, 16:24], ones8[:, 0:1], cntsb[:, 0:8],
                 start=True, stop=True)
    p.barrier()

    @p.on("scalar")
    def _(s):
        s.activation(out=e16[:], in_=e16[:], func=AF.Exp)

    @p.on("vector")
    def _(v):
        v.tensor_copy(out=lngb[:, 0:512], in_=sps[0][:, 0:512])
        v.tensor_copy(out=lngb[:, 512:1024], in_=sps[1][:, 0:512])
        v.tensor_copy(out=lnbb[:, 0:512], in_=fps[0][:, 0:512])
        v.tensor_copy(out=lnbb[:, 512:1024], in_=fps[1][:, 0:512])
        v.tensor_copy(out=iotab[:], in_=tps[0][:, 0:8])
        v.tensor_copy(out=baser[:], in_=tps[1][0:1, 0:16])
        v.tensor_tensor(out=baser[0:1, 8:16], in0=baser[0:1, 8:16],
                        in1=tps[1][0:1, 16:24], op=ALU.add)
    p.barrier()

    @p.on("vector")
    def _(v):
        v.tensor_tensor(out=densb[:], in0=densb[:], in1=e16[:], op=ALU.mult)
        v.tensor_scalar(out=densb[:], in0=densb[:], scalar1=1.0e-35,
                        scalar2=None, op0=ALU.max)
        v.reciprocal(out=densb[:], in_=densb[:])
        v.tensor_tensor(out=e16[:], in0=e16[:], in1=densb[:], op=ALU.mult)

    @p.on("tensor")
    def _(t):
        t.matmul(fps[0][:, 0:8], onesr[0:1, :], baser[0:1, 0:8],
                 start=True, stop=True)
        t.matmul(fps[0][:, 8:16], onesr[0:1, :], baser[0:1, 8:16],
                 start=True, stop=True)
    p.barrier()

    @p.on("vector")
    def _(v):
        v.tensor_copy(out=basejb[:], in_=fps[0][:, 0:16])
    p.barrier()

    s_fx, b_fx = p.use_sem("fx", 32)
    v0 = p._vcnt

    @p.on("tensor")
    def _(t):
        for k in range(ND):
            for ct in range(4):
                idx = 4 * k + ct
                if idx >= 2:
                    t.wait_ge(p.vsem, v0 + 3 * (idx - 2) + 1)
                t.matmul(fps[idx % 2][:, 0:512],
                         fselb[:, 128 * k:128 * (k + 1)],
                         e16[:, 512 * ct:512 * (ct + 1)],
                         start=True, stop=True).then_inc(s_fx, 1)

    @p.on("vector")
    def _(v):
        for k in range(ND):
            for ct in range(4):
                idx = 4 * k + ct
                sl = slice(512 * ct, 512 * (ct + 1))
                v.wait_ge(s_fx, b_fx + idx + 1)
                v.tensor_copy(out=fxb[:], in_=fps[idx % 2][:, 0:512])
                v.tensor_tensor(out=atg[:, k, sl], in0=atg[:, k, sl],
                                in1=fxb[:], op=ALU.mult)
                v.tensor_tensor(out=atg[:, k, sl], in0=atg[:, k, sl],
                                in1=gtg[:, k, sl], op=ALU.mult)
    p.barrier()

    for i in range(NT):
        d_g = p.dma_cnt["gpsimd"]

        @p.on("sync")
        def _(s, i=i):
            p.dma("sync", s, xs[i % 2][:], x_in[128 * i:128 * (i + 1), :])
            p.dma("sync", s, sg[i % 2][:], s0g0_i[128 * i:128 * (i + 1), :])
            p.dma("sync", s, xnl[i % 2][:], xn1_i[128 * i:128 * (i + 1), :])

        @p.on("gpsimd")
        def _(g, i=i):
            for j in range(2):
                p.idma(g, out=yr[j][:], out_offset=None, in_=y1own,
                       in_offset=IndirectOffsetOnAxis(
                           ap=posu[:, i, j:j + 1], axis=0))

        s_mm, b_mm = p.use_sem("x_mm", ND)
        s_ev, b_ev = p.use_sem("x_ev", ND)

        @p.on("tensor")
        def _(t, i=i, s_mm=s_mm, s_ev=s_ev, b_ev=b_ev):
            for m in range(ND):
                if m >= 2:
                    t.wait_ge(s_ev, b_ev + m - 1)
                for k in range(ND):
                    mm = t.matmul(sps[m % 2][:, 0:128],
                                  wos[:, k, 128 * m:128 * (m + 1)],
                                  atg[:, k, 128 * i:128 * (i + 1)],
                                  start=(k == 0), stop=(k == ND - 1))
                mm.then_inc(s_mm, 1)

        @p.on("scalar")
        def _(s, i=i, s_mm=s_mm, b_mm=b_mm, s_ev=s_ev):
            for m in range(ND):
                s.wait_ge(s_mm, b_mm + m + 1)
                s.activation(out=s1f[:, 128 * m:128 * (m + 1)],
                             in_=sps[m % 2][:, 0:128], func=AF.Identity,
                             bias=bot[:, m:m + 1]).then_inc(s_ev, 1)
        p.barrier()

        s_t, b_t = p.use_sem("x_tr", ND)
        v0t = p._vcnt

        @p.on("tensor")
        def _(t, s_t=s_t, v0t=v0t):
            for k in range(ND):
                if k >= 2:
                    t.wait_ge(p.vsem, v0t + k - 1)
                t.transpose(tps[k % 2][0:128, 0:128],
                            s1f[:, 128 * k:128 * (k + 1)],
                            ident[:]).then_inc(s_t, 1)

        @p.on("vector")
        def _(v, s_t=s_t, b_t=b_t):
            for k in range(ND):
                v.wait_ge(s_t, b_t + k + 1)
                v.tensor_copy(out=s1t[:, 128 * k:128 * (k + 1)],
                              in_=tps[k % 2][0:128, 0:128])
        p.barrier()

        @p.on("vector")
        def _(v, i=i, d_g=d_g):
            v.wait_ge(p.ldsem["gpsimd"], 16 * (d_g + 2))
            for j in range(2):
                v.tensor_scalar(out=oh8[:], in0=iotab[:],
                                scalar1=idxb[:, i, j:j + 1], scalar2=None,
                                op0=ALU.is_equal)
                v.tensor_tensor(out=t8b[:], in0=oh8[:],
                                in1=basejb[:, 8 * j:8 * (j + 1)], op=ALU.mult)
                v.reduce_sum(out=bsel[:], in_=t8b[:], axis=AX.X)
                v.tensor_tensor(out=bsel[:], in0=bsel[:],
                                in1=locb[:, i, j:j + 1], op=ALU.add)
                v.tensor_scalar(out=bsel[:], in0=bsel[:],
                                scalar1=float(C1) + 0.5, scalar2=None,
                                op0=ALU.is_lt)
                v.tensor_tensor(out=gec[:, j:j + 1], in0=bsel[:],
                                in1=gb[:, i, j:j + 1], op=ALU.mult)
            v.tensor_copy(out=xf32[:], in_=xnl[i % 2][:])
            v.tensor_scalar(out=t0f[:], in0=yr[0][:], scalar1=gec[:, 0:1],
                            scalar2=None, op0=ALU.mult)
            v.tensor_scalar(out=t1f[:], in0=yr[1][:], scalar1=gec[:, 1:2],
                            scalar2=None, op0=ALU.mult)
            v.tensor_tensor(out=res[:], in0=t0f[:], in1=t1f[:], op=ALU.add)
            v.tensor_tensor(out=res[:], in0=res[:], in1=xf32[:], op=ALU.add)
            v.reduce_sum(out=p.n_s[:, i:i + 1], in_=res[:], axis=AX.X)
        p.barrier()

        @p.on("scalar")
        def _(s, i=i):
            s.activation(out=p.n_sq[:, 0:D], in_=res[:], func=AF.Square,
                         accum_out=p.n_q[:, i:i + 1])
        p.barrier()

        @p.on("vector")
        def _(v, i=i):
            iv = 1.0 / D
            v.tensor_scalar(out=p.n_mu[:, i:i + 1], in0=p.n_s[:, i:i + 1],
                            scalar1=iv, scalar2=None, op0=ALU.mult)
            v.tensor_scalar(out=p.n_var[:, i:i + 1], in0=p.n_q[:, i:i + 1],
                            scalar1=iv, scalar2=None, op0=ALU.mult)
            v.tensor_tensor(out=p.n_t[:, 0:1], in0=p.n_mu[:, i:i + 1],
                            in1=p.n_mu[:, i:i + 1], op=ALU.mult)
            v.tensor_tensor(out=p.n_var[:, i:i + 1], in0=p.n_var[:, i:i + 1],
                            in1=p.n_t[:, 0:1], op=ALU.subtract)
            v.tensor_scalar(out=p.n_var[:, i:i + 1], in0=p.n_var[:, i:i + 1],
                            scalar1=EPS_LN, scalar2=None, op0=ALU.add)
        p.barrier()

        @p.on("scalar")
        def _(s, i=i):
            s.activation(out=p.n_var[:, i:i + 1], in_=p.n_var[:, i:i + 1],
                         func=AF.Sqrt)
        p.barrier()

        @p.on("vector")
        def _(v, i=i):
            v.reciprocal(out=p.n_var[:, i:i + 1], in_=p.n_var[:, i:i + 1])
            v.tensor_tensor(out=p.n_t[:, 1:2], in0=p.n_mu[:, i:i + 1],
                            in1=p.n_var[:, i:i + 1], op=ALU.mult)
            v.tensor_scalar(out=p.n_t[:, 1:2], in0=p.n_t[:, 1:2],
                            scalar1=-1.0, scalar2=None, op0=ALU.mult)
        p.barrier()

        @p.on("scalar")
        def _(s, i=i):
            s.activation(out=tmpg[:], in_=res[:], func=AF.Identity,
                         scale=p.n_var[:, i:i + 1], bias=p.n_t[:, 1:2])
        p.barrier()

        @p.on("vector")
        def _(v):
            v.tensor_tensor(out=tmpg[:], in0=tmpg[:], in1=lngb[:], op=ALU.mult)
            v.tensor_tensor(out=tmpg[:], in0=tmpg[:], in1=lnbb[:], op=ALU.add)
        p.barrier()

        @p.on("scalar")
        def _(s):
            s.activation(out=g1s[:], in_=tmpg[:], func=AF.Silu)
        p.barrier()

        @p.on("vector")
        def _(v, i=i):
            v.tensor_tensor(out=g1s[:], in0=g1s[:], in1=s1t[:], op=ALU.mult)
            v.tensor_copy(out=t0f[:], in_=sg[i % 2][:])
            v.tensor_tensor(out=xf32[:], in0=xs[i % 2][:], in1=t0f[:],
                            op=ALU.add)
            v.tensor_tensor(out=xf32[:], in0=xf32[:], in1=g1s[:], op=ALU.add)
        p.barrier()

        @p.on("sync")
        def _(s, i=i):
            p.dma("sync", s, x2_out[128 * i:128 * (i + 1), :], xf32[:])
        p.barrier()
    return p.finish()



def _consts():
    e = np.arange(8)
    lt = (e[None, :] < e[:, None]).astype(np.float32).reshape(1, 64)
    ltmask = np.tile(lt, (128, 1)).copy()
    ident = np.eye(128, dtype=np.float32)
    hh = np.arange(16)
    pp = np.arange(128)
    asel = (((pp[None, :] >= 64).astype(int) == (hh[:, None] % 2))
            ).astype(np.float32)
    kk = np.arange(8)
    bsel = (kk[None, :] == hh[:, None] // 2).astype(np.float32)
    return ltmask, ident, asel, bsel


def kernel(**inputs):
    I = {k: np.asarray(v, np.float32) for k, v in inputs.items()}
    ltmask, ident, asel, bsel = _consts()
    cores = list(range(NCORE))

    in1 = []
    for c in cores:
        in1.append(dict(
            x=np.ascontiguousarray(
                I["L0_attn_x"].reshape(B * N, D)[T * c:T * (c + 1)]),
            gate_wT=np.ascontiguousarray(I["L0_ffn_gate_w"].T),
            rms_w_pre=np.ascontiguousarray(
                I["L0_ffn_rms_weight"].reshape(8, 128).T),
            rms_row=I["L0_ffn_rms_weight"].reshape(1, D).copy(),
            lam_log=I["L0_attn_log_decay"].reshape(1, H).copy(),
            ltmask=ltmask, ident=ident,
            tokbase=np.full((128, 1), T * c, np.float32),
            asel=asel, bsel=bsel))
    r1 = run_bass_kernel_spmd(build_p1(), in1, core_ids=cores)
    h_all = [np.asarray(r1.results[c]["h_tm"]) for c in cores]
    hfull = np.concatenate(h_all, 0)
    ids_all = [np.asarray(r1.results[c]["ids"]) for c in cores]
    uend_all = [np.asarray(r1.results[c]["u_end"]) for c in cores]
    pos_all = [np.asarray(r1.results[c]["pos"]) for c in cores]
    g01_all = [np.asarray(r1.results[c]["g01"]) for c in cores]

    in2 = []
    for e in range(NE):
        myids = np.concatenate([ids_all[c][e] for c in cores])
        in2.append(dict(
            w1T=np.ascontiguousarray(I["L0_ffn_w1"][e].T),
            w3T=np.ascontiguousarray(I["L0_ffn_w3"][e].T),
            w2T=np.ascontiguousarray(I["L0_ffn_w2"][e].T),
            hfull=hfull,
            myids=np.ascontiguousarray(myids.reshape(64, 128).T),
            ident=ident))
    r2 = run_bass_kernel_spmd(build_p2(), in2, core_ids=cores)
    y0_all = [np.asarray(r2.results[e]["y0"]) for e in range(NE)]

    uends_pre = np.ascontiguousarray(
        np.concatenate(uend_all, 0).reshape(8, 8, 128).transpose(2, 0, 1))
    in3 = []
    for c in cores:
        cec = np.zeros((128, 8), np.float32)
        cmask = np.zeros((128, 8), np.float32)
        for j in range(8):
            if j // 4 == c // 4 and (j % 4) < (c % 4):
                cmask[:, j] = 1.0
                cec[:, j] = 2048.0 * ((c % 4) - 1 - (j % 4))
        y0own = np.concatenate(
            [y0_all[e][CAPL * c:CAPL * (c + 1)] for e in range(NE)], 0)
        in3.append(dict(
            x=in1[c]["x"], uends=uends_pre, cec=cec, cmask=cmask,
            lam_log=I["L0_attn_log_decay"].reshape(1, H).copy(),
            qvec=I["L0_attn_q"].reshape(H, dh).copy(),
            kvec=I["L0_attn_k"].reshape(H, dh).copy(),
            nw_row=I["L0_attn_norm_weight"].reshape(1, D).copy(),
            o_wT=np.ascontiguousarray(I["L0_attn_o_w"].T),
            y0own=y0own,
            pos0=pos_all[c], g01_0=g01_all[c],
            wg=I["L1_ffn_wg"].copy(),
            ltmask=ltmask, ident=ident,
            tokbase=np.full((128, 1), T * c, np.float32),
            asel=asel, bsel=bsel))
    r3 = run_bass_kernel_spmd(build_p3(), in3, core_ids=cores)
    xn1_all = [np.asarray(r3.results[c]["xn1"]) for c in cores]
    s0g0_all = [np.asarray(r3.results[c]["s0g0"]) for c in cores]
    ids1_all = [np.asarray(r3.results[c]["ids1"]) for c in cores]
    pos1_all = [np.asarray(r3.results[c]["pos1"]) for c in cores]
    idx1_all = [np.asarray(r3.results[c]["idx1"]) for c in cores]
    locj_all = [np.asarray(r3.results[c]["locj"]) for c in cores]
    g011_all = [np.asarray(r3.results[c]["g011"]) for c in cores]
    cnts_l = [np.asarray(r3.results[c]["cnts"]) for c in cores]

    tbl = I["L1_attn_relpos_emb"]
    deltas = [-256, -128, 0, 128, 256]
    ppg, ffg = np.meshgrid(np.arange(128), np.arange(128), indexing="ij")
    in45 = []
    for c in cores:
        bat = c // 4
        g = c % 4
        HG = 4 * g
        r5 = np.zeros((5, 4, 128, 128), np.float32)
        for ci, dl in enumerate(deltas):
            idxm = np.clip(ffg - ppg - dl, -255, 255) + 255
            for h2 in range(4):
                r5[ci, h2] = tbl[idxm, HG + h2]
        tconsts = np.concatenate(
            [tbl[0, HG:HG + 4], tbl[510, HG:HG + 4]]).reshape(1, 8)
        order = (list(range(4, 8)) + list(range(0, 4))) if bat == 1 \
            else list(range(8))
        myids = np.concatenate([ids1_all[cc][c] for cc in cores])
        m = {f"xn1_{cc}": xn1_all[order[cc]] for cc in cores}
        m.update(dict(
            myids1=np.ascontiguousarray(myids.reshape(64, 128).T),
            fc1T=np.ascontiguousarray(I["L1_ffn_fc1_w"][c].T),
            fc2n=I["L1_ffn_fc2_w"][c].copy(),
            fc1b_pre=np.ascontiguousarray(
                I["L1_ffn_fc1_b"][c].reshape(32, 128).T),
            fc2b_pre=np.ascontiguousarray(
                I["L1_ffn_fc2_b"][c].reshape(8, 128).T),
            wq_sl=np.ascontiguousarray(
                I["L1_attn_Wqkv"][:, 256 * g:256 * (g + 1)]),
            wk_sl=np.ascontiguousarray(
                I["L1_attn_Wqkv"][:, 1024 + 256 * g:1024 + 256 * (g + 1)]),
            wv_sl=np.ascontiguousarray(
                I["L1_attn_Wqkv"][:, 2048 + 256 * g:2048 + 256 * (g + 1)]),
            wgd_sl=np.ascontiguousarray(
                I["L1_attn_Wg"][:, 256 * g:256 * (g + 1)]),
            bq_pre=np.ascontiguousarray(
                I["L1_attn_bqkv"][256 * g:256 * (g + 1)].reshape(2, 128).T),
            bk_pre=np.ascontiguousarray(
                I["L1_attn_bqkv"][1024 + 256 * g:1024 + 256 * (g + 1)]
                .reshape(2, 128).T),
            bv_row=I["L1_attn_bqkv"][2048 + 256 * g:2048 + 256 * (g + 1)]
            .reshape(1, 256).copy(),
            bg_pre=np.ascontiguousarray(
                I["L1_attn_bg"][256 * g:256 * (g + 1)].reshape(2, 128).T),
            w1_sl=np.ascontiguousarray(
                I["L1_attn_W1"][:, 1024 * g:1024 * (g + 1)]),
            b1_pre=np.ascontiguousarray(
                I["L1_attn_b1"][1024 * g:1024 * (g + 1)].reshape(8, 128).T),
            w2_sl=np.ascontiguousarray(
                I["L1_attn_W2"][1024 * g:1024 * (g + 1), :]),
            b2_row=I["L1_attn_b2"].reshape(1, 16).copy(),
            r5=r5, tconsts=tconsts.astype(np.float32),
            boff=np.full((128, 1), float(bat * N), np.float32),
            ident=ident))
        in45.append(m)
    r45 = run_bass_kernel_spmd(build_p45(), in45, core_ids=cores)
    y1_all = [np.asarray(r45.results[c]["y1"]) for c in cores]
    numg_all = [np.asarray(r45.results[c]["numg"]) for c in cores]
    gate_all = [np.asarray(r45.results[c]["gate"]) for c in cores]
    den_all = [np.asarray(r45.results[c]["den"]) for c in cores]
    edgep_all = [np.asarray(r45.results[c]["edgep"]) for c in cores]

    hhv = np.arange(16)[:, None]
    kpv = np.arange(D)[None, :]
    fsel = (hhv == 2 * (kpv // 128) + ((kpv % 128) >= 64)).astype(np.float32)
    cnts_all = np.stack([cnts_l[cc].reshape(16) for cc in cores], 0)
    in6 = []
    for c in cores:
        bat = c // 4
        cs = slice(2048 * (c % 4), 2048 * (c % 4 + 1))
        attng = np.concatenate(
            [numg_all[4 * bat + g][:, cs] for g in range(4)], 0)
        gateg = np.concatenate(
            [gate_all[4 * bat + g][:, cs] for g in range(4)], 0)
        deng = np.concatenate(
            [den_all[4 * bat + g][:, cs] for g in range(4)], 0)
        edgepg = np.concatenate(
            [edgep_all[4 * bat + g][:, cs] for g in range(4)], 0)
        y1own = np.concatenate(
            [y1_all[e][CAPL * c:CAPL * (c + 1)] for e in range(NE)], 0)
        in6.append(dict(
            attng=np.ascontiguousarray(attng),
            gateg=np.ascontiguousarray(gateg),
            deng=np.ascontiguousarray(deng),
            edgepg=np.ascontiguousarray(edgepg),
            wo=I["L1_attn_Wo"].copy(),
            bo_pre=np.ascontiguousarray(I["L1_attn_bo"].reshape(8, 128).T),
            b2_col=I["L1_attn_b2"].reshape(16, 1).copy(),
            lng_row=I["L1_ffn_ln_gamma"].reshape(1, D).copy(),
            lnb_row=I["L1_ffn_ln_beta"].reshape(1, D).copy(),
            y1own=y1own, x=in1[c]["x"],
            s0g0=s0g0_all[c], xn1=xn1_all[c],
            pos1=pos1_all[c], idx1=idx1_all[c], locj=locj_all[c],
            g011=g011_all[c], cnts_all=cnts_all,
            cmask6=(np.arange(8) < c).astype(np.float32).reshape(8, 1),
            iota8=np.arange(8, dtype=np.float32).reshape(1, 8),
            fsel=fsel, ident=ident))
    r6 = run_bass_kernel_spmd(build_p6(), in6, core_ids=cores)
    x2 = np.concatenate([np.asarray(r6.results[c]["x2"]) for c in cores], 0)
    return np.ascontiguousarray(x2.reshape(B, N, D).astype(np.float32))


# version 07
# speedup vs baseline: 1.1588x; 1.1588x over previous
import numpy as np
import ml_dtypes
import concourse.bass as bass
import concourse.mybir as mybir
from concourse.bass import IndirectOffsetOnAxis
from concourse.bass_utils import run_bass_kernel_spmd
from contextlib import ExitStack

F32 = mybir.dt.float32
BF16 = mybir.dt.bfloat16
U32 = mybir.dt.uint32
AF = mybir.ActivationFunctionType
ALU = mybir.AluOpType
AX = mybir.AxisListType
BFNP = ml_dtypes.bfloat16

NCORE = 8
B, N, D = 2, 8192, 1024
T = 2048
NT = T // 128
ND = D // 128
H, dh = 16, 64
NE = 8
I0 = 3584
NI0 = I0 // 128
H1 = 4096
NH1 = H1 // 128
CAPL = 1024
RROWS = NE * CAPL
C1 = 5120
EPS_LN = 1e-5
EPS_RMS = 1e-6
EPS_G = 1.1920929e-07
OOBID = 10 ** 8

ENGS = ("tensor", "vector", "scalar", "gpsimd", "sync")
_SC_LIMIT = None
_SC_PHASE = None


class Prog:

    def __init__(self, name):
        self.name = name
        self.nc = bass.Bass("TRN2", target_bir_lowering=False, debug=False)
        self.stack = ExitStack()
        self.nbar = 0
        self.dma_cnt = {"sync": 0, "gpsimd": 0}
        self.block = None

    def sb(self, name, shape, dtype):
        return self.stack.enter_context(
            self.nc.sbuf_tensor(name, list(shape), dtype))

    def ps(self, name, shape, dtype=F32):
        shape = list(shape)
        per = int(np.prod(shape[1:])) * mybir.dt.size(dtype)
        assert per <= 2048, f"psum tensor {name} exceeds one bank"
        if per < 2048:
            shape = [shape[0], 2048 // mybir.dt.size(dtype)]
        return self.stack.enter_context(
            self.nc.psum_tensor(name, shape, dtype))

    def sem(self, name):
        return self.stack.enter_context(self.nc.semaphore(name))

    def use_sem(self, tag, nincs):
        if not hasattr(self, "_usems"):
            self._usems = {}
            self._usem_base = {}
        if tag not in self._usems:
            self._usems[tag] = self.sem(f"u_{tag}")
            self._usem_base[tag] = 0
        b = self._usem_base[tag]
        self._usem_base[tag] = b + nincs
        return self._usems[tag], b

    def inp(self, name, shape, dtype=F32):
        return self.nc.dram_tensor(name, list(shape), dtype,
                                   kind="ExternalInput").ap()

    def outp(self, name, shape, dtype=F32):
        return self.nc.dram_tensor(name, list(shape), dtype,
                                   kind="ExternalOutput").ap()

    def init_sems(self):
        self.bsem = {e: self.sem(f"b_{e}") for e in ENGS}
        self.ldsem = {"sync": self.sem("ld_sync"), "gpsimd": self.sem("ld_gp")}
        self.block = self.stack.enter_context(self.nc.Block())
        self.last_compute = {e: None for e in ENGS}
        self._proxies = {}
        self.bar_sb = self.sb("bar_sb", [1, 8], F32)
        self.bar_mm_src = None
        self.bar_ps = None
        self.vsem = self.sem("vsem")
        self.gsem = self.sem("gsem")
        self._vcnt = 0
        self._gcnt = 0

    _NON_COMPUTE = {"dma_start", "indirect_dma_start", "dma_start_transpose",
                    "wait_ge", "nop", "engine_nop", "maybe_nop_then_inc",
                    "then_inc"}

    class _EngProxy:
        def __init__(self, prog, eng_name, raw):
            self._p = prog
            self._e = eng_name
            self._raw = raw

        def __getattr__(self, name):
            fn = getattr(self._raw, name)
            if not callable(fn) or name.startswith("_"):
                return fn
            if name in Prog._NON_COMPUTE:
                return fn

            def wrapped(*a, **kw):
                p = self._p
                if self._e == "vector":
                    if p._vcnt > 0:
                        self._raw.wait_ge(p.vsem, p._vcnt)
                    r = fn(*a, **kw)
                    if r is not None and hasattr(r, "then_inc"):
                        r.then_inc(p.vsem, 1)
                        p._vcnt += 1
                    return r
                if self._e == "gpsimd":
                    if p._gcnt > 0:
                        self._raw.wait_ge(p.gsem, p._gcnt)
                    r = fn(*a, **kw)
                    if r is not None and hasattr(r, "then_inc"):
                        r.then_inc(p.gsem, 1)
                        p._gcnt += 1
                    return r
                r = fn(*a, **kw)
                if r is not None and hasattr(r, "then_inc"):
                    p.last_compute[self._e] = r
                return r
            return wrapped

    def on(self, eng):
        dec = getattr(self.block, eng)

        def deco(fn):
            def wrapped(raw):
                px = self._proxies.get(eng)
                if px is None or px._raw is not raw:
                    px = Prog._EngProxy(self, eng, raw)
                    self._proxies[eng] = px
                fn(px)
            dec(wrapped)
        return deco

    def dma(self, q, eng_obj, out, in_, **kw):
        self.dma_cnt[q] += 1
        eng_obj.dma_start(out=out, in_=in_, **kw).then_inc(self.ldsem[q], 16)

    def idma(self, eng_obj, **kw):
        self.dma_cnt["gpsimd"] += 1
        eng_obj.indirect_dma_start(**kw).then_inc(self.ldsem["gpsimd"], 16)

    def idma_lane(self, eng_obj, sem, **kw):
        eng_obj.indirect_dma_start(**kw).then_inc(sem, 16)

    def barrier(self):
        self.nbar += 1
        k = self.nbar
        cnt = dict(self.dma_cnt)
        for e in ENGS:
            li = self.last_compute[e]
            self.last_compute[e] = None
            busy = li is not None
            si = li.ins.sync_info if li is not None else None
            if li is not None and si is not None and len(si.on_update) > 0:
                li = None
            if li is not None:
                li.then_inc(self.bsem[e], 1)
            vcnt, gcnt = self._vcnt, self._gcnt

            @self.on(e)
            def _(eng, e=e, li=li, busy=busy):
                if e in ("sync", "gpsimd"):
                    eng.wait_ge(self.ldsem[e], 16 * cnt[e])
                if e == "vector":
                    eng._raw.wait_ge(self.vsem, vcnt) if hasattr(eng, "_raw")                         else eng.wait_ge(self.vsem, vcnt)
                    eng.nop().then_inc(self.bsem[e], 1)
                elif e == "gpsimd":
                    eng.wait_ge(self.gsem, gcnt)
                    eng.nop().then_inc(self.bsem[e], 1)
                elif li is None:
                    if busy and e == "scalar":
                        eng.memzero(self.bar_sb[0:1, 2:3]).then_inc(self.bsem[e], 1)
                    elif busy and e == "tensor" and self.bar_mm_src is not None:
                        eng.matmul(self.bar_ps[0:1, 0:1],
                                   self.bar_mm_src[0:1, 0:1],
                                   self.bar_mm_src[0:1, 0:1],
                                   start=True, stop=True).then_inc(self.bsem[e], 1)
                    else:
                        eng.nop().then_inc(self.bsem[e], 1)
                for o in ENGS:
                    eng.wait_ge(self.bsem[o], k)
                eng.wait_ge(self.ldsem["sync"], 16 * cnt["sync"])
                eng.wait_ge(self.ldsem["gpsimd"], 16 * cnt["gpsimd"])

    def finish(self):
        self.stack.close()
        return self.nc



def alloc_norm_scratch(p, nt):
    p.n_s = p.sb("n_s", [128, nt], F32)
    p.n_q = p.sb("n_q", [128, nt], F32)
    p.n_var = p.sb("n_var", [128, nt], F32)
    p.n_mu = p.sb("n_mu", [128, nt], F32)
    p.n_ab = p.sb("n_ab", [128, 2 * nt], F32)
    p.n_sq = p.sb("n_sq", [128, 1024], F32)
    p.n_t = p.sb("n_t", [128, 4], F32)


def emit_norm_stats(p, x_tiles, nt, eps, with_mean, width=D):
    @p.on("scalar")
    def _(s):
        for i in range(nt):
            s.activation(out=p.n_sq[:, 0:width], in_=x_tiles[i][:],
                         func=AF.Square, accum_out=p.n_q[:, i:i + 1])

    if with_mean:
        @p.on("vector")
        def _(v):
            for i in range(nt):
                v.reduce_sum(out=p.n_s[:, i:i + 1], in_=x_tiles[i][:], axis=AX.X)
    p.barrier()

    @p.on("vector")
    def _(v):
        inv = 1.0 / width
        if with_mean:
            v.tensor_scalar(out=p.n_mu[:, 0:nt], in0=p.n_s[:, 0:nt],
                            scalar1=inv, scalar2=None, op0=ALU.mult)
            v.tensor_scalar(out=p.n_var[:, 0:nt], in0=p.n_q[:, 0:nt],
                            scalar1=inv, scalar2=None, op0=ALU.mult)
            v.tensor_tensor(out=p.n_s[:, 0:nt], in0=p.n_mu[:, 0:nt],
                            in1=p.n_mu[:, 0:nt], op=ALU.mult)
            v.tensor_tensor(out=p.n_var[:, 0:nt], in0=p.n_var[:, 0:nt],
                            in1=p.n_s[:, 0:nt], op=ALU.subtract)
        else:
            v.tensor_scalar(out=p.n_var[:, 0:nt], in0=p.n_q[:, 0:nt],
                            scalar1=inv, scalar2=None, op0=ALU.mult)
        v.tensor_scalar(out=p.n_var[:, 0:nt], in0=p.n_var[:, 0:nt],
                        scalar1=eps, scalar2=None, op0=ALU.add)
    p.barrier()

    @p.on("scalar")
    def _(s):
        s.activation(out=p.n_var[:, 0:nt], in_=p.n_var[:, 0:nt], func=AF.Sqrt)
    p.barrier()

    @p.on("vector")
    def _(v):
        v.reciprocal(out=p.n_var[:, 0:nt], in_=p.n_var[:, 0:nt])
        for i in range(nt):
            v.tensor_copy(out=p.n_ab[:, 2 * i:2 * i + 1],
                          in_=p.n_var[:, i:i + 1])
            if with_mean:
                v.tensor_tensor(out=p.n_ab[:, 2 * i + 1:2 * i + 2],
                                in0=p.n_mu[:, i:i + 1], in1=p.n_var[:, i:i + 1],
                                op=ALU.mult)
                v.tensor_scalar(out=p.n_ab[:, 2 * i + 1:2 * i + 2],
                                in0=p.n_ab[:, 2 * i + 1:2 * i + 2],
                                scalar1=-1.0, scalar2=None, op0=ALU.mult)
            else:
                v.memset(p.n_ab[:, 2 * i + 1:2 * i + 2], 0.0)
    p.barrier()


def emit_norm_apply(p, x_tiles, out_tiles, nt):
    @p.on("scalar")
    def _(s):
        for i in range(nt):
            s.activation(out=out_tiles[i][:], in_=x_tiles[i][:], func=AF.Identity,
                         scale=p.n_ab[:, 2 * i:2 * i + 1],
                         bias=p.n_ab[:, 2 * i + 1:2 * i + 2])
    p.barrier()


def alloc_router_scratch(p, nt):
    p.r_nm = p.sb("r_nm", [128, nt], F32)
    p.r_sum = p.sb("r_sum", [128, nt], F32)
    p.r_rk = p.sb("r_rk", [128, 8], F32)
    p.r_64 = p.sb("r_64", [128, 64], F32)
    p.r_64b = p.sb("r_64b", [128, 64], F32)
    p.r_t8 = p.sb("r_t8", [128, 8], F32)
    p.r_t2 = p.sb("r_t2", [128, 2], F32)


def emit_router(p, lg_tiles, sc_tiles, m0_t, m1_t, g01_t, ltmask, nt, clip_eps):
    @p.on("vector")
    def _(v):
        for i in range(nt):
            v.reduce_max(out=p.r_nm[:, i:i + 1], in_=lg_tiles[i][:], axis=AX.X)
        v.tensor_scalar(out=p.r_nm[:, 0:nt], in0=p.r_nm[:, 0:nt],
                        scalar1=-1.0, scalar2=None, op0=ALU.mult)
    p.barrier()

    @p.on("scalar")
    def _(s):
        for i in range(nt):
            s.activation(out=sc_tiles[i][:], in_=lg_tiles[i][:], func=AF.Exp,
                         bias=p.r_nm[:, i:i + 1],
                         accum_out=p.r_sum[:, i:i + 1])
    p.barrier()

    @p.on("vector")
    def _(v):
        v.reciprocal(out=p.r_sum[:, 0:nt], in_=p.r_sum[:, 0:nt])
        for i in range(nt):
            v.tensor_scalar(out=sc_tiles[i][:], in0=sc_tiles[i][:],
                            scalar1=p.r_sum[:, i:i + 1], scalar2=None,
                            op0=ALU.mult)
    p.barrier()

    @p.on("vector")
    def _(v):
        for i in range(nt):
            sc = sc_tiles[i][:]
            A = sc.rearrange("p (a e) -> p a e", a=1).broadcast_to([128, 8, 8])
            Bv = sc.rearrange("p (e a) -> p e a", a=1).broadcast_to([128, 8, 8])
            g3 = p.r_64[:].rearrange("p (e f) -> p e f", f=8)
            g3b = p.r_64b[:].rearrange("p (e f) -> p e f", f=8)
            v.tensor_tensor(out=g3, in0=Bv, in1=A, op=ALU.is_lt)
            v.reduce_sum(out=p.r_rk[:].rearrange("p (e a) -> p e a", a=1),
                         in_=g3, axis=AX.X)
            v.tensor_tensor(out=g3, in0=Bv, in1=A, op=ALU.is_equal)
            v.tensor_tensor(out=g3b, in0=g3,
                            in1=ltmask[:].rearrange("p (e f) -> p e f", f=8),
                            op=ALU.mult)
            v.reduce_sum(out=p.r_t8[:].rearrange("p (e a) -> p e a", a=1),
                         in_=g3b, axis=AX.X)
            v.tensor_tensor(out=p.r_rk[:], in0=p.r_rk[:], in1=p.r_t8[:],
                            op=ALU.add)
            v.tensor_scalar(out=m0_t[i][:], in0=p.r_rk[:], scalar1=0.5,
                            scalar2=None, op0=ALU.is_lt)
            v.tensor_scalar(out=m1_t[i][:], in0=p.r_rk[:], scalar1=0.5,
                            scalar2=None, op0=ALU.is_ge)
            v.tensor_scalar(out=p.r_t8[:], in0=p.r_rk[:], scalar1=1.5,
                            scalar2=None, op0=ALU.is_lt)
            v.tensor_tensor(out=m1_t[i][:], in0=m1_t[i][:], in1=p.r_t8[:],
                            op=ALU.mult)
            v.tensor_tensor(out=p.r_t8[:], in0=m0_t[i][:], in1=sc, op=ALU.mult)
            v.reduce_sum(out=g01_t[i][:, 0:1], in_=p.r_t8[:], axis=AX.X)
            v.tensor_tensor(out=p.r_t8[:], in0=m1_t[i][:], in1=sc, op=ALU.mult)
            v.reduce_sum(out=g01_t[i][:, 1:2], in_=p.r_t8[:], axis=AX.X)
            v.tensor_tensor(out=p.r_t2[:, 0:1], in0=g01_t[i][:, 0:1],
                            in1=g01_t[i][:, 1:2], op=ALU.add)
            if clip_eps is not None:
                v.tensor_scalar(out=p.r_t2[:, 0:1], in0=p.r_t2[:, 0:1],
                                scalar1=clip_eps, scalar2=None, op0=ALU.max)
            v.reciprocal(out=p.r_t2[:, 0:1], in_=p.r_t2[:, 0:1])
            v.tensor_scalar(out=g01_t[i][:], in0=g01_t[i][:],
                            scalar1=p.r_t2[:, 0:1], scalar2=None, op0=ALU.mult)
    p.barrier()


def _ap_base(ap):
    row = int(np.prod(ap.tensor.shape[1:]))
    return int(ap.offset) // row if row else 0


def emit_transpose_seq(p, pairs, ident, tps):
    p._tseq = getattr(p, "_tseq", 0) + 1
    s_t = p.sem(f"tt{p._tseq}")
    nb = len(tps)
    v0 = p._vcnt

    @p.on("tensor")
    def _(t):
        for k, (src, dst) in enumerate(pairs):
            if k >= nb:
                t.wait_ge(p.vsem, v0 + k - nb + 1)
            bp_ = src.tensor.base_partition if hasattr(src.tensor, "base_partition") else 0
            b0 = src.ap[0]
            t.transpose(tps[k % nb][0:src.shape[1], 0:src.shape[0]], src,
                        ident[0:src.shape[0], 0:src.shape[0]]
                        if _ap_base(src) == 0 else
                        ident[_ap_base(src):_ap_base(src) + src.shape[0],
                              _ap_base(src):_ap_base(src) + src.shape[0]]
                        ).then_inc(s_t, 1)

    @p.on("vector")
    def _(v):
        for k, (src, dst) in enumerate(pairs):
            v.wait_ge(s_t, k + 1)
            v.tensor_copy(out=dst,
                          in_=tps[k % nb][0:src.shape[1], 0:src.shape[0]])
    p.barrier()


def emit_perhead_to_chan(p, row16, out128, Asel, Bsel, colps, outps, scr16):
    @p.on("tensor")
    def _(t):
        t.transpose(colps[0:16, 0:1], row16[:], p._id16[0:1, 0:1])
    p.barrier()

    @p.on("vector")
    def _(v):
        v.tensor_copy(out=scr16[:, 0:1], in_=colps[0:16, 0:1])
        v.tensor_scalar(out=scr16[:, 1:9], in0=Bsel[:], scalar1=scr16[:, 0:1],
                        scalar2=None, op0=ALU.mult)
    p.barrier()

    @p.on("tensor")
    def _(t):
        t.matmul(outps[:, 0:8], Asel[:], scr16[:, 1:9], start=True, stop=True)
    p.barrier()

    @p.on("vector")
    def _(v):
        v.tensor_copy(out=out128[:], in_=outps[:, 0:8])
    p.barrier()



def build_p1(debug=False):
    p = Prog("p1")
    x = p.inp("x", [T, D])
    gate_wT = p.inp("gate_wT", [D, NE])
    rms_w_pre = p.inp("rms_w_pre", [128, ND])
    lam_log = p.inp("lam_log", [1, H])
    ltmask_i = p.inp("ltmask", [128, 64])
    ident_i = p.inp("ident", [128, 128])
    tokbase = p.inp("tokbase", [128, 1])
    asel_i = p.inp("asel", [16, 128])
    bsel_i = p.inp("bsel", [16, 8])
    rms_row = p.inp("rms_row", [1, D])

    if debug:
        dbg_xn = p.outp("dbg_xn", [128, D])
        dbg_lamc = p.outp("dbg_lamc", [128, ND])
        dbg_ab = p.outp("dbg_ab", [128, 2 * NT])
        dbg_xnfm = p.outp("dbg_xnfm", [128, 256])
        dbg_lg = p.outp("dbg_lg", [128, 8])
        dbg_rk = p.outp("dbg_rk", [128, 8])
    h_out = p.outp("h_tm", [T, D], BF16)
    uend_out = p.outp("u_end", [1, D])
    ids_out = p.outp("ids", [NE, CAPL])
    pos_out = p.outp("pos", [T, 2])
    g01_out = p.outp("g01", [T, 2])

    p.init_sems()
    xt = [p.sb(f"xt{i}", [128, D], F32) for i in range(NT)]
    xn_fm = [p.sb(f"xnfm{k}", [128, T], F32) for k in range(ND)]
    u1 = p.sb("u1", [128, T], F32)
    ht = [p.sb(f"ht{i}", [128, D], BF16) for i in range(2)]
    lamc = p.sb("lamc", [128, ND], F32)
    gwp = [p.sb(f"gwp{k}", [128, 8], F32) for k in range(ND)]
    lgt = [p.sb(f"lgt{i}", [128, 8], F32) for i in range(NT)]
    sct = [p.sb(f"sct{i}", [128, 8], F32) for i in range(NT)]
    m0t = [p.sb(f"m0t{i}", [128, 8], F32) for i in range(NT)]
    m1t = [p.sb(f"m1t{i}", [128, 8], F32) for i in range(NT)]
    g01t = [p.sb(f"g01t{i}", [128, 2], F32) for i in range(NT)]
    rk_tm = [p.sb(f"rktm{i}", [128, 8], F32) for i in range(NT)]
    ltmask = p.sb("ltm", [128, 64], F32)
    ident = p.sb("id128", [128, 128], F32)
    p._id16 = ident
    lamrow = p.sb("lamrow", [1, H], F32)
    asel = p.sb("asel_s", [16, 128], F32)
    bsel = p.sb("bsel_s", [16, 8], F32)
    scr16 = p.sb("scr16", [16, 16], F32)
    mscan = p.sb("mscan", [128, T], F32)
    rkscan = p.sb("rkscan", [128, T], F32)
    z81 = p.sb("z81", [128, 1], F32)
    tokb = p.sb("tokb", [128, 1], F32)
    iotap = p.sb("iotap", [128, 1], F32)
    econ = p.sb("econ", [128, 8], F32)
    scr = p.sb("scrA", [128, 16], F32)
    scrB = p.sb("scrB", [128, 16], F32)
    dstu = p.sb("dstu", [128, 1], U32)
    vals = p.sb("vals", [128, 1], F32)
    posf = [p.sb(f"posf{i}", [128, 2], F32) for i in range(NT)]
    oobfill = p.sb("oobfill", [128, 64], F32)
    ue8 = p.sb("ue8", [128, ND], F32)
    rmsb = p.sb("rmsb", [128, ND], F32)
    tp2 = [p.ps(f"tp{i}", [128, 128]) for i in range(2)]
    lgps = p.ps("lgps", [128, 8])
    php = p.ps("php", [128, 128])
    p.bar_ps = p.ps("barps", [128, 512])
    p.bar_mm_src = ident
    alloc_norm_scratch(p, NT)
    alloc_router_scratch(p, NT)

    @p.on("sync")
    def _(s):
        for i in range(NT):
            p.dma("sync", s, xt[i][:], x[128 * i:128 * (i + 1), :])
        p.dma("sync", s, ltmask[:], ltmask_i)
        p.dma("sync", s, ident[:], ident_i)
        p.dma("sync", s, lamrow[:], lam_log)
        p.dma("sync", s, tokb[:], tokbase)
        p.dma("sync", s, rmsb[:], rms_w_pre)
        p.dma("sync", s, asel[:], asel_i)
        p.dma("sync", s, bsel[:], bsel_i)
        for k in range(ND):
            p.dma("sync", s, gwp[k][:], gate_wT[128 * k:128 * (k + 1), :])

    @p.on("gpsimd")
    def _(g):
        g.iota(iotap[:], pattern=[[0, 1]], base=0, channel_multiplier=1,
               allow_small_or_imprecise_dtypes=True)
        g.iota(econ[:], pattern=[[1, 8]], base=0, channel_multiplier=0,
               allow_small_or_imprecise_dtypes=True)
        g.memset(oobfill[:], float(OOBID))
        g.memset(z81[:], 0.0)
    p.barrier()

    @p.on("gpsimd")
    def _(g):
        p.dma("gpsimd", g,
              ids_out.rearrange("e (r q) -> (e r) q", q=64), oobfill[:])
    p.barrier()

    emit_norm_stats(p, xt, NT, EPS_LN, with_mean=True)
    emit_norm_apply(p, xt, xt, NT)

    pairs = []
    for i in range(NT):
        for k in range(ND):
            pairs.append((xt[i][:, 128 * k:128 * (k + 1)],
                          xn_fm[k][:, 128 * i:128 * (i + 1)]))
    emit_transpose_seq(p, pairs, ident, tp2)

    @p.on("scalar")
    def _(s):
        s.activation(out=lamrow[:], in_=lamrow[:], func=AF.Sigmoid)
    p.barrier()
    emit_perhead_to_chan(p, lamrow, lamc, asel, bsel, tp2[0], php, scr16)

    @p.on("vector")
    def _(v):
        for k in range(ND):
            v.tensor_tensor_scan(out=u1[:], data0=lamc[:, k:k + 1].broadcast_to([128, T]),
                                 data1=xn_fm[k][:], initial=0.0,
                                 op0=ALU.mult, op1=ALU.add)
            v.tensor_copy(out=ue8[:, k:k + 1], in_=u1[:, T - 1:T])
    p.barrier()
    @p.on("tensor")
    def _(t):
        t.transpose(tp2[0][0:8, 0:128], ue8[:], ident[:])
    p.barrier()

    @p.on("vector")
    def _(v):
        v.tensor_copy(out=mscan[0:8, 0:128], in_=tp2[0][0:8, 0:128])
    p.barrier()

    @p.on("gpsimd")
    def _(g):
        p.dma("gpsimd", g, uend_out.rearrange("a (k q) -> (a k) q", q=128),
              mscan[0:8, 0:128])
    p.barrier()

    emit_norm_stats(p, xt, NT, EPS_RMS, with_mean=False)

    rmsw_row = p.sb("rmsw_row", [128, D], F32)
    onescol = p.sb("onescol", [1, 128], F32)
    rmsrow_s = p.sb("rmsrow_s", [1, D], F32)
    rps = [p.ps(f"rps{i}", [128, 512]) for i in range(2)]

    @p.on("vector")
    def _(v):
        v.memset(onescol[:], 1.0)

    @p.on("sync")
    def _(s):
        p.dma("sync", s, rmsrow_s[:], rms_row)
    p.barrier()

    @p.on("tensor")
    def _(t):
        for half in range(2):
            t.matmul(rps[half][:], onescol[:],
                     rmsrow_s[:, 512 * half:512 * (half + 1)],
                     start=True, stop=True)
    p.barrier()

    @p.on("vector")
    def _(v):
        for half in range(2):
            v.tensor_copy(out=rmsw_row[:, 512 * half:512 * (half + 1)],
                          in_=rps[half][:])
    p.barrier()

    v0h = p._vcnt
    dbase = p.dma_cnt["gpsimd"]

    @p.on("vector")
    def _(v):
        for i in range(NT):
            if i >= 2:
                v.wait_ge(p.ldsem["gpsimd"], 16 * (dbase + i - 1))
            v.scalar_tensor_tensor(out=ht[i % 2][:], in0=xt[i][:],
                                   scalar=p.n_ab[:, 2 * i:2 * i + 1],
                                   in1=rmsw_row[:], op0=ALU.mult,
                                   op1=ALU.mult)

    @p.on("gpsimd")
    def _(g):
        for i in range(NT):
            g.wait_ge(p.vsem, v0h + i + 1)
            p.dma("gpsimd", g, h_out[128 * i:128 * (i + 1), :], ht[i % 2][:])
    p.barrier()

    @p.on("vector")
    def _(v):
        for k in range(ND):
            v.tensor_scalar(out=gwp[k][:], in0=gwp[k][:],
                            scalar1=rmsb[:, k:k + 1], scalar2=None,
                            op0=ALU.mult)
    p.barrier()

    lgsem = p.sem("lgsem")
    v0l = p._vcnt

    @p.on("tensor")
    def _(t):
        for i in range(NT):
            if i >= 1:
                t.wait_ge(p.vsem, v0l + i)
            for k in range(ND):
                mm = t.matmul(lgps[:, 0:8], xn_fm[k][:, 128 * i:128 * (i + 1)],
                              gwp[k][:], start=(k == 0), stop=(k == ND - 1))
            mm.then_inc(lgsem, 1)

    @p.on("vector")
    def _(v):
        for i in range(NT):
            v.wait_ge(lgsem, i + 1)
            v.tensor_scalar(out=lgt[i][:], in0=lgps[:, 0:8],
                            scalar1=p.n_ab[:, 2 * i:2 * i + 1], scalar2=None,
                            op0=ALU.mult)
    p.barrier()

    emit_router(p, lgt, sct, m0t, m1t, g01t, ltmask, NT, clip_eps=None)

    @p.on("vector")
    def _(v):
        for i in range(NT):
            v.tensor_tensor(out=rk_tm[i][:], in0=m0t[i][:], in1=m1t[i][:],
                            op=ALU.add)
    p.barrier()
    pairs = [(rk_tm[i][:, 0:8], mscan[0:8, 128 * i:128 * (i + 1)])
             for i in range(NT)]
    emit_transpose_seq(p, pairs, ident, tp2)

    @p.on("vector")
    def _(v):
        v.tensor_tensor_scan(out=rkscan[0:8, 0:T],
                             data0=z81[0:8, 0:1].broadcast_to([8, T]),
                             data1=mscan[0:8, 0:T], initial=0.0,
                             op0=ALU.add, op1=ALU.add)
    p.barrier()
    pairs = [(rkscan[0:8, 128 * i:128 * (i + 1)], rk_tm[i][:, 0:8])
             for i in range(NT)]
    emit_transpose_seq(p, pairs, ident, tp2)

    if debug:
        @p.on("sync")
        def _(s):
            p.dma("sync", s, dbg_xn, xt[0][:])
            p.dma("sync", s, dbg_lamc, lamc[:])
            p.dma("sync", s, dbg_ab, p.n_ab[:])
            p.dma("sync", s, dbg_xnfm, xn_fm[0][:, 0:256])
            p.dma("sync", s, dbg_lg, lgt[0][:])
            p.dma("sync", s, dbg_rk, rk_tm[0][:])
        p.barrier()

    @p.on("vector")
    def _(v):
        for i in range(NT):
            for j, mt in ((0, m0t), (1, m1t)):
                v.tensor_tensor(out=scr[:, 0:8], in0=mt[i][:], in1=rk_tm[i][:],
                                op=ALU.mult)
                v.reduce_sum(out=scr[:, 8:9], in_=scr[:, 0:8], axis=AX.X)
                v.tensor_tensor(out=scr[:, 0:8], in0=mt[i][:], in1=econ[:],
                                op=ALU.mult)
                v.reduce_sum(out=scr[:, 9:10], in_=scr[:, 0:8], axis=AX.X)
                v.tensor_scalar(out=scr[:, 10:11], in0=scr[:, 9:10],
                                scalar1=float(CAPL), scalar2=None, op0=ALU.mult)
                v.tensor_tensor(out=scr[:, 10:11], in0=scr[:, 10:11],
                                in1=scr[:, 8:9], op=ALU.add)
                v.tensor_scalar(out=scr[:, 10:11], in0=scr[:, 10:11],
                                scalar1=-1.0, scalar2=None, op0=ALU.add)
                v.tensor_scalar(out=scr[:, 11:12], in0=scr[:, 8:9],
                                scalar1=float(CAPL) + 0.5, scalar2=None,
                                op0=ALU.is_gt)
                v.tensor_scalar(out=scr[:, 11:12], in0=scr[:, 11:12],
                                scalar1=float(10 ** 7), scalar2=None,
                                op0=ALU.mult)
                v.tensor_tensor(out=posf[i][:, j:j + 1], in0=scr[:, 10:11],
                                in1=scr[:, 11:12], op=ALU.add)
    p.barrier()

    ids_flat = ids_out.rearrange("e (c a) -> (e c) a", a=1)
    v0s = p._vcnt
    dbase2 = p.dma_cnt["gpsimd"]

    @p.on("vector")
    def _(v):
        for i in range(NT):
            for j in range(2):
                s = 2 * i + j
                if s >= 1:
                    v.wait_ge(p.ldsem["gpsimd"], 16 * (dbase2 + s))
                v.tensor_tensor(out=vals[:], in0=iotap[:], in1=tokb[:],
                                op=ALU.add)
                v.tensor_scalar(out=vals[:], in0=vals[:],
                                scalar1=float(128 * i), scalar2=None,
                                op0=ALU.add)
                v.tensor_copy(out=dstu[:], in_=posf[i][:, j:j + 1])

    @p.on("gpsimd")
    def _(g):
        for i in range(NT):
            for j in range(2):
                s = 2 * i + j
                g.wait_ge(p.vsem, v0s + 3 * (s + 1))
                p.idma(g, out=ids_flat, out_offset=IndirectOffsetOnAxis(
                    ap=dstu[:], axis=0), in_=vals[:], in_offset=None,
                    bounds_check=NE * CAPL - 1, oob_is_err=False)
    p.barrier()

    @p.on("sync")
    def _(s):
        for i in range(NT):
            p.dma("sync", s, pos_out[128 * i:128 * (i + 1), :], posf[i][:])
            p.dma("sync", s, g01_out[128 * i:128 * (i + 1), :], g01t[i][:])
    p.barrier()
    return p.finish()



def build_p2():
    p = Prog("p2")
    RH = RROWS // 2
    NTQ = RH // 512
    NRT = RH // 128

    w1T = p.inp("w1T", [D, I0])
    w3T = p.inp("w3T", [D, I0])
    w2T = p.inp("w2T", [I0, D])
    hfull = p.inp("hfull", [NCORE * T, D], BF16)
    myids = p.inp("myids", [128, RROWS // 128])
    ident_i = p.inp("ident", [128, 128])
    y_out = p.outp("y0", [RROWS, D], BF16)

    p.init_sems()
    ident = p.sb("id128", [128, 128], F32)
    idsb = p.sb("idsb", [128, RROWS // 128], F32)
    idsu = p.sb("idsu", [128, RROWS // 128], U32)
    hrow = [p.sb(f"hrow{i}", [128, D], BF16) for i in range(4)]
    h_fm = [p.sb(f"hfm{k}", [128, RH], BF16) for k in range(ND)]
    w1s = [p.sb(f"w1s{i}", [128, ND, 128], BF16) for i in range(2)]
    w3s = [p.sb(f"w3s{i}", [128, ND, 128], BF16) for i in range(2)]
    w2s = [p.sb(f"w2s{i}", [128, NI0, 128], BF16) for i in range(2)]
    asil = p.sb("asil", [128, 512], BF16)
    act = [p.sb(f"act{i}", [128, 512], BF16) for i in range(NI0)]
    y_fm = [p.sb(f"yfm{m}", [128, 512], BF16) for m in range(ND)]
    y_tm = [p.sb(f"ytm{i}", [128, D], BF16) for i in range(2)]
    aps = p.ps("aps", [128, 512])
    gps = p.ps("gps", [128, 512])
    eps2 = [p.ps(f"eps{i}", [128, 512]) for i in range(2)]
    tpp = [p.ps(f"tpp{i}", [128, 1024], BF16) for i in range(2)]
    p.bar_ps = p.ps("barps", [128, 512])
    p.bar_mm_src = ident

    w1b = p.nc.dram_tensor("w1b", [D, I0], BF16, kind="Internal").ap()
    w3b = p.nc.dram_tensor("w3b", [D, I0], BF16, kind="Internal").ap()
    w2b = p.nc.dram_tensor("w2b", [I0, D], BF16, kind="Internal").ap()

    @p.on("sync")
    def _(s):
        p.dma("sync", s, ident[:], ident_i)
        p.dma("sync", s, idsb[:], myids)

    @p.on("gpsimd")
    def _(g):
        p.dma("gpsimd", g, w1b, w1T)
        p.dma("gpsimd", g, w3b, w3T)
        p.dma("gpsimd", g, w2b, w2T)
    p.barrier()

    identb = p.sb("id128b", [128, 128], BF16)

    @p.on("vector")
    def _(v):
        v.tensor_scalar(out=idsb[:], in0=idsb[:], scalar1=float(NCORE * T - 1),
                        scalar2=None, op0=ALU.min)
        v.tensor_copy(out=idsu[:], in_=idsb[:])
        v.tensor_copy(out=identb[:], in_=ident[:])
    p.barrier()

    for half in range(2):
        s_t = p.sem(f"h{half}t")
        v0 = p._vcnt
        d0 = p.dma_cnt["gpsimd"]
        tick_ms = lambda r: v0 + 9 * r + 1
        tick_ev = lambda kk: v0 + 9 * (kk // 8) + (kk % 8) + 2

        @p.on("vector")
        def _(v):
            for r in range(NRT):
                if r >= 4:
                    v.wait_ge(s_t, 8 * (r - 3))
                v.memset(hrow[r % 4][:], 0.0)
                for k in range(ND):
                    kk = 8 * r + k
                    v.wait_ge(s_t, kk + 1)
                    v.tensor_copy(out=h_fm[k][:, 128 * r:128 * (r + 1)],
                                  in_=tpp[kk % 2][0:128, 0:128])

        sgl = [p.sem(f"sg{half}_{l}") for l in range(4)]

        @p.on("gpsimd")
        def _(g):
            for r in range(NRT):
                g.wait_ge(p.vsem, tick_ms(r))
                if r >= 4:
                    g.wait_ge(sgl[r % 4], 16 * (r // 4))
                p.idma_lane(g, sgl[r % 4], out=hrow[r % 4][:], out_offset=None,
                            in_=hfull,
                            in_offset=IndirectOffsetOnAxis(
                                ap=idsu[:, half * NRT + r:half * NRT + r + 1],
                                axis=0))

        @p.on("tensor")
        def _(t):
            for r in range(NRT):
                t.wait_ge(sgl[r % 4], 16 * (r // 4 + 1))
                for k in range(ND):
                    kk = 8 * r + k
                    if kk >= 2:
                        t.wait_ge(p.vsem, tick_ev(kk - 2))
                    t.transpose(tpp[kk % 2][0:128, 0:128],
                                hrow[r % 4][:, 128 * k:128 * (k + 1)],
                                identb[:]).then_inc(s_t, 1)
        p.barrier()

        for tq in range(NTQ):
            s_w = p.sem(f"w{half}_{tq}")
            s_mm = p.sem(f"mm{half}_{tq}")
            s_si = p.sem(f"si{half}_{tq}")
            d1 = p.dma_cnt["sync"]
            v1 = p._vcnt

            @p.on("sync")
            def _(s):
                for i in range(NI0):
                    buf = i % 2
                    if i >= 2:
                        s.wait_ge(s_mm, i - 1)
                    p.dma("sync", s, w1s[buf][:],
                          w1b.rearrange("(k pp) i -> pp k i", pp=128)
                          [:, :, 128 * i:128 * (i + 1)])
                    p.dma("sync", s, w3s[buf][:],
                          w3b.rearrange("(k pp) i -> pp k i", pp=128)
                          [:, :, 128 * i:128 * (i + 1)])

            @p.on("tensor")
            def _(t):
                for i in range(NI0):
                    buf = i % 2
                    t.wait_ge(p.ldsem["sync"], 16 * (d1 + 2 * (i + 1)))
                    if i >= 1:
                        t.wait_ge(s_si, i)
                        t.wait_ge(p.vsem, v1 + i)
                    for k in range(ND):
                        t.matmul(aps[:], w1s[buf][:, k, :],
                                 h_fm[k][:, 512 * tq:512 * (tq + 1)],
                                 start=(k == 0), stop=(k == ND - 1))
                    for k in range(ND):
                        mm = t.matmul(gps[:], w3s[buf][:, k, :],
                                      h_fm[k][:, 512 * tq:512 * (tq + 1)],
                                      start=(k == 0), stop=(k == ND - 1))
                    mm.then_inc(s_mm, 1)

            @p.on("scalar")
            def _(s):
                for i in range(NI0):
                    s.wait_ge(s_mm, i + 1)
                    if i >= 1:
                        s.wait_ge(p.vsem, v1 + i)
                    s.activation(out=asil[:], in_=aps[:],
                                 func=AF.Silu).then_inc(s_si, 1)

            @p.on("vector")
            def _(v):
                for i in range(NI0):
                    v.wait_ge(s_si, i + 1)
                    v.tensor_tensor(out=act[i][:], in0=asil[:], in1=gps[:],
                                    op=ALU.mult)
            p.barrier()

            s_eo = p.sem(f"eo{half}_{tq}")
            d2 = p.dma_cnt["sync"]
            v2 = p._vcnt

            @p.on("sync")
            def _(s):
                for m in range(ND):
                    buf = m % 2
                    if m >= 2:
                        s.wait_ge(s_eo, m - 1)
                    p.dma("sync", s, w2s[buf][:],
                          w2b.rearrange("(ii pp) dd -> pp ii dd", pp=128)
                          [:, :, 128 * m:128 * (m + 1)])

            @p.on("tensor")
            def _(t):
                for m in range(ND):
                    buf = m % 2
                    t.wait_ge(p.ldsem["sync"], 16 * (d2 + m + 1))
                    if m >= 2:
                        t.wait_ge(p.vsem, v2 + m - 1)
                    for i in range(NI0):
                        mm = t.matmul(eps2[m % 2][:], w2s[buf][:, i, :],
                                      act[i][:],
                                      start=(i == 0), stop=(i == NI0 - 1))
                    mm.then_inc(s_eo, 1)

            @p.on("vector")
            def _(v):
                for m in range(ND):
                    v.wait_ge(s_eo, m + 1)
                    v.tensor_copy(out=y_fm[m][:], in_=eps2[m % 2][:])
            p.barrier()

            s_yt = p.sem(f"yt{half}_{tq}")
            v3 = p._vcnt
            d3 = p.dma_cnt["gpsimd"]

            @p.on("tensor")
            def _(t):
                for rr in range(4):
                    for m in range(ND):
                        kk = 8 * rr + m
                        if kk >= 2:
                            t.wait_ge(p.vsem, v3 + kk - 1)
                        t.transpose(tpp[kk % 2][0:128, 0:128],
                                    y_fm[m][:, 128 * rr:128 * (rr + 1)],
                                    identb[:]).then_inc(s_yt, 1)

            @p.on("vector")
            def _(v):
                for rr in range(4):
                    if rr >= 2:
                        v.wait_ge(p.ldsem["gpsimd"], 16 * (d3 + rr - 1))
                    for m in range(ND):
                        kk = 8 * rr + m
                        v.wait_ge(s_yt, kk + 1)
                        v.tensor_copy(out=y_tm[rr % 2][:, 128 * m:128 * (m + 1)],
                                      in_=tpp[kk % 2][0:128, 0:128])

            @p.on("gpsimd")
            def _(g):
                for rr in range(4):
                    g.wait_ge(p.vsem, v3 + 8 * (rr + 1))
                    row0 = half * RH + 512 * tq + 128 * rr
                    p.dma("gpsimd", g, y_out[row0:row0 + 128, :],
                          y_tm[rr % 2][:])
            p.barrier()
    return p.finish()



def build_p3():
    p = Prog("p3")
    x = p.inp("x", [T, D])
    uends = p.inp("uends", [128, NE, ND])
    cec = p.inp("cec", [128, NE])
    cmask = p.inp("cmask", [128, NE])
    lam_log = p.inp("lam_log", [1, H])
    qvec = p.inp("qvec", [H, dh])
    kvec = p.inp("kvec", [H, dh])
    nw_row = p.inp("nw_row", [1, D])
    o_wT = p.inp("o_wT", [D, D])
    y0own = p.inp("y0own", [RROWS, D], BF16)
    pos0 = p.inp("pos0", [T, 2])
    g01_0 = p.inp("g01_0", [T, 2])
    wg = p.inp("wg", [D, NE])
    ltmask_i = p.inp("ltmask", [128, 64])
    ident_i = p.inp("ident", [128, 128])
    tokbase = p.inp("tokbase", [128, 1])
    asel_i = p.inp("asel", [16, 128])
    bsel_i = p.inp("bsel", [16, 8])

    s0g0_out = p.outp("s0g0", [T, D], BF16)
    xn1_out = p.outp("xn1", [T, D], BF16)
    ids1_out = p.outp("ids1", [NE, CAPL])
    pos1_out = p.outp("pos1", [T, 2])
    idx1_out = p.outp("idx1", [T, 2])
    locj_out = p.outp("locj", [T, 2])
    g011_out = p.outp("g011", [T, 2])
    cnts_out = p.outp("cnts", [2, NE])

    p.init_sems()
    xs = [p.sb(f"xs{i}", [128, D], F32) for i in range(3)]
    xn0b = [p.sb(f"xn0b{i}", [128, D], BF16) for i in range(NT)]
    s0p = [p.sb(f"s0p{i}", [128, D], BF16) for i in range(NT)]
    otb = [p.sb(f"otb{i}", [128, D], BF16) for i in range(NT)]
    fmb = [p.sb(f"fmb{k}", [128, T], BF16) for k in range(ND)]
    p8 = [p.sb(f"p8_{i}", [128, T], F32) for i in range(2)]
    xnfb = p.sb("xnfb", [128, T], BF16)
    lamc = p.sb("lamc", [128, ND], F32)
    cpp = p.sb("cpp", [128, ND], F32)
    carry = p.sb("carry", [128, ND], F32)
    loglam = p.sb("loglam", [128, ND], F32)
    tmp8 = p.sb("tmp8", [128, ND], F32)
    uendt = p.sb("uendt", [128, NE, ND], F32)
    cect = p.sb("cect", [128, NE], F32)
    cmaskt = p.sb("cmaskt", [128, NE], F32)
    qkt = p.sb("qkt", [16, 64], F32)
    qk2 = p.sb("qk2", [16, 64], F32)
    ccol = p.sb("ccol", [16, 16], F32)
    nwb = p.sb("nwb", [128, D], F32)
    nwrow_s = p.sb("nwrow_s", [1, D], F32)
    onescol = p.sb("onescol", [1, 128], F32)
    yrow = [p.sb(f"yrow{i}", [128, D], BF16) for i in range(2)]
    posb = p.sb("posb", [128, NT, 2], F32)
    posu = p.sb("posu", [128, NT, 2], U32)
    g01b = p.sb("g01b", [128, NT, 2], F32)
    wgs = [p.sb(f"wgs{k}", [128, 8], F32) for k in range(ND)]
    wgsb = [p.sb(f"wgsb{k}", [128, 8], BF16) for k in range(ND)]
    lgt = [p.sb(f"lgt{i}", [128, 8], F32) for i in range(NT)]
    sct = [p.sb(f"sct{i}", [128, 8], F32) for i in range(NT)]
    m0t = [p.sb(f"m0t{i}", [128, 8], F32) for i in range(NT)]
    m1t = [p.sb(f"m1t{i}", [128, 8], F32) for i in range(NT)]
    g01t = [p.sb(f"g01t{i}", [128, 2], F32) for i in range(NT)]
    rk_tm = [p.sb(f"rktm{i}", [128, 8], F32) for i in range(NT)]
    rkj_tm = [p.sb(f"rkj{i}", [128, 16], F32) for i in range(NT)]
    ltmask = p.sb("ltm", [128, 64], F32)
    ident = p.sb("id128", [128, 128], F32)
    identb = p.sb("id128b", [128, 128], BF16)
    p._id16 = ident
    lamrow = p.sb("lamrow", [1, H], F32)
    asel = p.sb("asel_s", [16, 128], F32)
    bsel = p.sb("bsel_s", [16, 8], F32)
    scr16 = p.sb("scr16", [16, 16], F32)
    z81 = p.sb("z81", [128, 1], F32)
    tokb = p.sb("tokb", [128, 1], F32)
    iotap = p.sb("iotap", [128, 1], F32)
    econ = p.sb("econ", [128, 8], F32)
    scr = p.sb("scrA", [128, 16], F32)
    dstu = p.sb("dstu", [128, 1], U32)
    vals = p.sb("vals", [128, 1], F32)
    posf1 = [p.sb(f"posf1_{i}", [128, 2], F32) for i in range(NT)]
    locjf = [p.sb(f"locjf{i}", [128, 2], F32) for i in range(NT)]
    idxf = [p.sb(f"idxf{i}", [128, 2], F32) for i in range(NT)]
    oobfill = p.sb("oobfill", [128, 64], F32)
    tp2 = [p.ps(f"tp{i}", [128, 128]) for i in range(2)]
    tpb = [p.ps(f"tpb{i}", [128, 128], BF16) for i in range(2)]
    lgps = p.ps("lgps", [128, 8])
    mmps = [p.ps(f"mmps{i}", [128, 512]) for i in range(2)]
    php = mmps[0]
    p.bar_ps = p.ps("barps", [128, 512])
    p.bar_mm_src = ident
    alloc_norm_scratch(p, NT)
    alloc_router_scratch(p, NT)

    @p.on("sync")
    def _(s):
        p.dma("sync", s, ltmask[:], ltmask_i)
        p.dma("sync", s, ident[:], ident_i)
        p.dma("sync", s, lamrow[:], lam_log)
        p.dma("sync", s, tokb[:], tokbase)
        p.dma("sync", s, asel[:], asel_i)
        p.dma("sync", s, bsel[:], bsel_i)
        p.dma("sync", s, uendt[:], uends)
        p.dma("sync", s, cect[:], cec)
        p.dma("sync", s, cmaskt[:], cmask)
        p.dma("sync", s, qkt[:], qvec)
        p.dma("sync", s, qk2[:], kvec)
        p.dma("sync", s, nwrow_s[:], nw_row)
        p.dma("sync", s, posb[:], pos0.rearrange("(i pp) j -> pp i j", pp=128))
        p.dma("sync", s, g01b[:], g01_0.rearrange("(i pp) j -> pp i j", pp=128))
        for k in range(ND):
            p.dma("sync", s, wgs[k][:], wg[128 * k:128 * (k + 1), :])

    @p.on("gpsimd")
    def _(g):
        g.iota(iotap[:], pattern=[[0, 1]], base=0, channel_multiplier=1,
               allow_small_or_imprecise_dtypes=True)
        g.iota(econ[:], pattern=[[1, 8]], base=0, channel_multiplier=0,
               allow_small_or_imprecise_dtypes=True)
        g.memset(oobfill[:], float(OOBID))
        g.memset(z81[:], 0.0)
    p.barrier()

    @p.on("gpsimd")
    def _(g):
        p.dma("gpsimd", g,
              ids1_out.rearrange("e (r q) -> (e r) q", q=64), oobfill[:])

    @p.on("vector")
    def _(v):
        v.memset(onescol[:], 1.0)
        v.tensor_copy(out=identb[:], in_=ident[:])
        v.tensor_copy(out=posu[:], in_=posb[:])
    p.barrier()

    @p.on("tensor")
    def _(t):
        for hf in range(2):
            t.matmul(mmps[hf][:], onescol[:],
                     nwrow_s[:, 512 * hf:512 * (hf + 1)], start=True, stop=True)
    p.barrier()

    @p.on("vector")
    def _(v):
        for hf in range(2):
            v.tensor_copy(out=nwb[:, 512 * hf:512 * (hf + 1)], in_=mmps[hf][:])
    p.barrier()

    @p.on("scalar")
    def _(s):
        s.activation(out=lamrow[:], in_=lamrow[:], func=AF.Sigmoid)
    p.barrier()
    emit_perhead_to_chan(p, lamrow, lamc, asel, bsel, tp2[0], php, scr16)

    @p.on("vector")
    def _(v):
        v.tensor_tensor(out=qk2[:], in0=qkt[:], in1=qk2[:], op=ALU.mult)
        v.reduce_sum(out=ccol[:, 0:1], in_=qk2[:], axis=AX.X)
    p.barrier()

    @p.on("tensor")
    def _(t):
        t.transpose(tp2[0][0:1, 0:16], ccol[:, 0:1], ident[0:16, 0:16])
    p.barrier()

    @p.on("vector")
    def _(v):
        v.tensor_copy(out=ccol[0:1, 0:16], in_=tp2[0][0:1, 0:16])
    p.barrier()
    emit_perhead_to_chan(p, ccol[0:1, 0:16], cpp, asel, bsel, tp2[0], php,
                         scr16)

    @p.on("scalar")
    def _(s):
        s.activation(out=loglam[:], in_=lamc[:], func=AF.Ln)
    p.barrier()

    @p.on("vector")
    def _(v):
        v.memset(carry[:], 0.0)
    p.barrier()
    for j in range(NE):
        @p.on("scalar")
        def _(s, j=j):
            s.activation(out=tmp8[:], in_=loglam[:], func=AF.Exp,
                         scale=cect[:, j:j + 1])
        p.barrier()

        @p.on("vector")
        def _(v, j=j):
            v.tensor_scalar(out=tmp8[:], in0=tmp8[:],
                            scalar1=cmaskt[:, j:j + 1], scalar2=None,
                            op0=ALU.mult)
            v.tensor_tensor(out=tmp8[:], in0=tmp8[:], in1=uendt[:, j, :],
                            op=ALU.mult)
            v.tensor_tensor(out=carry[:], in0=carry[:], in1=tmp8[:],
                            op=ALU.add)
        p.barrier()

    def stream_stats(src_dram, with_mean, eps, extra=None):
        d0 = p.dma_cnt["sync"]
        s_used = p.sem(f"su{p.nbar}")

        @p.on("sync")
        def _(s):
            for i in range(NT):
                if i >= 2:
                    s.wait_ge(s_used, i - 1)
                p.dma("sync", s, xs[i % 2][:],
                      src_dram[128 * i:128 * (i + 1), :])

        @p.on("scalar")
        def _(s):
            for i in range(NT):
                s.wait_ge(p.ldsem["sync"], 16 * (d0 + i + 1))
                if extra is not None:
                    extra(i, "scalar", s)
                s.activation(out=p.n_sq[:], in_=(xs[i % 2][:] if extra is None
                                                else xs[2][:]),
                             func=AF.Square,
                             accum_out=p.n_q[:, i:i + 1]).then_inc(s_used, 1)

        if with_mean:
            @p.on("vector")
            def _(v):
                for i in range(NT):
                    v.wait_ge(s_used, i + 1)
                    v.reduce_sum(out=p.n_s[:, i:i + 1],
                                 in_=(xs[i % 2][:] if extra is None
                                      else xs[2][:]), axis=AX.X)
        p.barrier()
        emit_stats_rows(p, NT, eps, with_mean)

    def emit_stats_rows(p, nt, eps, with_mean):
        @p.on("vector")
        def _(v):
            inv = 1.0 / D
            if with_mean:
                v.tensor_scalar(out=p.n_mu[:, 0:nt], in0=p.n_s[:, 0:nt],
                                scalar1=inv, scalar2=None, op0=ALU.mult)
                v.tensor_scalar(out=p.n_var[:, 0:nt], in0=p.n_q[:, 0:nt],
                                scalar1=inv, scalar2=None, op0=ALU.mult)
                v.tensor_tensor(out=p.n_s[:, 0:nt], in0=p.n_mu[:, 0:nt],
                                in1=p.n_mu[:, 0:nt], op=ALU.mult)
                v.tensor_tensor(out=p.n_var[:, 0:nt], in0=p.n_var[:, 0:nt],
                                in1=p.n_s[:, 0:nt], op=ALU.subtract)
            else:
                v.tensor_scalar(out=p.n_var[:, 0:nt], in0=p.n_q[:, 0:nt],
                                scalar1=inv, scalar2=None, op0=ALU.mult)
            v.tensor_scalar(out=p.n_var[:, 0:nt], in0=p.n_var[:, 0:nt],
                            scalar1=eps, scalar2=None, op0=ALU.add)
        p.barrier()

        @p.on("scalar")
        def _(s):
            s.activation(out=p.n_var[:, 0:nt], in_=p.n_var[:, 0:nt],
                         func=AF.Sqrt)
        p.barrier()

        @p.on("vector")
        def _(v):
            v.reciprocal(out=p.n_var[:, 0:nt], in_=p.n_var[:, 0:nt])
            for i in range(nt):
                v.tensor_copy(out=p.n_ab[:, 2 * i:2 * i + 1],
                              in_=p.n_var[:, i:i + 1])
                if with_mean:
                    v.tensor_tensor(out=p.n_ab[:, 2 * i + 1:2 * i + 2],
                                    in0=p.n_mu[:, i:i + 1],
                                    in1=p.n_var[:, i:i + 1], op=ALU.mult)
                    v.tensor_scalar(out=p.n_ab[:, 2 * i + 1:2 * i + 2],
                                    in0=p.n_ab[:, 2 * i + 1:2 * i + 2],
                                    scalar1=-1.0, scalar2=None, op0=ALU.mult)
                else:
                    v.memset(p.n_ab[:, 2 * i + 1:2 * i + 2], 0.0)
        p.barrier()

    stream_stats(x, True, EPS_LN)

    d0 = p.dma_cnt["sync"]
    s_ap = p.sem("s_ap0")

    @p.on("sync")
    def _(s):
        for i in range(NT):
            if i >= 2:
                s.wait_ge(s_ap, i - 1)
            p.dma("sync", s, xs[i % 2][:], x[128 * i:128 * (i + 1), :])

    @p.on("scalar")
    def _(s):
        for i in range(NT):
            s.wait_ge(p.ldsem["sync"], 16 * (d0 + i + 1))
            s.activation(out=xn0b[i][:], in_=xs[i % 2][:], func=AF.Identity,
                         scale=p.n_ab[:, 2 * i:2 * i + 1],
                         bias=p.n_ab[:, 2 * i + 1:2 * i + 2]).then_inc(s_ap, 1)
    p.barrier()

    for k in range(ND):
        pairs = [(xn0b[i][:, 128 * k:128 * (k + 1)],
                  xnfb[:, 128 * i:128 * (i + 1)]) for i in range(NT)]
        emit_transpose_seq(p, pairs, identb, tpb)

        @p.on("vector")
        def _(v, k=k):
            v.tensor_tensor_scan(out=p8[0][:],
                                 data0=lamc[:, k:k + 1].broadcast_to([128, T]),
                                 data1=xnfb[:], initial=carry[:, k:k + 1],
                                 op0=ALU.mult, op1=ALU.add)
            v.tensor_scalar(out=p8[0][:], in0=p8[0][:],
                            scalar1=cpp[:, k:k + 1], scalar2=None,
                            op0=ALU.mult)
        p.barrier()
        pairs = [(p8[0][:, 128 * i:128 * (i + 1)],
                  otb[i][:, 128 * k:128 * (k + 1)]) for i in range(NT)]
        emit_transpose_seq(p, pairs, ident, tp2)

    @p.on("scalar")
    def _(s):
        for i in range(NT):
            s.activation(out=p.n_sq[:], in_=otb[i][:], func=AF.Square,
                         accum_out=p.n_q[:, i:i + 1])
    p.barrier()
    emit_stats_rows(p, NT, EPS_RMS, False)

    @p.on("vector")
    def _(v):
        for i in range(NT):
            v.scalar_tensor_tensor(out=otb[i][:], in0=otb[i][:],
                                   scalar=p.n_ab[:, 2 * i:2 * i + 1],
                                   in1=nwb[:], op0=ALU.mult, op1=ALU.mult)
    p.barrier()

    pairs = []
    for i in range(NT):
        for k in range(ND):
            pairs.append((otb[i][:, 128 * k:128 * (k + 1)],
                          fmb[k][:, 128 * i:128 * (i + 1)]))
    emit_transpose_seq(p, pairs, identb, tpb)

    s_ow = p.sem("s_ow")
    d_ow = p.dma_cnt["gpsimd"]
    v_ow = p._vcnt
    owts8 = p.sb("owts8", [128, ND, D], BF16)

    @p.on("gpsimd")
    def _(g):
        for k in range(ND):
            p.dma("gpsimd", g, owts8[:, k, :].rearrange(
                "pp (db dd) -> pp db dd", dd=128),
                o_wT[128 * k:128 * (k + 1), :]
                .rearrange("pp (db dd) -> pp db dd", dd=128))
    p.barrier()

    @p.on("tensor")
    def _(t):
        for i in range(NT):
            for dh2 in range(2):
                s = 2 * i + dh2
                if s >= 2:
                    t.wait_ge(p.vsem, v_ow + s - 1)
                for k in range(ND):
                    mm = t.matmul(mmps[s % 2][:],
                                  fmb[k][:, 128 * i:128 * (i + 1)],
                                  owts8[:, k, 512 * dh2:512 * (dh2 + 1)],
                                  start=(k == 0), stop=(k == ND - 1))
                mm.then_inc(s_ow, 1)

    @p.on("vector")
    def _(v):
        for i in range(NT):
            for dh2 in range(2):
                v.wait_ge(s_ow, 2 * i + dh2 + 1)
                v.tensor_copy(out=s0p[i][:, 512 * dh2:512 * (dh2 + 1)],
                              in_=mmps[(2 * i + dh2) % 2][:])
    p.barrier()
    return _p3_part2(p, locals())


def _p3_part2(p, L):
    (x, xs, xn0b, s0p, otb, fmb, yrow, posu, g01b) = (
        L["x"], L["xs"], L["xn0b"], L["s0p"], L["otb"], L["fmb"],
        L["yrow"], L["posu"], L["g01b"])
    y0own = L["y0own"]
    (wgs, lgt, sct, m0t, m1t, g01t, rk_tm, rkj_tm) = (
        L["wgs"], L["lgt"], L["sct"], L["m0t"], L["m1t"], L["g01t"],
        L["rk_tm"], L["rkj_tm"])
    (ltmask, ident, identb, z81, tokb, iotap, econ, scr) = (
        L["ltmask"], L["ident"], L["identb"], L["z81"], L["tokb"],
        L["iotap"], L["econ"], L["scr"])
    (dstu, vals, posf1, locjf, idxf, p8) = (
        L["dstu"], L["vals"], L["posf1"], L["locjf"], L["idxf"], L["p8"])
    tp2, tpb, lgps = L["tp2"], L["tpb"], L["lgps"]
    (s0g0_out, xn1_out, ids1_out, pos1_out, idx1_out, locj_out, g011_out,
     cnts_out) = (L["s0g0_out"], L["xn1_out"], L["ids1_out"], L["pos1_out"],
                  L["idx1_out"], L["locj_out"], L["g011_out"], L["cnts_out"])
    emit_stats_rows = L["emit_stats_rows"]

    d_y = p.dma_cnt["gpsimd"]
    d_x = p.dma_cnt["sync"]
    v_y = p._vcnt
    s_f = p.sem("s_f")
    s_x1 = p.sem("s_x1")

    syl = [p.sem(f"syl{l}") for l in range(2)]

    @p.on("gpsimd")
    def _(gg):
        for i in range(NT):
            for j in range(2):
                s = 2 * i + j
                if s >= 2:
                    gg.wait_ge(p.vsem, v_y + 8 * (i - 1) + 2 + 2 * j)
                    gg.wait_ge(syl[j], 16 * i)
                p.idma_lane(gg, syl[j], out=yrow[s % 2][:], out_offset=None,
                            in_=y0own,
                            in_offset=IndirectOffsetOnAxis(
                                ap=posu[:, i, j:j + 1], axis=0))

    @p.on("sync")
    def _(s):
        for i in range(NT):
            if i >= 2:
                s.wait_ge(p.vsem, v_y + 8 * (i - 1))
                s.wait_ge(s_x1, i - 1)
            p.dma("sync", s, xs[i % 2][:], x[128 * i:128 * (i + 1), :])

    sqd = p.sb("sqdmy", [128, D], BF16)

    @p.on("vector")
    def _(v):
        for i in range(NT):
            v.wait_ge(syl[0], 16 * (i + 1))
            v.tensor_copy(out=xs[2][:], in_=xn0b[i][:])
            v.tensor_scalar(out=p.n_sq[:], in0=yrow[0][:],
                            scalar1=g01b[:, i, 0:1], scalar2=None,
                            op0=ALU.mult)
            v.tensor_tensor(out=xs[2][:], in0=xs[2][:], in1=p.n_sq[:],
                            op=ALU.add)
            v.wait_ge(syl[1], 16 * (i + 1))
            v.tensor_scalar(out=p.n_sq[:], in0=yrow[1][:],
                            scalar1=g01b[:, i, 1:2], scalar2=None,
                            op0=ALU.mult)
            v.tensor_tensor(out=xs[2][:], in0=xs[2][:], in1=p.n_sq[:],
                            op=ALU.add)
            v.wait_ge(s_f, i + 1)
            v.tensor_tensor(out=s0p[i][:], in0=s0p[i][:], in1=xs[2][:],
                            op=ALU.mult)
            v.wait_ge(p.ldsem["sync"], 16 * (d_x + i + 1))
            v.tensor_tensor(out=xs[i % 2][:], in0=xs[i % 2][:],
                            in1=s0p[i][:], op=ALU.add)
            v.reduce_sum(out=p.n_s[:, i:i + 1], in_=xs[i % 2][:], axis=AX.X)

    @p.on("scalar")
    def _(s):
        for i in range(NT):
            s.wait_ge(p.vsem, v_y + 8 * i + 5)
            s.activation(out=xs[2][:], in_=xs[2][:],
                         func=AF.Silu).then_inc(s_f, 1)
            s.wait_ge(p.vsem, v_y + 8 * i + 7)
            s.activation(out=sqd[:], in_=xs[i % 2][:], func=AF.Square,
                         accum_out=p.n_q[:, i:i + 1]).then_inc(s_x1, 1)
    p.barrier()

    @p.on("gpsimd")
    def _(gg):
        for i in range(NT):
            p.dma("gpsimd", gg, s0g0_out[128 * i:128 * (i + 1), :], s0p[i][:])

    emit_stats_rows(p, NT, EPS_LN, True)

    d0 = p.dma_cnt["sync"]
    s_ap = p.sem("s_ap1")
    v_a = p._vcnt

    @p.on("sync")
    def _(s):
        for i in range(NT):
            if i >= 2:
                s.wait_ge(s_ap, i - 1)
            p.dma("sync", s, xs[i % 2][:], x[128 * i:128 * (i + 1), :])

    @p.on("vector")
    def _(v):
        for i in range(NT):
            v.wait_ge(p.ldsem["sync"], 16 * (d0 + i + 1))
            v.tensor_tensor(out=xs[i % 2][:], in0=xs[i % 2][:],
                            in1=s0p[i][:], op=ALU.add)

    @p.on("scalar")
    def _(s):
        for i in range(NT):
            s.wait_ge(p.vsem, v_a + i + 1)
            s.activation(out=otb[i][:], in_=xs[i % 2][:], func=AF.Identity,
                         scale=p.n_ab[:, 2 * i:2 * i + 1],
                         bias=p.n_ab[:, 2 * i + 1:2 * i + 2]).then_inc(s_ap, 1)
    p.barrier()

    @p.on("gpsimd")
    def _(gg):
        for i in range(NT):
            p.dma("gpsimd", gg, xn1_out[128 * i:128 * (i + 1), :], otb[i][:])

    pairs = []
    for i in range(NT):
        for k in range(ND):
            pairs.append((otb[i][:, 128 * k:128 * (k + 1)],
                          fmb[k][:, 128 * i:128 * (i + 1)]))
    emit_transpose_seq(p, pairs, identb, tpb)

    wgsb = L["wgsb"]

    @p.on("vector")
    def _(v):
        for k in range(ND):
            v.tensor_copy(out=wgsb[k][:], in_=wgs[k][:])
    p.barrier()
    s_lg = p.sem("s_lg")
    v_lg = p._vcnt

    @p.on("tensor")
    def _(t):
        for i in range(NT):
            if i >= 1:
                t.wait_ge(p.vsem, v_lg + i)
            for k in range(ND):
                mm = t.matmul(lgps[:, 0:8], fmb[k][:, 128 * i:128 * (i + 1)],
                              wgsb[k][:], start=(k == 0), stop=(k == ND - 1))
            mm.then_inc(s_lg, 1)

    @p.on("vector")
    def _(v):
        for i in range(NT):
            v.wait_ge(s_lg, i + 1)
            v.tensor_copy(out=lgt[i][:], in_=lgps[:, 0:8])
    p.barrier()

    emit_router(p, lgt, sct, m0t, m1t, g01t, ltmask, NT, clip_eps=EPS_G)

    @p.on("vector")
    def _(v):
        for i in range(NT):
            v.tensor_tensor(out=rk_tm[i][:], in0=m0t[i][:], in1=m1t[i][:],
                            op=ALU.add)
            v.tensor_copy(out=rkj_tm[i][:, 0:8], in_=m0t[i][:])
            v.tensor_copy(out=rkj_tm[i][:, 8:16], in_=m1t[i][:])
    p.barrier()
    pairs = [(rk_tm[i][:, 0:8], p8[0][0:8, 128 * i:128 * (i + 1)])
             for i in range(NT)]
    emit_transpose_seq(p, pairs, ident, tp2)
    pairs = [(rkj_tm[i][:, 0:16], p8[1][0:16, 128 * i:128 * (i + 1)])
             for i in range(NT)]
    emit_transpose_seq(p, pairs, ident, tp2)

    rks8 = p8[0][32:40, 0:T]
    rks16 = p8[1][32:48, 0:T]

    @p.on("vector")
    def _(v):
        v.tensor_tensor_scan(out=rks8,
                             data0=z81[0:8, 0:1].broadcast_to([8, T]),
                             data1=p8[0][0:8, 0:T], initial=0.0,
                             op0=ALU.add, op1=ALU.add)
        v.tensor_tensor_scan(out=rks16,
                             data0=z81[0:16, 0:1].broadcast_to([16, T]),
                             data1=p8[1][0:16, 0:T], initial=0.0,
                             op0=ALU.add, op1=ALU.add)
    p.barrier()
    pairs = [(p8[0][32:40, 128 * i:128 * (i + 1)], rk_tm[i][:, 0:8])
             for i in range(NT)]
    emit_transpose_seq(p, pairs, ident, tp2)
    pairs = [(p8[1][32:48, 128 * i:128 * (i + 1)], rkj_tm[i][:, 0:16])
             for i in range(NT)]
    emit_transpose_seq(p, pairs, ident, tp2)

    @p.on("gpsimd")
    def _(gg):
        p.dma("gpsimd", gg, cnts_out.rearrange("j e -> (j e)")
              .rearrange("(q a) -> q a", a=1), p8[1][32:48, T - 1:T])

    @p.on("vector")
    def _(v):
        for i in range(NT):
            for j, mt in ((0, m0t), (1, m1t)):
                v.tensor_tensor(out=scr[:, 0:8], in0=mt[i][:], in1=rk_tm[i][:],
                                op=ALU.mult)
                v.reduce_sum(out=scr[:, 8:9], in_=scr[:, 0:8], axis=AX.X)
                v.tensor_tensor(out=scr[:, 0:8], in0=mt[i][:], in1=econ[:],
                                op=ALU.mult)
                v.reduce_sum(out=idxf[i][:, j:j + 1], in_=scr[:, 0:8],
                             axis=AX.X)
                v.tensor_scalar(out=scr[:, 10:11], in0=idxf[i][:, j:j + 1],
                                scalar1=float(CAPL), scalar2=None,
                                op0=ALU.mult)
                v.tensor_tensor(out=scr[:, 10:11], in0=scr[:, 10:11],
                                in1=scr[:, 8:9], op=ALU.add)
                v.tensor_scalar(out=posf1[i][:, j:j + 1], in0=scr[:, 10:11],
                                scalar1=-1.0, scalar2=None, op0=ALU.add)
                v.tensor_tensor(out=scr[:, 0:8], in0=mt[i][:],
                                in1=rkj_tm[i][:, 8 * j:8 * (j + 1)],
                                op=ALU.mult)
                v.reduce_sum(out=locjf[i][:, j:j + 1], in_=scr[:, 0:8],
                             axis=AX.X)
    p.barrier()

    v0s = p._vcnt
    dbase2 = p.dma_cnt["gpsimd"]
    ids_flat = ids1_out.rearrange("e (c a) -> (e c) a", a=1)

    @p.on("vector")
    def _(v):
        for i in range(NT):
            for j in range(2):
                s = 2 * i + j
                if s >= 1:
                    v.wait_ge(p.ldsem["gpsimd"], 16 * (dbase2 + s))
                v.tensor_tensor(out=vals[:], in0=iotap[:], in1=tokb[:],
                                op=ALU.add)
                v.tensor_scalar(out=vals[:], in0=vals[:],
                                scalar1=float(128 * i), scalar2=None,
                                op0=ALU.add)
                v.tensor_copy(out=dstu[:], in_=posf1[i][:, j:j + 1])

    @p.on("gpsimd")
    def _(gg):
        for i in range(NT):
            for j in range(2):
                s = 2 * i + j
                gg.wait_ge(p.vsem, v0s + 3 * (s + 1))
                p.idma(gg, out=ids_flat, out_offset=IndirectOffsetOnAxis(
                    ap=dstu[:], axis=0), in_=vals[:], in_offset=None,
                    bounds_check=NE * CAPL - 1, oob_is_err=False)
    p.barrier()

    @p.on("sync")
    def _(s):
        for i in range(NT):
            p.dma("sync", s, pos1_out[128 * i:128 * (i + 1), :], posf1[i][:])
            p.dma("sync", s, idx1_out[128 * i:128 * (i + 1), :], idxf[i][:])
            p.dma("sync", s, locj_out[128 * i:128 * (i + 1), :], locjf[i][:])
            p.dma("sync", s, g011_out[128 * i:128 * (i + 1), :], g01t[i][:])
    p.barrier()
    return p.finish()



def build_p45():
    p = Prog("p45")
    TB = N
    NQC = TB // 512
    NKT = TB // 128
    NH = 4

    xn1_in = [p.inp(f"xn1_{c}", [T, D], BF16) for c in range(NCORE)]
    myids1 = p.inp("myids1", [128, RROWS // 128])
    fc1T = p.inp("fc1T", [D, H1])
    fc2n = p.inp("fc2n", [H1, D])
    fc1b_pre = p.inp("fc1b_pre", [128, NH1])
    fc2b_pre = p.inp("fc2b_pre", [128, ND])
    wq_sl = p.inp("wq_sl", [D, 256])
    wk_sl = p.inp("wk_sl", [D, 256])
    wv_sl = p.inp("wv_sl", [D, 256])
    wgd_sl = p.inp("wgd_sl", [D, 256])
    bq_pre = p.inp("bq_pre", [128, 2])
    bk_pre = p.inp("bk_pre", [128, 2])
    bv_row = p.inp("bv_row", [1, 256])
    bg_pre = p.inp("bg_pre", [128, 2])
    w1_sl = p.inp("w1_sl", [D, 1024])
    b1_pre = p.inp("b1_pre", [128, 8])
    w2_sl = p.inp("w2_sl", [1024, 16])
    b2_row = p.inp("b2_row", [1, 16])
    r5 = p.inp("r5", [5, NH, 128, 128])
    tconsts = p.inp("tconsts", [1, 2 * NH])
    boff_i = p.inp("boff", [128, 1])
    ident_i = p.inp("ident", [128, 128])

    y1_out = p.outp("y1", [RROWS, D], BF16)
    numg_out = p.outp("numg", [256, TB], BF16)
    gate_out = p.outp("gate", [256, TB], BF16)
    den_out = p.outp("den", [NH, TB])
    edgep_out = p.outp("edgep", [16, TB])

    p.init_sems()
    xn1full = p.nc.dram_tensor("xn1full", [NCORE * T, D], BF16,
                               kind="Internal").ap()
    fc1bf = p.nc.dram_tensor("fc1bf", [D, H1], BF16, kind="Internal").ap()
    fc2bf = p.nc.dram_tensor("fc2bf", [H1, D], BF16, kind="Internal").ap()

    G = [p.sb(f"G{i}", [128, 8192], BF16) for i in range(6)]
    Vt = p.sb("Vt", [128, NKT * 4 * 65], BF16)
    ident = p.sb("id128", [128, 128], F32)
    identb = p.sb("id128b", [128, 128], BF16)
    idsb = p.sb("idsb", [128, RROWS // 128], F32)
    idsu = p.sb("idsu", [128, RROWS // 128], U32)
    hrow = [p.sb(f"hrow{i}", [128, D], BF16) for i in range(2)]
    w1s = [p.sb(f"w1s{i}", [128, ND, 128], BF16) for i in range(2)]
    w2s1 = p.sb("w2s1", [128, NH1, 128], BF16)
    w2s = [w2s1, w2s1]
    fc1bt = p.sb("fc1bt", [128, NH1], F32)
    fc2bt = p.sb("fc2bt", [128, ND], F32)
    vmask = p.sb("vmask", [128, RROWS // 128], F32)
    boffc = p.sb("boffc", [128, 1], F32)
    tns = p.sb("tns", [128, RROWS // 128], F32)
    iotap = p.sb("iotap", [128, 1], F32)
    xtm = [p.sb(f"xtm{i}", [128, D], BF16) for i in range(4)]
    xfm = [p.sb(f"xfm{k}", [128, 512], BF16) for k in range(ND)]
    y_fm = xfm
    y_tm = xtm[0:2]
    asil = p.sb("asil", [128, 512], BF16)
    m1fm = p.sb("m1fm", [128, ND, 512], BF16)
    wqs = p.sb("wqs", [128, ND, 256], BF16)
    wks = p.sb("wks", [128, ND, 256], BF16)
    wvs = p.sb("wvs", [128, ND, 256], BF16)
    wgs = p.sb("wgs", [128, ND, 256], BF16)
    w1es = [p.sb(f"w1es{i}", [128, ND, 128], BF16) for i in range(2)]
    w2ss = p.sb("w2ss", [128, ND, 16], BF16)
    bqt = p.sb("bqt", [128, 2], F32)
    bkt = p.sb("bkt", [128, 2], F32)
    bgt = p.sb("bgt", [128, 2], F32)
    b1t = p.sb("b1t", [128, 8], F32)
    bvrow = p.sb("bvrow", [1, 256], F32)
    b2row = p.sb("b2row", [1, 16], F32)
    bvb = p.sb("bvb", [128, 256], F32)
    onescol = p.sb("onescol", [1, 128], F32)
    eb = p.sb("eb", [128, 5 * NH * 128], BF16)
    r5row = p.sb("r5row", [1, 1], F32)
    chs = p.sb("chs", [128, 12], F32)
    chrow = p.sb("chrow", [1, 16], F32)
    sq64 = p.sb("sq64", [128, 512], F32)
    den_sb = p.sb("den_sb", [128, 512], F32)
    ng_sb = p.sb("ng_sb", [128, 512], BF16)
    tpb = [p.ps(f"tpb{i}", [128, 1024], BF16) for i in range(2)]
    aps = p.ps("aps", [128, 512])
    gps = p.ps("gps", [128, 512])
    eps2 = [p.ps(f"eps{i}", [128, 512]) for i in range(2)]
    p.bar_ps = p.ps("barps", [128, 512])
    p.bar_mm_src = ident

    RH = RROWS // 2
    NRT = RH // 128
    NTQ = RH // 512

    @p.on("sync")
    def _(s):
        p.dma("sync", s, ident[:], ident_i)
        p.dma("sync", s, idsb[:], myids1)
        p.dma("sync", s, fc1bt[:], fc1b_pre)
        p.dma("sync", s, fc2bt[:], fc2b_pre)
        p.dma("sync", s, bqt[:], bq_pre)
        p.dma("sync", s, bkt[:], bk_pre)
        p.dma("sync", s, bgt[:], bg_pre)
        p.dma("sync", s, b1t[:], b1_pre)
        p.dma("sync", s, bvrow[:], bv_row)
        p.dma("sync", s, b2row[:], b2_row)
        p.dma("sync", s, boffc[:], boff_i)
        for c in range(NCORE):
            p.dma("sync", s, xn1full[T * c:T * (c + 1), :], xn1_in[c])

    @p.on("gpsimd")
    def _(g):
        g.iota(iotap[:], pattern=[[0, 1]], base=0, channel_multiplier=1,
               allow_small_or_imprecise_dtypes=True)
        p.dma("gpsimd", g, wqs[:], wq_sl.rearrange("(k pp) m -> pp k m", pp=128))
        p.dma("gpsimd", g, wks[:], wk_sl.rearrange("(k pp) m -> pp k m", pp=128))
        p.dma("gpsimd", g, wvs[:], wv_sl.rearrange("(k pp) m -> pp k m", pp=128))
        p.dma("gpsimd", g, wgs[:], wgd_sl.rearrange("(k pp) m -> pp k m", pp=128))
        p.dma("gpsimd", g, w2ss[:], w2_sl.rearrange("(k pp) m -> pp k m", pp=128))
        p.dma("gpsimd", g, eb[:].rearrange("pp (a b c) -> pp a b c", a=5, b=NH),
              r5.rearrange("a b pp c -> pp a b c"))
        p.dma("gpsimd", g, fc1bf, fc1T)
        p.dma("gpsimd", g, fc2bf, fc2n)
    p.barrier()

    @p.on("vector")
    def _(v):
        v.tensor_scalar(out=vmask[:], in0=idsb[:], scalar1=float(OOBID) / 2,
                        scalar2=None, op0=ALU.is_lt)
        v.tensor_scalar(out=idsb[:], in0=idsb[:], scalar1=boffc[:, 0:1],
                        scalar2=None, op0=ALU.subtract)
        v.tensor_scalar(out=tns[:], in0=idsb[:], scalar1=0.0,
                        scalar2=None, op0=ALU.is_lt)
        v.scalar_tensor_tensor(out=idsb[:], in0=tns[:],
                               scalar=float(NCORE * T), op0=ALU.mult,
                               op1=ALU.add, in1=idsb[:])
        v.tensor_scalar(out=idsb[:], in0=idsb[:], scalar1=float(NCORE * T - 1),
                        scalar2=None, op0=ALU.min)
        v.tensor_copy(out=idsu[:], in_=idsb[:])
        v.tensor_copy(out=identb[:], in_=ident[:])
        v.memset(onescol[:], 1.0)

    @p.on("scalar")
    def _(s):
        s.activation(out=eb[:], in_=eb[:], func=AF.Exp)
    p.barrier()
    return _p45_moe(p, locals())


def _p45_moe(p, L):
    G, Vt = L["G"], L["Vt"]
    xn1full, idsu = L["xn1full"], L["idsu"]
    fc1T, fc2n = L["fc1T"], L["fc2n"]
    fc1bf, fc2bf = L["fc1bf"], L["fc2bf"]
    fc1bt, fc2bt = L["fc1bt"], L["fc2bt"]
    hrow, w1s, w2s, asil = L["hrow"], L["w1s"], L["w2s"], L["asil"]
    y_fm, y_tm = L["y_fm"], L["y_tm"]
    hrow4 = [hrow[0], hrow[1], L["xtm"][2], L["xtm"][3]]
    identb, ident = L["identb"], L["ident"]
    aps, gps, eps2, tpb = L["aps"], L["gps"], L["eps2"], L["tpb"]
    y1_out = L["y1_out"]
    vmask, iotap, idsb = L["vmask"], L["iotap"], L["idsb"]
    RH = RROWS // 2
    NRT = RH // 128
    NTQ = RH // 512
    h1_fm = [G[k // 2][:, 4096 * (k % 2):4096 * (k % 2 + 1)] for k in range(ND)]
    act = [G[4 + i // 16][:, 512 * (i % 16):512 * (i % 16) + 512]
           for i in range(NH1)]

    for half in range(2):
        s_t, b_t = p.use_sem("m_tr", NRT * ND)
        v0 = p._vcnt
        d0 = p.dma_cnt["gpsimd"]
        tick_ms = lambda r: v0 + 9 * r + 1
        tick_ev = lambda kk: v0 + 9 * (kk // 8) + (kk % 8) + 2

        @p.on("vector")
        def _(v):
            for r in range(NRT):
                if r >= 4:
                    v.wait_ge(s_t, b_t + 8 * (r - 3))
                v.memset(hrow4[r % 4][:], 0.0)
                for k in range(ND):
                    kk = 8 * r + k
                    v.wait_ge(s_t, b_t + kk + 1)
                    v.tensor_copy(out=h1_fm[k][:, 128 * r:128 * (r + 1)],
                                  in_=tpb[kk % 2][0:128, 0:128])

        sgl = [p.sem(f"mg{half}_{l}") for l in range(4)]

        @p.on("gpsimd")
        def _(g):
            for r in range(NRT):
                g.wait_ge(p.vsem, tick_ms(r))
                if r >= 4:
                    g.wait_ge(sgl[r % 4], 16 * (r // 4))
                p.idma_lane(g, sgl[r % 4], out=hrow4[r % 4][:],
                            out_offset=None, in_=xn1full,
                            in_offset=IndirectOffsetOnAxis(
                                ap=idsu[:, half * NRT + r:half * NRT + r + 1],
                                axis=0))

        @p.on("tensor")
        def _(t):
            for r in range(NRT):
                t.wait_ge(sgl[r % 4], 16 * (r // 4 + 1))
                for k in range(ND):
                    kk = 8 * r + k
                    if kk >= 2:
                        t.wait_ge(p.vsem, tick_ev(kk - 2))
                    t.transpose(tpb[kk % 2][0:128, 0:128],
                                hrow4[r % 4][:, 128 * k:128 * (k + 1)],
                                identb[:]).then_inc(s_t, 1)
        p.barrier()

        for tq in range(NTQ):
            s_mm, b_mm = p.use_sem("m_mm", NH1)
            s_si, b_si = p.use_sem("m_si", NH1)
            d1 = p.dma_cnt["sync"]

            @p.on("sync")
            def _(s):
                for i in range(NH1):
                    buf = i % 2
                    if i >= 2:
                        s.wait_ge(s_mm, b_mm + i - 1)
                    p.dma("sync", s, w1s[buf][:],
                          fc1bf.rearrange("(k pp) i -> pp k i", pp=128)
                          [:, :, 128 * i:128 * (i + 1)])

            @p.on("tensor")
            def _(t):
                for i in range(NH1):
                    buf = i % 2
                    t.wait_ge(p.ldsem["sync"], 16 * (d1 + i + 1))
                    if i >= 1:
                        t.wait_ge(s_si, b_si + i)
                    for k in range(ND):
                        mm = t.matmul(aps[:], w1s[buf][:, k, :],
                                      h1_fm[k][:, 512 * tq:512 * (tq + 1)],
                                      start=(k == 0), stop=(k == ND - 1))
                    mm.then_inc(s_mm, 1)

            @p.on("scalar")
            def _(s):
                for i in range(NH1):
                    s.wait_ge(s_mm, b_mm + i + 1)
                    s.activation(out=act[i][:], in_=aps[:], func=AF.Silu,
                                 bias=fc1bt[:, i:i + 1]).then_inc(s_si, 1)
            p.barrier()

            s_eo, b_eo = p.use_sem("m_eo", ND)
            d2 = p.dma_cnt["sync"]
            s_ev, b_ev = p.use_sem("m_ev", ND)

            @p.on("sync")
            def _(s):
                for m in range(ND):
                    buf = m % 2
                    if m >= 1:
                        s.wait_ge(s_eo, b_eo + m)
                    p.dma("sync", s, w2s[buf][:],
                          fc2bf.rearrange("(ii pp) dd -> pp ii dd", pp=128)
                          [:, :, 128 * m:128 * (m + 1)])

            @p.on("tensor")
            def _(t):
                for m in range(ND):
                    buf = m % 2
                    t.wait_ge(p.ldsem["sync"], 16 * (d2 + m + 1))
                    if m >= 2:
                        t.wait_ge(s_ev, b_ev + m - 1)
                    for i in range(NH1):
                        mm = t.matmul(eps2[m % 2][:], w2s[buf][:, i, :],
                                      act[i][:],
                                      start=(i == 0), stop=(i == NH1 - 1))
                    mm.then_inc(s_eo, 1)

            @p.on("scalar")
            def _(s):
                for m in range(ND):
                    s.wait_ge(s_eo, b_eo + m + 1)
                    s.activation(out=y_fm[m][:], in_=eps2[m % 2][:],
                                 func=AF.Identity,
                                 bias=fc2bt[:, m:m + 1]).then_inc(s_ev, 1)

            s_yt, b_yt = p.use_sem("m_yt", 32)
            v3 = p._vcnt
            d3 = p.dma_cnt["gpsimd"]

            @p.on("tensor")
            def _(t, s_ev=s_ev, b_ev=b_ev):
                for rr in range(4):
                    for m in range(ND):
                        kk = 8 * rr + m
                        t.wait_ge(s_ev, b_ev + m + 1)
                        if kk >= 2:
                            kp = kk - 2
                            t.wait_ge(p.vsem, v3 + kp + 1 + kp // 8)
                        t.transpose(tpb[kk % 2][0:128, 0:128],
                                    y_fm[m][:, 128 * rr:128 * (rr + 1)],
                                    identb[:]).then_inc(s_yt, 1)

            @p.on("vector")
            def _(v):
                for rr in range(4):
                    if rr >= 2:
                        v.wait_ge(p.ldsem["gpsimd"], 16 * (d3 + rr - 1))
                    for m in range(ND):
                        kk = 8 * rr + m
                        v.wait_ge(s_yt, b_yt + kk + 1)
                        v.tensor_copy(out=y_tm[rr % 2][:, 128 * m:128 * (m + 1)],
                                      in_=tpb[kk % 2][0:128, 0:128])
                    rt = half * NRT + 4 * tq + rr
                    v.tensor_scalar(out=y_tm[rr % 2][:], in0=y_tm[rr % 2][:],
                                    scalar1=vmask[:, rt:rt + 1], scalar2=None,
                                    op0=ALU.mult)

            @p.on("gpsimd")
            def _(g):
                for rr in range(4):
                    g.wait_ge(p.vsem, v3 + 9 * rr + 9)
                    row0 = half * RH + 512 * tq + 128 * rr
                    p.dma("gpsimd", g, y1_out[row0:row0 + 128, :],
                          y_tm[rr % 2][:])
            p.barrier()
    return _p45_attn(p, L)


def _p45_attn(p, L):
    G, Vt = L["G"], L["Vt"]
    xn1full = L["xn1full"]
    (wqs, wks, wvs, wgs, w1es, w2ss) = (L["wqs"], L["wks"], L["wvs"],
                                        L["wgs"], L["w1es"], L["w2ss"])
    (bqt, bkt, bgt, b1t, bvrow, b2row, bvb, onescol) = (
        L["bqt"], L["bkt"], L["bgt"], L["b1t"], L["bvrow"], L["b2row"],
        L["bvb"], L["onescol"])
    xtm, xfm, m1fm = L["xtm"], L["xfm"], L["m1fm"]
    eb, chs, chrow, sq64, den_sb, ng_sb = (L["eb"], L["chs"], L["chrow"],
                                           L["sq64"], L["den_sb"], L["ng_sb"])
    ident, identb = L["ident"], L["identb"]
    tpb, aps, gps, eps2 = L["tpb"], L["aps"], L["gps"], L["eps2"]
    (y1_out, numg_out, gate_out, den_out, edgep_out) = (
        L["y1_out"], L["numg_out"], L["gate_out"], L["den_out"],
        L["edgep_out"])
    r5, tconsts, w1_sl = L["r5"], L["tconsts"], L["w1_sl"]
    TB, NQC, NKT, NH = N, N // 512, N // 128, 4
    ep16 = p.sb("ep16", [16, 512], F32)
    gevac = L["asil"]

    @p.on("tensor")
    def _(t):
        t.matmul(aps[:, 0:256], onescol[:], bvrow[:], start=True, stop=True)
    p.barrier()

    @p.on("vector")
    def _(v):
        v.tensor_copy(out=bvb[:], in_=aps[:, 0:256])
    p.barrier()

    for tq in range(NQC):
        @p.on("sync")
        def _(s, tq=tq):
            for i4 in range(4):
                p.dma("sync", s, xtm[i4][:],
                      xn1full[512 * tq + 128 * i4:512 * tq + 128 * (i4 + 1), :])
        p.barrier()
        pairs = []
        for i4 in range(4):
            for k in range(ND):
                pairs.append((xtm[i4][:, 128 * k:128 * (k + 1)],
                              xfm[k][:, 128 * i4:128 * (i4 + 1)]))
        emit_transpose_seq(p, pairs, identb, tpb)

        s_mm, b_mm = p.use_sem("a_mm", 18)
        s_ev, b_ev = p.use_sem("a_ev", 14)
        v_vv = p._vcnt
        ngrp = 0
        specs = []
        for mc in range(2):
            specs.append(("q", mc))
            specs.append(("k", mc))
            specs.append(("g", mc))
        for ts4 in range(4):
            specs.append(("v", ts4))
        for mc in range(ND):
            specs.append(("m1", mc))

        d_w = p.dma_cnt["gpsimd"]

        @p.on("gpsimd")
        def _(g, tq=tq):
            for mi, (kind, mc) in enumerate([sp for sp in specs
                                             if sp[0] == "m1"]):
                buf = mc % 2
                if mc >= 2:
                    g.wait_ge(s_mm, b_mm + 10 + mc - 1)
                p.dma("gpsimd", g, w1es[buf][:],
                      w1_sl.rearrange("(k pp) m -> pp k m", pp=128)
                      [:, :, 128 * mc:128 * (mc + 1)])

        kinds_list = [k for k, m in specs]

        def evc(gi):
            ns = sum(1 for g2 in range(gi + 1) if kinds_list[g2] != "v")
            nv = sum(1 for g2 in range(gi + 1) if kinds_list[g2] == "v")
            return ns, nv

        @p.on("tensor")
        def _(t, tq=tq):
            gi = 0
            for kind, mc in specs:
                ps = aps if gi % 2 == 0 else gps
                if gi >= 2:
                    ns, nv = evc(gi - 2)
                    t.wait_ge(s_ev, b_ev + ns)
                    if nv > 0:
                        t.wait_ge(p.vsem, v_vv + 2 * nv)
                if kind == "q":
                    for k in range(ND):
                        mm = t.matmul(ps[:], wqs[:, k, 128 * mc:128 * (mc + 1)],
                                      xfm[k][:], start=(k == 0),
                                      stop=(k == ND - 1))
                elif kind == "k":
                    for k in range(ND):
                        mm = t.matmul(ps[:], wks[:, k, 128 * mc:128 * (mc + 1)],
                                      xfm[k][:], start=(k == 0),
                                      stop=(k == ND - 1))
                elif kind == "g":
                    for k in range(ND):
                        mm = t.matmul(ps[:], wgs[:, k, 128 * mc:128 * (mc + 1)],
                                      xfm[k][:], start=(k == 0),
                                      stop=(k == ND - 1))
                elif kind == "v":
                    for k in range(ND):
                        mm = t.matmul(ps[:, 0:256],
                                      xfm[k][:, 128 * mc:128 * (mc + 1)],
                                      wvs[:, k, :], start=(k == 0),
                                      stop=(k == ND - 1))
                else:
                    t.wait_ge(p.ldsem["gpsimd"], 16 * (d_w + mc + 1))
                    for k in range(ND):
                        mm = t.matmul(ps[:], w1es[mc % 2][:, k, :], xfm[k][:],
                                      start=(k == 0), stop=(k == ND - 1))
                mm.then_inc(s_mm, 1)
                gi += 1

        v_ev = p._vcnt

        @p.on("scalar")
        def _(s, tq=tq):
            gi = 0
            for kind, mc in specs:
                ps = aps if gi % 2 == 0 else gps
                s.wait_ge(s_mm, b_mm + gi + 1)
                if kind == "q":
                    s.activation(out=G[mc][:, 512 * tq:512 * (tq + 1)],
                                 in_=ps[:], func=AF.Identity,
                                 bias=bqt[:, mc:mc + 1]).then_inc(s_ev, 1)
                elif kind == "k":
                    s.activation(out=G[2 + mc][:, 512 * tq:512 * (tq + 1)],
                                 in_=ps[:], func=AF.Identity,
                                 bias=bkt[:, mc:mc + 1]).then_inc(s_ev, 1)
                elif kind == "g":
                    s.activation(out=(gevac if mc == 0 else ng_sb)[:],
                                 in_=ps[:], func=AF.Sigmoid,
                                 bias=bgt[:, mc:mc + 1]).then_inc(s_ev, 1)
                elif kind == "m1":
                    s.activation(out=m1fm[:, mc, :], in_=ps[:], func=AF.Silu,
                                 bias=b1t[:, mc:mc + 1]).then_inc(s_ev, 1)
                else:
                    s.nop()
                gi += 1

        @p.on("vector")
        def _(v, tq=tq):
            gi = 0
            for kind, mc in specs:
                ps = aps if gi % 2 == 0 else gps
                if kind == "v":
                    v.wait_ge(s_mm, b_mm + gi + 1)
                    kt = 4 * tq + mc
                    vv = Vt[:, 260 * kt:260 * (kt + 1)].rearrange(
                        "pp (h q) -> pp h q", q=65)
                    v.tensor_tensor(out=vv[:, :, 0:64],
                                    in0=ps[:, 0:256].rearrange(
                                        "pp (h q) -> pp h q", q=64),
                                    in1=bvb[:].rearrange(
                                        "pp (h q) -> pp h q", q=64),
                                    op=ALU.add)
                    v.memset(vv[:, :, 64:65], 1.0)
                gi += 1

        @p.on("gpsimd")
        def _(g, tq=tq):
            gidx = 0
            for gi, (kind, mc) in enumerate(specs):
                if kind == "g":
                    ns, _ = evc(gi)
                    g.wait_ge(s_ev, b_ev + ns)
                    p.dma("gpsimd", g,
                          gate_out[128 * mc:128 * (mc + 1),
                                   512 * tq:512 * (tq + 1)],
                          (gevac if mc == 0 else ng_sb)[:])
                    gidx += 1
        p.barrier()

        s_ed, b_ed = p.use_sem("a_ed", 1)

        @p.on("tensor")
        def _(t, tq=tq):
            for mc in range(ND):
                mm = t.matmul(eps2[0][0:16, :], w2ss[:, mc, :], m1fm[:, mc, :],
                              start=(mc == 0), stop=(mc == ND - 1))
            mm.then_inc(s_ed, 1)

        @p.on("vector")
        def _(v, tq=tq, b_ed=b_ed):
            v.wait_ge(s_ed, b_ed + 1)
            v.tensor_copy(out=ep16[:], in_=eps2[0][0:16, :])

        @p.on("gpsimd")
        def _(g, tq=tq):
            g.wait_ge(p.vsem, p._vcnt)
            p.dma("gpsimd", g, edgep_out[:, 512 * tq:512 * (tq + 1)], ep16[:])
        p.barrier()
    return _p45_scores(p, L)


def _p45_scores(p, L):
    G, Vt = L["G"], L["Vt"]
    eb, chs, chrow, sq64, den_sb, ng_sb = (L["eb"], L["chs"], L["chrow"],
                                           L["sq64"], L["den_sb"], L["ng_sb"])
    ident, identb, onescol = L["ident"], L["identb"], L["onescol"]
    tpb, aps, gps, eps2 = L["tpb"], L["aps"], L["gps"], L["eps2"]
    numg_out, den_out = L["numg_out"], L["den_out"]
    r5, tconsts = L["r5"], L["tconsts"]
    TB, NQC, NKT, NH = N, N // 512, N // 128, 4
    m1fm = L["m1fm"]
    ones2 = p.sb("onesb2", [128, 1], BF16)
    onesf = p.sb("onesf", [128, 1], F32)
    mx1 = p.sb("mx1", [1, 1], F32)
    pt2 = [m1fm[:, i, :] for i in range(2)]
    tct = p.sb("tct2", [1, 2 * NH], F32)

    @p.on("sync")
    def _(s):
        p.dma("sync", s, tct[:], tconsts)

    @p.on("vector")
    def _(v):
        v.memset(ones2[:], 1.0)
        v.memset(onesf[:], 1.0)
    p.barrier()

    @p.on("vector")
    def _(v):
        v.memset(chrow[:, 0:16], -1.0e30)
    p.barrier()
    if _SC_PHASE is not None and _SC_PHASE <= 0:
        return p.finish()

    sqb = p.sb("sqb", [128, 512], BF16)
    for h in range(NH):
        b0 = 64 * (h % 2)
        for (which, Gt, col) in (("q", G[h // 2], 0), ("k", G[2 + h // 2], 4)):
            s_sq, b_sq = p.use_sem("a_sq", 2 * NQC)
            v_sq = p._vcnt

            @p.on("scalar")
            def _(s, Gt=Gt, b0=b0, s_sq=s_sq, b_sq=b_sq):
                for tq in range(NQC):
                    if tq >= 1:
                        s.wait_ge(s_sq, b_sq + 2 * tq)
                    s.activation(out=sqb[b0:b0 + 64, 0:512],
                                 in_=Gt[b0:b0 + 64, 512 * tq:512 * (tq + 1)],
                                 func=AF.Square).then_inc(s_sq, 1)

            @p.on("tensor")
            def _(t, b0=b0, s_sq=s_sq, b_sq=b_sq, v_sq=v_sq):
                for tq in range(NQC):
                    t.wait_ge(s_sq, b_sq + 2 * tq + 1)
                    if tq >= 1:
                        t.wait_ge(p.vsem, v_sq + 2 * tq - 1)
                    t.matmul(gps[0:1, :], ones2[b0:b0 + 64, 0:1],
                             sqb[b0:b0 + 64, 0:512],
                             start=True, stop=True).then_inc(s_sq, 1)
            @p.on("vector")
            def _(v, h=h, col=col):
                for tq in range(NQC):
                    v.wait_ge(s_sq, b_sq + 2 * (tq + 1))
                    v.reduce_max(out=mx1[0:1, 0:1], in_=gps[0:1, :],
                                 axis=AX.X)
                    v.tensor_tensor(out=chrow[0:1, col + h:col + h + 1],
                                    in0=chrow[0:1, col + h:col + h + 1],
                                    in1=mx1[0:1, 0:1], op=ALU.max)
            p.barrier()

    if _SC_PHASE is not None and _SC_PHASE <= 1:
        return p.finish()
    mx128 = p.sb("mx128", [128, 1], F32)
    mx128b = p.sb("mx128b", [128, 1], F32)
    for h in range(NH):
        @p.on("vector")
        def _(v, h=h):
            v.memset(mx128[:], -1.0e30)
            for cl in range(5):
                v.reduce_max(out=mx128b[:],
                             in_=eb[:, 128 * (NH * cl + h):
                                    128 * (NH * cl + h + 1)], axis=AX.X)
                v.tensor_tensor(out=mx128[:], in0=mx128[:], in1=mx128b[:],
                                op=ALU.max)
        p.barrier()

        @p.on("tensor")
        def _(t, h=h):
            t.transpose(gps[0:1, 0:128], mx128[:], ident[:])
        p.barrier()

        @p.on("vector")
        def _(v, h=h):
            v.tensor_copy(out=sq64[0:1, 0:128], in_=gps[0:1, 0:128])
            v.reduce_max(out=chrow[0:1, 8 + h:8 + h + 1],
                         in_=sq64[0:1, 0:128], axis=AX.X)
        p.barrier()

    if _SC_PHASE is not None and _SC_PHASE <= 2:
        return p.finish()
    @p.on("scalar")
    def _(s):
        s.activation(out=chrow[0:1, 8:12], in_=chrow[0:1, 8:12], func=AF.Ln)
    p.barrier()

    @p.on("scalar")
    def _(s):
        s.activation(out=chrow[0:1, 0:8], in_=chrow[0:1, 0:8], func=AF.Sqrt)
    p.barrier()

    @p.on("vector")
    def _(v):
        v.tensor_tensor(out=chrow[0:1, 12:16], in0=chrow[0:1, 0:4],
                        in1=chrow[0:1, 4:8], op=ALU.mult)
        v.tensor_scalar(out=chrow[0:1, 12:16], in0=chrow[0:1, 12:16],
                        scalar1=0.125, scalar2=None, op0=ALU.mult)
        v.tensor_tensor(out=chrow[0:1, 12:16], in0=chrow[0:1, 12:16],
                        in1=chrow[0:1, 8:12], op=ALU.add)
    p.barrier()

    brow = p.sb("brow", [1, 12], F32)

    @p.on("vector")
    def _(v):
        v.tensor_scalar(out=brow[0:1, 0:4], in0=chrow[0:1, 12:16],
                        scalar1=-1.0, scalar2=None, op0=ALU.mult)
        v.tensor_tensor(out=brow[0:1, 4:8], in0=tct[0:1, NH:2 * NH],
                        in1=brow[0:1, 0:4], op=ALU.add)
        v.tensor_tensor(out=brow[0:1, 8:12], in0=tct[0:1, 0:NH],
                        in1=brow[0:1, 0:4], op=ALU.add)
    p.barrier()

    @p.on("tensor")
    def _(t):
        t.matmul(aps[:, 0:12], onescol[:], brow[:], start=True, stop=True)
    p.barrier()

    @p.on("vector")
    def _(v):
        v.tensor_copy(out=chs[:, 0:12], in_=aps[:, 0:12])
    p.barrier()

    if _SC_PHASE is not None and _SC_PHASE <= 3:
        return p.finish()
    _sc_carry = {}
    for h in range(NH):
        b0 = 64 * (h % 2)
        for qc in range(NQC):
            if _SC_LIMIT is not None and h * NQC + qc >= _SC_LIMIT:
                continue
            s_sc, b_sc = p.use_sem("s_sc", NKT)
            s_ex, b_ex = p.use_sem("s_ex", NKT)
            s_av, b_av = p.use_sem("s_av", NKT)
            v_ib = p._vcnt


            def klass(kt, qsub, qc=qc):
                d = 128 * kt - (512 * qc + 128 * qsub)
                if -256 <= d <= 256:
                    return ("ib", (d + 256) // 128)
                return ("lo", None) if d < 0 else ("hi", None)

            ibcnt = []
            tot = 0
            for kt in range(NKT):
                n = sum(1 for qs in range(4) if klass(kt, qs)[0] == "ib")
                ibcnt.append(tot)
                tot += n

            @p.on("scalar")
            def _(s, h=h, qc=qc):
                for kt in range(NKT):
                    ps = eps2[kt % 2]
                    s.wait_ge(s_sc, b_sc + kt + 1)
                    if b_av + kt - 1 > 0:
                        s.wait_ge(s_av, b_av + kt - 1)
                    kinds = [klass(kt, qs) for qs in range(4)]
                    if all(kk[0] != "ib" for kk in kinds):
                        if all(kk[0] == kinds[0][0] for kk in kinds):
                            col = 4 + h if kinds[0][0] == "lo" else 8 + h
                            s.activation(out=pt2[kt % 2], in_=ps[:],
                                         func=AF.Exp, scale=0.125,
                                         bias=chs[:, col:col + 1]
                                         ).then_inc(s_ex, 1)
                            continue
                    for qs in range(4):
                        kk = kinds[qs]
                        col = (h if kk[0] == "ib"
                               else (4 + h if kk[0] == "lo" else 8 + h))
                        act = s.activation(
                            out=pt2[kt % 2][:, 128 * qs:128 * (qs + 1)],
                            in_=ps[:, 128 * qs:128 * (qs + 1)],
                            func=AF.Exp, scale=0.125,
                            bias=chs[:, col:col + 1])
                    act.then_inc(s_ex, 1)

            @p.on("vector")
            def _(v, h=h, qc=qc):
                for kt in range(NKT):
                    for qs in range(4):
                        kk = klass(kt, qs)
                        if kk[0] == "ib":
                            v.wait_ge(s_ex, b_ex + kt + 1)
                            v.tensor_tensor(
                                out=pt2[kt % 2][:, 128 * qs:128 * (qs + 1)],
                                in0=pt2[kt % 2][:, 128 * qs:128 * (qs + 1)],
                                in1=eb[:, 128 * (5 * h + kk[1]):
                                       128 * (5 * h + kk[1] + 1)],
                                op=ALU.mult)

            pe_prev = _sc_carry.get("ev2")

            @p.on("tensor")
            def _(t, h=h, qc=qc, b0=b0, pe_prev=pe_prev):
                def av(kt):
                    nib_k = sum(1 for qs in range(4)
                                if klass(kt, qs)[0] == "ib")
                    if nib_k > 0:
                        t.wait_ge(p.vsem, v_ib + ibcnt[kt] + nib_k)
                    else:
                        t.wait_ge(s_ex, b_ex + kt + 1)
                    if kt == 0 and pe_prev is not None:
                        t.wait_ge(p.vsem, pe_prev)
                    mm = t.matmul(aps[0:65, :], Vt[:, 260 * kt + 65 * h:
                                                   260 * kt + 65 * (h + 1)],
                                  pt2[kt % 2], start=(kt == 0),
                                  stop=(kt == NKT - 1))
                    mm.then_inc(s_av, 1)

                for kt in range(NKT):
                    ps = eps2[kt % 2]
                    if b_ex + kt - 1 > 0:
                        t.wait_ge(s_ex, b_ex + kt - 1)
                    t.matmul(ps[:], G[2 + h // 2][b0:b0 + 64,
                                                  128 * kt:128 * (kt + 1)],
                             G[h // 2][b0:b0 + 64, 512 * qc:512 * (qc + 1)],
                             start=True, stop=True).then_inc(s_sc, 1)
                    if kt >= 1:
                        av(kt - 1)
                av(NKT - 1)

            d_prev = _sc_carry.get("d")
            t_ev2 = v_ib + tot + 2

            @p.on("vector")
            def _(v, s_av=s_av, b_av=b_av, d_prev=d_prev):
                v.wait_ge(s_av, b_av + NKT)
                if d_prev is not None:
                    v.wait_ge(p.ldsem["gpsimd"], 16 * (d_prev + 2))
                v.tensor_copy(out=ng_sb[0:64, :], in_=aps[0:64, :])
                v.tensor_copy(out=den_sb[64:65, :], in_=aps[64:65, :])

            _sc_carry["d"] = p.dma_cnt["gpsimd"]
            _sc_carry["ev2"] = t_ev2

            @p.on("gpsimd")
            def _(g, h=h, qc=qc, t_ev2=t_ev2):
                g.wait_ge(p.vsem, t_ev2)
                p.dma("gpsimd", g,
                      numg_out[64 * h:64 * (h + 1), 512 * qc:512 * (qc + 1)],
                      ng_sb[0:64, :])
                p.dma("gpsimd", g, den_out[h:h + 1, 512 * qc:512 * (qc + 1)],
                      den_sb[64:65, :])
    p.barrier()
    return p.finish()



def build_p6():
    p = Prog("p6")
    attng_i = p.inp("attng", [D, T], BF16)
    gateg_i = p.inp("gateg", [D, T], BF16)
    deng_i = p.inp("deng", [H, T])
    edgep_i = p.inp("edgepg", [4 * H, T])
    wo = p.inp("wo", [D, D])
    bo_pre = p.inp("bo_pre", [128, ND])
    b2_col = p.inp("b2_col", [H, 1])
    lng_row_i = p.inp("lng_row", [1, D])
    lnb_row_i = p.inp("lnb_row", [1, D])
    y1own = p.inp("y1own", [RROWS, D], BF16)
    x_in = p.inp("x", [T, D])
    s0g0_i = p.inp("s0g0", [T, D], BF16)
    xn1_i = p.inp("xn1", [T, D], BF16)
    pos1 = p.inp("pos1", [T, 2])
    idx1 = p.inp("idx1", [T, 2])
    locj = p.inp("locj", [T, 2])
    g011 = p.inp("g011", [T, 2])
    cnts_all = p.inp("cnts_all", [NE, 2 * NE])
    cmask6 = p.inp("cmask6", [NE, 1])
    iota8 = p.inp("iota8", [1, NE])
    fsel = p.inp("fsel", [H, D])
    ident_i = p.inp("ident", [128, 128])

    x2_out = p.outp("x2", [T, D])

    p.init_sems()

    atg = p.sb("atg", [128, ND, T], BF16)
    gtg = p.sb("gtg", [128, ND, T], BF16)
    wos = p.sb("wos", [128, ND, D], BF16)
    ident = p.sb("id128", [128, 128], F32)
    e16 = p.sb("e16", [H, T], F32)
    ebuf = [p.sb(f"ebuf{i}", [H, T], F32) for i in range(2)]
    densb = p.sb("densb", [H, T], F32)
    fxb = p.sb("fxb", [128, 512], BF16)
    lngr = p.sb("lngr", [1, D], F32)
    lnbr = p.sb("lnbr", [1, D], F32)
    lngb = p.sb("lngb", [128, D], F32)
    lnbb = p.sb("lnbb", [128, D], F32)
    iotab = p.sb("iotab", [128, NE], F32)
    basejb = p.sb("basejb", [128, 2 * NE], F32)
    baser = p.sb("baser", [1, 2 * NE], F32)
    cntsb = p.sb("cntsb", [NE, 2 * NE], F32)
    cmk = p.sb("cmk", [NE, 1], F32)
    ones8 = p.sb("ones8", [NE, 1], F32)
    onesr = p.sb("onesr", [1, 128], F32)
    io8 = p.sb("io8", [1, NE], F32)
    b2c = p.sb("b2c", [H, 1], F32)
    bot = p.sb("bot", [128, ND], F32)
    fselb = p.sb("fselb", [H, D], F32)
    posb = p.sb("posb", [128, NT, 2], F32)
    posu = p.sb("posu", [128, NT, 2], U32)
    idxb = p.sb("idxb", [128, NT, 2], F32)
    locb = p.sb("locb", [128, NT, 2], F32)
    gb = p.sb("gb", [128, NT, 2], F32)
    xs = [p.sb(f"xs{i}", [128, D], F32) for i in range(2)]
    sg = [p.sb(f"sg{i}", [128, D], BF16) for i in range(2)]
    xnl = [p.sb(f"xnl{i}", [128, D], BF16) for i in range(2)]
    yr = [p.sb(f"yr{i}", [128, D], BF16) for i in range(2)]
    s1f = p.sb("s1f", [128, D], F32)
    s1t = p.sb("s1t", [128, D], F32)
    t0f = p.sb("t0f", [128, D], F32)
    t1f = p.sb("t1f", [128, D], F32)
    res = p.sb("res", [128, D], F32)
    tmpg = p.sb("tmpg", [128, D], F32)
    g1s = p.sb("g1s", [128, D], F32)
    xf32 = p.sb("xf32", [128, D], F32)
    oh8 = p.sb("oh8", [128, NE], F32)
    t8b = p.sb("t8b", [128, NE], F32)
    gec = p.sb("gec", [128, 2], F32)
    bsel = p.sb("bsel", [128, 1], F32)
    alloc_norm_scratch(p, NT)

    sps = [p.ps(f"sps{i}", [128, 512]) for i in range(2)]
    tps = [p.ps(f"tps{i}", [128, 512]) for i in range(2)]
    fps = [p.ps(f"fps{i}", [128, 512]) for i in range(2)]
    p.bar_ps = p.ps("barps", [128, 512])
    p.bar_mm_src = ident

    @p.on("sync")
    def _(s):
        p.dma("sync", s, ident[:], ident_i)
        for k in range(ND):
            p.dma("sync", s, atg[:, k, :], attng_i[128 * k:128 * (k + 1), :])
            p.dma("sync", s, gtg[:, k, :], gateg_i[128 * k:128 * (k + 1), :])
        p.dma("sync", s, densb[:], deng_i)
        p.dma("sync", s, ebuf[0][:], edgep_i[0:H, :])
        p.dma("sync", s, ebuf[1][:], edgep_i[H:2 * H, :])
        p.dma("sync", s, bot[:], bo_pre)
        p.dma("sync", s, b2c[:], b2_col)
        p.dma("sync", s, cntsb[:], cnts_all)
        p.dma("sync", s, cmk[:], cmask6)
        p.dma("sync", s, io8[:], iota8)
        p.dma("sync", s, fselb[:], fsel)
        p.dma("sync", s, lngr[:], lng_row_i)
        p.dma("sync", s, lnbr[:], lnb_row_i)
        p.dma("sync", s, posb[:], pos1.rearrange("(i pp) j -> pp i j", pp=128))
        p.dma("sync", s, idxb[:], idx1.rearrange("(i pp) j -> pp i j", pp=128))
        p.dma("sync", s, locb[:], locj.rearrange("(i pp) j -> pp i j", pp=128))
        p.dma("sync", s, gb[:], g011.rearrange("(i pp) j -> pp i j", pp=128))

    @p.on("gpsimd")
    def _(g):
        p.dma("gpsimd", g, wos[:], wo.rearrange("(k pp) d -> pp k d", pp=128))

    @p.on("vector")
    def _(v):
        v.memset(onesr[:], 1.0)
        v.memset(ones8[:], 1.0)
    p.barrier()

    @p.on("vector")
    def _(v):
        v.tensor_tensor(out=e16[:], in0=ebuf[0][:], in1=ebuf[1][:], op=ALU.add)
        v.tensor_copy(out=posu[:], in_=posb[:])
    p.barrier()

    @p.on("sync")
    def _(s):
        p.dma("sync", s, ebuf[0][:], edgep_i[2 * H:3 * H, :])
        p.dma("sync", s, ebuf[1][:], edgep_i[3 * H:4 * H, :])
    p.barrier()

    @p.on("vector")
    def _(v):
        v.tensor_tensor(out=e16[:], in0=e16[:], in1=ebuf[0][:], op=ALU.add)
        v.tensor_tensor(out=e16[:], in0=e16[:], in1=ebuf[1][:], op=ALU.add)
        v.tensor_scalar(out=e16[:], in0=e16[:], scalar1=b2c[:, 0:1],
                        scalar2=None, op0=ALU.add)
        v.tensor_scalar(out=e16[:], in0=e16[:], scalar1=60.0,
                        scalar2=None, op0=ALU.min)
        v.tensor_scalar(out=e16[:], in0=e16[:], scalar1=-60.0,
                        scalar2=None, op0=ALU.max)

    @p.on("tensor")
    def _(t):
        t.matmul(sps[0][:, 0:512], onesr[0:1, :], lngr[0:1, 0:512],
                 start=True, stop=True)
        t.matmul(sps[1][:, 0:512], onesr[0:1, :], lngr[0:1, 512:1024],
                 start=True, stop=True)
        t.matmul(fps[0][:, 0:512], onesr[0:1, :], lnbr[0:1, 0:512],
                 start=True, stop=True)
        t.matmul(fps[1][:, 0:512], onesr[0:1, :], lnbr[0:1, 512:1024],
                 start=True, stop=True)
        t.matmul(tps[0][:, 0:8], onesr[0:1, :], io8[0:1, :],
                 start=True, stop=True)
        t.matmul(tps[1][0:1, 0:16], cmk[:, 0:1], cntsb[:, :],
                 start=True, stop=True)
        t.matmul(tps[1][0:1, 16:24], ones8[:, 0:1], cntsb[:, 0:8],
                 start=True, stop=True)
    p.barrier()

    @p.on("scalar")
    def _(s):
        s.activation(out=e16[:], in_=e16[:], func=AF.Exp)

    @p.on("vector")
    def _(v):
        v.tensor_copy(out=lngb[:, 0:512], in_=sps[0][:, 0:512])
        v.tensor_copy(out=lngb[:, 512:1024], in_=sps[1][:, 0:512])
        v.tensor_copy(out=lnbb[:, 0:512], in_=fps[0][:, 0:512])
        v.tensor_copy(out=lnbb[:, 512:1024], in_=fps[1][:, 0:512])
        v.tensor_copy(out=iotab[:], in_=tps[0][:, 0:8])
        v.tensor_copy(out=baser[:], in_=tps[1][0:1, 0:16])
        v.tensor_tensor(out=baser[0:1, 8:16], in0=baser[0:1, 8:16],
                        in1=tps[1][0:1, 16:24], op=ALU.add)
    p.barrier()

    @p.on("vector")
    def _(v):
        v.tensor_tensor(out=densb[:], in0=densb[:], in1=e16[:], op=ALU.mult)
        v.tensor_scalar(out=densb[:], in0=densb[:], scalar1=1.0e-35,
                        scalar2=None, op0=ALU.max)
        v.reciprocal(out=densb[:], in_=densb[:])
        v.tensor_tensor(out=e16[:], in0=e16[:], in1=densb[:], op=ALU.mult)

    @p.on("tensor")
    def _(t):
        t.matmul(fps[0][:, 0:8], onesr[0:1, :], baser[0:1, 0:8],
                 start=True, stop=True)
        t.matmul(fps[0][:, 8:16], onesr[0:1, :], baser[0:1, 8:16],
                 start=True, stop=True)
    p.barrier()

    @p.on("vector")
    def _(v):
        v.tensor_copy(out=basejb[:], in_=fps[0][:, 0:16])
    p.barrier()

    s_fx, b_fx = p.use_sem("fx", 32)
    v0 = p._vcnt

    @p.on("tensor")
    def _(t):
        for k in range(ND):
            for ct in range(4):
                idx = 4 * k + ct
                if idx >= 2:
                    t.wait_ge(p.vsem, v0 + 3 * (idx - 2) + 1)
                t.matmul(fps[idx % 2][:, 0:512],
                         fselb[:, 128 * k:128 * (k + 1)],
                         e16[:, 512 * ct:512 * (ct + 1)],
                         start=True, stop=True).then_inc(s_fx, 1)

    @p.on("vector")
    def _(v):
        for k in range(ND):
            for ct in range(4):
                idx = 4 * k + ct
                sl = slice(512 * ct, 512 * (ct + 1))
                v.wait_ge(s_fx, b_fx + idx + 1)
                v.tensor_copy(out=fxb[:], in_=fps[idx % 2][:, 0:512])
                v.tensor_tensor(out=atg[:, k, sl], in0=atg[:, k, sl],
                                in1=fxb[:], op=ALU.mult)
                v.tensor_tensor(out=atg[:, k, sl], in0=atg[:, k, sl],
                                in1=gtg[:, k, sl], op=ALU.mult)
    p.barrier()

    for i in range(NT):
        d_g = p.dma_cnt["gpsimd"]

        @p.on("sync")
        def _(s, i=i):
            p.dma("sync", s, xs[i % 2][:], x_in[128 * i:128 * (i + 1), :])
            p.dma("sync", s, sg[i % 2][:], s0g0_i[128 * i:128 * (i + 1), :])
            p.dma("sync", s, xnl[i % 2][:], xn1_i[128 * i:128 * (i + 1), :])

        @p.on("gpsimd")
        def _(g, i=i):
            for j in range(2):
                p.idma(g, out=yr[j][:], out_offset=None, in_=y1own,
                       in_offset=IndirectOffsetOnAxis(
                           ap=posu[:, i, j:j + 1], axis=0))

        s_mm, b_mm = p.use_sem("x_mm", ND)
        s_ev, b_ev = p.use_sem("x_ev", ND)

        @p.on("tensor")
        def _(t, i=i, s_mm=s_mm, s_ev=s_ev, b_ev=b_ev):
            for m in range(ND):
                if m >= 2:
                    t.wait_ge(s_ev, b_ev + m - 1)
                for k in range(ND):
                    mm = t.matmul(sps[m % 2][:, 0:128],
                                  wos[:, k, 128 * m:128 * (m + 1)],
                                  atg[:, k, 128 * i:128 * (i + 1)],
                                  start=(k == 0), stop=(k == ND - 1))
                mm.then_inc(s_mm, 1)

        @p.on("scalar")
        def _(s, i=i, s_mm=s_mm, b_mm=b_mm, s_ev=s_ev):
            for m in range(ND):
                s.wait_ge(s_mm, b_mm + m + 1)
                s.activation(out=s1f[:, 128 * m:128 * (m + 1)],
                             in_=sps[m % 2][:, 0:128], func=AF.Identity,
                             bias=bot[:, m:m + 1]).then_inc(s_ev, 1)
        p.barrier()

        s_t, b_t = p.use_sem("x_tr", ND)
        v0t = p._vcnt

        @p.on("tensor")
        def _(t, s_t=s_t, v0t=v0t):
            for k in range(ND):
                if k >= 2:
                    t.wait_ge(p.vsem, v0t + k - 1)
                t.transpose(tps[k % 2][0:128, 0:128],
                            s1f[:, 128 * k:128 * (k + 1)],
                            ident[:]).then_inc(s_t, 1)

        @p.on("vector")
        def _(v, s_t=s_t, b_t=b_t):
            for k in range(ND):
                v.wait_ge(s_t, b_t + k + 1)
                v.tensor_copy(out=s1t[:, 128 * k:128 * (k + 1)],
                              in_=tps[k % 2][0:128, 0:128])
        p.barrier()

        @p.on("vector")
        def _(v, i=i, d_g=d_g):
            v.wait_ge(p.ldsem["gpsimd"], 16 * (d_g + 2))
            for j in range(2):
                v.tensor_scalar(out=oh8[:], in0=iotab[:],
                                scalar1=idxb[:, i, j:j + 1], scalar2=None,
                                op0=ALU.is_equal)
                v.tensor_tensor(out=t8b[:], in0=oh8[:],
                                in1=basejb[:, 8 * j:8 * (j + 1)], op=ALU.mult)
                v.reduce_sum(out=bsel[:], in_=t8b[:], axis=AX.X)
                v.tensor_tensor(out=bsel[:], in0=bsel[:],
                                in1=locb[:, i, j:j + 1], op=ALU.add)
                v.tensor_scalar(out=bsel[:], in0=bsel[:],
                                scalar1=float(C1) + 0.5, scalar2=None,
                                op0=ALU.is_lt)
                v.tensor_tensor(out=gec[:, j:j + 1], in0=bsel[:],
                                in1=gb[:, i, j:j + 1], op=ALU.mult)
            v.tensor_copy(out=xf32[:], in_=xnl[i % 2][:])
            v.tensor_scalar(out=t0f[:], in0=yr[0][:], scalar1=gec[:, 0:1],
                            scalar2=None, op0=ALU.mult)
            v.tensor_scalar(out=t1f[:], in0=yr[1][:], scalar1=gec[:, 1:2],
                            scalar2=None, op0=ALU.mult)
            v.tensor_tensor(out=res[:], in0=t0f[:], in1=t1f[:], op=ALU.add)
            v.tensor_tensor(out=res[:], in0=res[:], in1=xf32[:], op=ALU.add)
            v.reduce_sum(out=p.n_s[:, i:i + 1], in_=res[:], axis=AX.X)
        p.barrier()

        @p.on("scalar")
        def _(s, i=i):
            s.activation(out=p.n_sq[:, 0:D], in_=res[:], func=AF.Square,
                         accum_out=p.n_q[:, i:i + 1])
        p.barrier()

        @p.on("vector")
        def _(v, i=i):
            iv = 1.0 / D
            v.tensor_scalar(out=p.n_mu[:, i:i + 1], in0=p.n_s[:, i:i + 1],
                            scalar1=iv, scalar2=None, op0=ALU.mult)
            v.tensor_scalar(out=p.n_var[:, i:i + 1], in0=p.n_q[:, i:i + 1],
                            scalar1=iv, scalar2=None, op0=ALU.mult)
            v.tensor_tensor(out=p.n_t[:, 0:1], in0=p.n_mu[:, i:i + 1],
                            in1=p.n_mu[:, i:i + 1], op=ALU.mult)
            v.tensor_tensor(out=p.n_var[:, i:i + 1], in0=p.n_var[:, i:i + 1],
                            in1=p.n_t[:, 0:1], op=ALU.subtract)
            v.tensor_scalar(out=p.n_var[:, i:i + 1], in0=p.n_var[:, i:i + 1],
                            scalar1=EPS_LN, scalar2=None, op0=ALU.add)
        p.barrier()

        @p.on("scalar")
        def _(s, i=i):
            s.activation(out=p.n_var[:, i:i + 1], in_=p.n_var[:, i:i + 1],
                         func=AF.Sqrt)
        p.barrier()

        @p.on("vector")
        def _(v, i=i):
            v.reciprocal(out=p.n_var[:, i:i + 1], in_=p.n_var[:, i:i + 1])
            v.tensor_tensor(out=p.n_t[:, 1:2], in0=p.n_mu[:, i:i + 1],
                            in1=p.n_var[:, i:i + 1], op=ALU.mult)
            v.tensor_scalar(out=p.n_t[:, 1:2], in0=p.n_t[:, 1:2],
                            scalar1=-1.0, scalar2=None, op0=ALU.mult)
        p.barrier()

        @p.on("scalar")
        def _(s, i=i):
            s.activation(out=tmpg[:], in_=res[:], func=AF.Identity,
                         scale=p.n_var[:, i:i + 1], bias=p.n_t[:, 1:2])
        p.barrier()

        @p.on("vector")
        def _(v):
            v.tensor_tensor(out=tmpg[:], in0=tmpg[:], in1=lngb[:], op=ALU.mult)
            v.tensor_tensor(out=tmpg[:], in0=tmpg[:], in1=lnbb[:], op=ALU.add)
        p.barrier()

        @p.on("scalar")
        def _(s):
            s.activation(out=g1s[:], in_=tmpg[:], func=AF.Silu)
        p.barrier()

        @p.on("vector")
        def _(v, i=i):
            v.tensor_tensor(out=g1s[:], in0=g1s[:], in1=s1t[:], op=ALU.mult)
            v.tensor_copy(out=t0f[:], in_=sg[i % 2][:])
            v.tensor_tensor(out=xf32[:], in0=xs[i % 2][:], in1=t0f[:],
                            op=ALU.add)
            v.tensor_tensor(out=xf32[:], in0=xf32[:], in1=g1s[:], op=ALU.add)
        p.barrier()

        @p.on("sync")
        def _(s, i=i):
            p.dma("sync", s, x2_out[128 * i:128 * (i + 1), :], xf32[:])
        p.barrier()
    return p.finish()



def _consts():
    e = np.arange(8)
    lt = (e[None, :] < e[:, None]).astype(np.float32).reshape(1, 64)
    ltmask = np.tile(lt, (128, 1)).copy()
    ident = np.eye(128, dtype=np.float32)
    hh = np.arange(16)
    pp = np.arange(128)
    asel = (((pp[None, :] >= 64).astype(int) == (hh[:, None] % 2))
            ).astype(np.float32)
    kk = np.arange(8)
    bsel = (kk[None, :] == hh[:, None] // 2).astype(np.float32)
    return ltmask, ident, asel, bsel


def kernel(**inputs):
    I = {k: np.asarray(v, np.float32) for k, v in inputs.items()}
    ltmask, ident, asel, bsel = _consts()
    cores = list(range(NCORE))

    in1 = []
    for c in cores:
        in1.append(dict(
            x=np.ascontiguousarray(
                I["L0_attn_x"].reshape(B * N, D)[T * c:T * (c + 1)]),
            gate_wT=np.ascontiguousarray(I["L0_ffn_gate_w"].T),
            rms_w_pre=np.ascontiguousarray(
                I["L0_ffn_rms_weight"].reshape(8, 128).T),
            rms_row=I["L0_ffn_rms_weight"].reshape(1, D).copy(),
            lam_log=I["L0_attn_log_decay"].reshape(1, H).copy(),
            ltmask=ltmask, ident=ident,
            tokbase=np.full((128, 1), T * c, np.float32),
            asel=asel, bsel=bsel))
    r1 = run_bass_kernel_spmd(build_p1(), in1, core_ids=cores)
    h_all = [np.asarray(r1.results[c]["h_tm"]) for c in cores]
    hfull = np.concatenate(h_all, 0)
    ids_all = [np.asarray(r1.results[c]["ids"]) for c in cores]
    uend_all = [np.asarray(r1.results[c]["u_end"]) for c in cores]
    pos_all = [np.asarray(r1.results[c]["pos"]) for c in cores]
    g01_all = [np.asarray(r1.results[c]["g01"]) for c in cores]

    in2 = []
    for e in range(NE):
        myids = np.concatenate([ids_all[c][e] for c in cores])
        in2.append(dict(
            w1T=np.ascontiguousarray(I["L0_ffn_w1"][e].T),
            w3T=np.ascontiguousarray(I["L0_ffn_w3"][e].T),
            w2T=np.ascontiguousarray(I["L0_ffn_w2"][e].T),
            hfull=hfull,
            myids=np.ascontiguousarray(myids.reshape(64, 128).T),
            ident=ident))
    r2 = run_bass_kernel_spmd(build_p2(), in2, core_ids=cores)
    y0_all = [np.asarray(r2.results[e]["y0"]) for e in range(NE)]

    uends_pre = np.ascontiguousarray(
        np.concatenate(uend_all, 0).reshape(8, 8, 128).transpose(2, 0, 1))
    in3 = []
    for c in cores:
        cec = np.zeros((128, 8), np.float32)
        cmask = np.zeros((128, 8), np.float32)
        for j in range(8):
            if j // 4 == c // 4 and (j % 4) < (c % 4):
                cmask[:, j] = 1.0
                cec[:, j] = 2048.0 * ((c % 4) - 1 - (j % 4))
        y0own = np.concatenate(
            [y0_all[e][CAPL * c:CAPL * (c + 1)] for e in range(NE)], 0)
        in3.append(dict(
            x=in1[c]["x"], uends=uends_pre, cec=cec, cmask=cmask,
            lam_log=I["L0_attn_log_decay"].reshape(1, H).copy(),
            qvec=I["L0_attn_q"].reshape(H, dh).copy(),
            kvec=I["L0_attn_k"].reshape(H, dh).copy(),
            nw_row=I["L0_attn_norm_weight"].reshape(1, D).copy(),
            o_wT=np.ascontiguousarray(I["L0_attn_o_w"].T),
            y0own=y0own,
            pos0=pos_all[c], g01_0=g01_all[c],
            wg=I["L1_ffn_wg"].copy(),
            ltmask=ltmask, ident=ident,
            tokbase=np.full((128, 1), T * c, np.float32),
            asel=asel, bsel=bsel))
    r3 = run_bass_kernel_spmd(build_p3(), in3, core_ids=cores)
    xn1_all = [np.asarray(r3.results[c]["xn1"]) for c in cores]
    s0g0_all = [np.asarray(r3.results[c]["s0g0"]) for c in cores]
    ids1_all = [np.asarray(r3.results[c]["ids1"]) for c in cores]
    pos1_all = [np.asarray(r3.results[c]["pos1"]) for c in cores]
    idx1_all = [np.asarray(r3.results[c]["idx1"]) for c in cores]
    locj_all = [np.asarray(r3.results[c]["locj"]) for c in cores]
    g011_all = [np.asarray(r3.results[c]["g011"]) for c in cores]
    cnts_l = [np.asarray(r3.results[c]["cnts"]) for c in cores]

    tbl = I["L1_attn_relpos_emb"]
    deltas = [-256, -128, 0, 128, 256]
    ppg, ffg = np.meshgrid(np.arange(128), np.arange(128), indexing="ij")
    in45 = []
    for c in cores:
        bat = c // 4
        g = c % 4
        HG = 4 * g
        r5 = np.zeros((5, 4, 128, 128), np.float32)
        for ci, dl in enumerate(deltas):
            idxm = np.clip(ffg - ppg - dl, -255, 255) + 255
            for h2 in range(4):
                r5[ci, h2] = tbl[idxm, HG + h2]
        tconsts = np.concatenate(
            [tbl[0, HG:HG + 4], tbl[510, HG:HG + 4]]).reshape(1, 8)
        order = (list(range(4, 8)) + list(range(0, 4))) if bat == 1 \
            else list(range(8))
        myids = np.concatenate([ids1_all[cc][c] for cc in cores])
        m = {f"xn1_{cc}": xn1_all[order[cc]] for cc in cores}
        m.update(dict(
            myids1=np.ascontiguousarray(myids.reshape(64, 128).T),
            fc1T=np.ascontiguousarray(I["L1_ffn_fc1_w"][c].T),
            fc2n=I["L1_ffn_fc2_w"][c].copy(),
            fc1b_pre=np.ascontiguousarray(
                I["L1_ffn_fc1_b"][c].reshape(32, 128).T),
            fc2b_pre=np.ascontiguousarray(
                I["L1_ffn_fc2_b"][c].reshape(8, 128).T),
            wq_sl=np.ascontiguousarray(
                I["L1_attn_Wqkv"][:, 256 * g:256 * (g + 1)]),
            wk_sl=np.ascontiguousarray(
                I["L1_attn_Wqkv"][:, 1024 + 256 * g:1024 + 256 * (g + 1)]),
            wv_sl=np.ascontiguousarray(
                I["L1_attn_Wqkv"][:, 2048 + 256 * g:2048 + 256 * (g + 1)]),
            wgd_sl=np.ascontiguousarray(
                I["L1_attn_Wg"][:, 256 * g:256 * (g + 1)]),
            bq_pre=np.ascontiguousarray(
                I["L1_attn_bqkv"][256 * g:256 * (g + 1)].reshape(2, 128).T),
            bk_pre=np.ascontiguousarray(
                I["L1_attn_bqkv"][1024 + 256 * g:1024 + 256 * (g + 1)]
                .reshape(2, 128).T),
            bv_row=I["L1_attn_bqkv"][2048 + 256 * g:2048 + 256 * (g + 1)]
            .reshape(1, 256).copy(),
            bg_pre=np.ascontiguousarray(
                I["L1_attn_bg"][256 * g:256 * (g + 1)].reshape(2, 128).T),
            w1_sl=np.ascontiguousarray(
                I["L1_attn_W1"][:, 1024 * g:1024 * (g + 1)]),
            b1_pre=np.ascontiguousarray(
                I["L1_attn_b1"][1024 * g:1024 * (g + 1)].reshape(8, 128).T),
            w2_sl=np.ascontiguousarray(
                I["L1_attn_W2"][1024 * g:1024 * (g + 1), :]),
            b2_row=I["L1_attn_b2"].reshape(1, 16).copy(),
            r5=r5, tconsts=tconsts.astype(np.float32),
            boff=np.full((128, 1), float(bat * N), np.float32),
            ident=ident))
        in45.append(m)
    r45 = run_bass_kernel_spmd(build_p45(), in45, core_ids=cores)
    y1_all = [np.asarray(r45.results[c]["y1"]) for c in cores]
    numg_all = [np.asarray(r45.results[c]["numg"]) for c in cores]
    gate_all = [np.asarray(r45.results[c]["gate"]) for c in cores]
    den_all = [np.asarray(r45.results[c]["den"]) for c in cores]
    edgep_all = [np.asarray(r45.results[c]["edgep"]) for c in cores]

    hhv = np.arange(16)[:, None]
    kpv = np.arange(D)[None, :]
    fsel = (hhv == 2 * (kpv // 128) + ((kpv % 128) >= 64)).astype(np.float32)
    cnts_all = np.stack([cnts_l[cc].reshape(16) for cc in cores], 0)
    in6 = []
    for c in cores:
        bat = c // 4
        cs = slice(2048 * (c % 4), 2048 * (c % 4 + 1))
        attng = np.concatenate(
            [numg_all[4 * bat + g][:, cs] for g in range(4)], 0)
        gateg = np.concatenate(
            [gate_all[4 * bat + g][:, cs] for g in range(4)], 0)
        deng = np.concatenate(
            [den_all[4 * bat + g][:, cs] for g in range(4)], 0)
        edgepg = np.concatenate(
            [edgep_all[4 * bat + g][:, cs] for g in range(4)], 0)
        y1own = np.concatenate(
            [y1_all[e][CAPL * c:CAPL * (c + 1)] for e in range(NE)], 0)
        in6.append(dict(
            attng=np.ascontiguousarray(attng),
            gateg=np.ascontiguousarray(gateg),
            deng=np.ascontiguousarray(deng),
            edgepg=np.ascontiguousarray(edgepg),
            wo=I["L1_attn_Wo"].copy(),
            bo_pre=np.ascontiguousarray(I["L1_attn_bo"].reshape(8, 128).T),
            b2_col=I["L1_attn_b2"].reshape(16, 1).copy(),
            lng_row=I["L1_ffn_ln_gamma"].reshape(1, D).copy(),
            lnb_row=I["L1_ffn_ln_beta"].reshape(1, D).copy(),
            y1own=y1own, x=in1[c]["x"],
            s0g0=s0g0_all[c], xn1=xn1_all[c],
            pos1=pos1_all[c], idx1=idx1_all[c], locj=locj_all[c],
            g011=g011_all[c], cnts_all=cnts_all,
            cmask6=(np.arange(8) < c).astype(np.float32).reshape(8, 1),
            iota8=np.arange(8, dtype=np.float32).reshape(1, 8),
            fsel=fsel, ident=ident))
    r6 = run_bass_kernel_spmd(build_p6(), in6, core_ids=cores)
    x2 = np.concatenate([np.asarray(r6.results[c]["x2"]) for c in cores], 0)
    return np.ascontiguousarray(x2.reshape(B, N, D).astype(np.float32))
